# version 36
# baseline (speedup 1.0000x reference)
import numpy as np
import ml_dtypes
import concourse.bass as bass
import concourse.mybir as mybir
from concourse.bass_utils import run_bass_kernel_spmd
from contextlib import ExitStack

F32 = mybir.dt.float32
BF16 = mybir.dt.bfloat16
I32 = mybir.dt.int32
AF = mybir.ActivationFunctionType
ALU = mybir.AluOpType

NCORE = 8
NSEQ = 4
S = 2048
D = 1024
H = 16
FQ = 3072
FF = 3072
NPAIR = 8
QW = 512
NQQ = 4
NT = 16
EPS = 1e-5
DEQ = 1.0 / 127.0


def build_program():
    nc = bass.Bass("TRN2", target_bir_lowering=False)

    x_d = nc.dram_tensor("x", [NSEQ, S, D], F32, kind="ExternalInput").ap()
    xT_d = nc.dram_tensor("xT", [NSEQ, D, S], F32, kind="ExternalInput").ap()
    wqkv_d = nc.dram_tensor("wqkv", [D, FQ], F32, kind="ExternalInput").ap()
    wout_d = nc.dram_tensor("wout", [D, D], F32, kind="ExternalInput").ap()
    wgT_d = nc.dram_tensor("wgT", [D, FF], I32, kind="ExternalInput").ap()
    wuT_d = nc.dram_tensor("wuT", [D, FF], I32, kind="ExternalInput").ap()
    wdT_d = nc.dram_tensor("wdT", [FF, D], I32, kind="ExternalInput").ap()
    bqkT_d = nc.dram_tensor("bqkT", [128, 16], F32, kind="ExternalInput").ap()
    bvb_d = nc.dram_tensor("bvb", [128, D], F32, kind="ExternalInput").ap()
    bob_d = nc.dram_tensor("bob", [1, D], F32, kind="ExternalInput").ap()
    maskT_d = nc.dram_tensor("maskT", [128, 128], F32, kind="ExternalInput").ap()
    ident_d = nc.dram_tensor("ident", [128, 128], BF16, kind="ExternalInput").ap()
    out_d = nc.dram_tensor("out", [NSEQ, S, D], F32, kind="ExternalOutput").ap()

    wqkv_b = nc.dram_tensor("wqkv_b", [D, FQ], BF16, kind="Internal").ap()
    wout_b = nc.dram_tensor("wout_b", [D, D], BF16, kind="Internal").ap()
    wg_b = nc.dram_tensor("wg_b", [D, FF], BF16, kind="Internal").ap()
    wu_b = nc.dram_tensor("wu_b", [D, FF], BF16, kind="Internal").ap()
    wd_b = nc.dram_tensor("wd_b", [FF, D], BF16, kind="Internal").ap()

    with ExitStack() as es:
        e = es.enter_context

        T1 = e(nc.sbuf_tensor("T1", [128, 32768], BF16))
        T2 = e(nc.sbuf_tensor("T2", [128, 16384], BF16))
        T3 = e(nc.sbuf_tensor("T3", [128, 16384], BF16))
        T4 = e(nc.sbuf_tensor("T4", [128, 24576], BF16))
        pT = e(nc.sbuf_tensor("pT", [128, 2048], BF16))
        gsb = e(nc.sbuf_tensor("gsb", [128, 1024], BF16))
        x1bb = e(nc.sbuf_tensor("x1bb", [128, 2048], BF16))

        bqkT = e(nc.sbuf_tensor("bqkT_s", [128, 16], F32))
        bvb = e(nc.sbuf_tensor("bvb_s", [128, D], F32))
        statscr = e(nc.sbuf_tensor("statscr", [128, 1024], F32))
        mskt = e(nc.sbuf_tensor("mskt", [128, 128], F32))
        idb = e(nc.sbuf_tensor("idb", [128, 128], BF16))
        negc = e(nc.sbuf_tensor("negc", [128, 1], F32))
        ones64 = e(nc.sbuf_tensor("ones64", [128, 64], BF16))
        ones128b = e(nc.sbuf_tensor("ones128b", [128, 1], BF16))
        rrb = e(nc.sbuf_tensor("rrb", [128, 512], F32))
        suma = e(nc.sbuf_tensor("suma", [128, 16], F32))
        sqa = e(nc.sbuf_tensor("sqa", [128, 16], F32))
        suma2 = e(nc.sbuf_tensor("suma2", [128, 4], F32))
        sqa2 = e(nc.sbuf_tensor("sqa2", [128, 4], F32))
        epsb = e(nc.sbuf_tensor("epsb", [128, 1], F32))
        bobres = e(nc.sbuf_tensor("bobres", [128, 1024], F32))
        ln1s = e(nc.sbuf_tensor("ln1s", [128, 48], F32))
        ln2s = e(nc.sbuf_tensor("ln2s", [128, 12], F32))
        onesr = e(nc.sbuf_tensor("onesr", [1, 128], F32))
        bob_row = e(nc.sbuf_tensor("bob_row", [1, D], F32))

        T1f = T1.ap().bitcast(F32)
        T3f = T3.ap().bitcast(F32)
        T4f = T4.ap().bitcast(F32)

        def xTb(dc):
            return T1.ap()[:, 2048 * dc:2048 * (dc + 1)]

        def qkw(slot):
            return T1.ap()[:, 16384 + 2048 * slot:16384 + 2048 * (slot + 1)]

        def Vw(dc):
            return T1.ap()[:, 20480 + 1024 * dc:20480 + 1024 * (dc + 1)]

        def x1t(ti):
            return T1f[:, 1024 * ti:1024 * (ti + 1)]

        V2 = T2.ap()[:, 0:16384]

        def hT(hc):
            return T2.ap()[:, 512 * hc:512 * (hc + 1)]

        x1T = T2.ap()[:, 12288:16384]

        valsT = T3.ap()[:, 0:16384]

        def wgub(mat, par):
            if par == 0:
                return pT.ap()[:, 1024 * mat:1024 * (mat + 1)]
            off = 2048 * mat + 1024
            return T3.ap()[:, off:off + 1024]

        def y2q(tr):
            return T3f[:, 3072 + 1024 * tr:3072 + 1024 * (tr + 1)]

        def qkTs(chunk):
            return T4.ap()[:, 2048 * chunk:2048 * (chunk + 1)]

        def woutb(j):
            base = 8192 + 1024 * j if j < 4 else 1024 * (j - 4)
            return T4.ap()[:, base:base + 1024]

        def residt(slot):
            return T4f[:, 6144 + 1024 * slot:6144 + 1024 * (slot + 1)]

        def wdTb(hc):
            return T4.ap()[:, 1024 * hc:1024 * (hc + 1)]

        def x1b(par):
            return x1bb.ap()[:, 1024 * par:1024 * (par + 1)]

        ps = [e(nc.psum_tensor(f"ps{i}", [128, 512], F32)) for i in range(8)]
        psb = [p.ap().bitcast(BF16) for p in ps]

        sem_names = [
            "d_xT", "d_wq", "d_vw", "d_rs0", "d_rs1", "d_wo", "d_wd",
            "d_g0", "d_g1", "d_u0", "d_u1", "d_out0", "d_out1",
            "d_ri0", "d_ri1", "d_ro0", "d_ro1", "d_cst",
            "c_qk", "c_qke", "c_v", "c_ve", "c_s", "c_m", "c_x",
            "c_pv", "c_pq", "c_rr", "c_rb", "c_nm", "c_o", "c_y", "c_bn",
            "c_ln", "c_xb", "c_tp", "c_te", "c_gu", "c_si", "c_hm",
            "c_dn", "c_y2", "c_bn2", "c_ln2", "c_n2", "c_g1", "c_g2",
            "c_lg", "c_vc",
        ]
        sems = {n: e(nc.semaphore(n)) for n in sem_names}

        audit_streams = {}

        def walk(which):
            cnt = {n: 0 for n in sem_names}
            waited = {n: -1 for n in sem_names}

            def on(engine, fn, waits=(), incs=()):
                wl = [(s_, v) for (s_, v) in waits if v > 0]
                if which == "audit":
                    audit_streams.setdefault(engine, []).append((list(wl), list(incs)))
                inst = None
                if which == engine:
                    E = getattr(nc, engine)
                    for s_, v in wl:
                        if waited[s_] < v:
                            E.wait_ge(sems[s_], v)
                            waited[s_] = v
                    inst = fn(E)
                for s_, n in incs:
                    cnt[s_] += n
                    if inst is not None:
                        inst.then_inc(sems[s_], n)
                return inst

            def C(s_):
                return cnt[s_]

            on("sync", lambda E: E.dma_start(out=bqkT[:], in_=bqkT_d), incs=[("d_cst", 16)])
            on("sync", lambda E: E.dma_start(out=bvb[:], in_=bvb_d), incs=[("d_cst", 16)])
            on("sync", lambda E: E.dma_start(out=bob_row[:], in_=bob_d), incs=[("d_cst", 16)])
            on("sync", lambda E: E.dma_start(out=mskt[:], in_=maskT_d), incs=[("d_cst", 16)])
            on("sync", lambda E: E.dma_start(out=idb[:], in_=ident_d), incs=[("d_cst", 16)])

            def mk_init(E):
                E.memset(negc[:], -4.0)
                E.memset(onesr[:], 1.0)
                E.memset(epsb[:], EPS)
                E.memset(ones128b[:], 1.0)
                return E.memset(ones64[:], 1.0)
            on("vector", mk_init, incs=[("c_m", 1)])


            def qkw_src_f32(pj, c):
                return wqkv_d.rearrange("(dc p) f -> p dc f", p=128)[
                    :, :, 1024 * c + 128 * pj:1024 * c + 128 * (pj + 1)]

            def qkw_src_b(pj, c):
                return wqkv_b.rearrange("(dc p) f -> p dc f", p=128)[
                    :, :, 1024 * c + 128 * pj:1024 * c + 128 * (pj + 1)]

            def qkw_dst(slot, c):
                return qkw(slot).rearrange("p (dc c f) -> p dc c f", dc=8, c=2)[:, :, c, :]

            for dc in range(8):
                on("gpsimd", lambda E, dc=dc:
                    E.dma_start(out=xTb(dc), in_=xT_d[0, 128 * dc:128 * (dc + 1), :]),
                    incs=[("d_xT", 16)])
            on("gpsimd", lambda E:
                E.dma_start(out=T1.ap()[:, 20480:28672].rearrange("p (dc f) -> p dc f", dc=8),
                            in_=wqkv_d.rearrange("(dc p) f -> p dc f", p=128)[:, :, 2048:3072]),
                incs=[("d_vw", 16)])
            for pj in range(2):
                for c in range(2):
                    on("gpsimd", lambda E, pj=pj, c=c:
                        E.dma_start(out=qkw_dst(pj, c), in_=qkw_src_f32(pj, c)),
                        incs=[("d_wq", 16)])
            s0_dwq_at = {0: 32, 1: 64}
            for nxt in range(2, NPAIR):
                for c in range(2):
                    on("gpsimd", lambda E, nxt=nxt, c=c:
                        E.dma_start(out=qkw_dst(nxt % 2, c), in_=qkw_src_f32(nxt, c)),
                        waits=[("c_qk", 4 * (nxt - 1))], incs=[("d_wq", 16)])
                s0_dwq_at[nxt] = C("d_wq")

            rp_wait = {}
            rp_list = [("wout", wout_d, wout_b), ("wg", wgT_d, wg_b),
                       ("wu", wuT_d, wu_b), ("wd", wdT_d, wd_b),
                       ("wqkv", wqkv_d, wqkv_b)]
            for i, (key, src, dst) in enumerate(rp_list):
                on("gpsimd", lambda E, src=src, dst=dst:
                    E.dma_start(out=dst, in_=src), incs=[("d_ro0", 16)])
                rp_wait[key] = ("d_ro0", C("d_ro0"))

            dn_final = 0
            y2_final = 0
            y_final = 0
            o_final = 0
            x_final = 0
            nm_final = 0
            out_cnt = [0, 0]
            rs_cnt = [0, 0]
            g_cnt = [0, 0]
            u_cnt = [0, 0]
            gstep = 0
            cx_at = {}
            y2_at = {}

            for s in range(NSEQ):
                if s > 0:
                    for dc in range(8):
                        on("gpsimd", lambda E, s=s, dc=dc:
                            E.dma_start(out=xTb(dc), in_=xT_d[s, 128 * dc:128 * (dc + 1), :]),
                            waits=[("c_y2", y2_at.get(7, y2_final))], incs=[("d_xT", 16)])
                    on("sync", lambda E:
                        E.dma_start(out=T1.ap()[:, 20480:28672].rearrange("p (dc f) -> p dc f", dc=8),
                                    in_=wqkv_b.rearrange("(dc p) f -> p dc f", p=128)[:, :, 2048:3072]),
                        waits=[rp_wait["wqkv"], ("c_y2", y2_at.get(13, y2_final))], incs=[("d_vw", 16)])
                    for pj in range(2):
                        for c in range(2):
                            on("sync", lambda E, pj=pj, c=c:
                                E.dma_start(out=qkw_dst(pj, c), in_=qkw_src_b(pj, c)),
                                waits=[rp_wait["wqkv"], ("c_y2", y2_at.get(9, y2_final))],
                                incs=[("d_wq", 16)])
                dxT = C("d_xT")
                dvw = C("d_vw")
                if s == 0:
                    dwq_at = dict(s0_dwq_at)
                else:
                    dwq_at = {0: C("d_wq") - 32, 1: C("d_wq")}

                qke_at = {}
                sc_pair = {}
                qk_state = {"pj": 0, "u": 0, "upf": {}}
                dn_final_seq = dn_final

                def emit_qk_unit(upfront=False):
                    pj = qk_state["pj"]
                    if pj >= NPAIR:
                        return
                    u = qk_state["u"]
                    wslot = pj % 2
                    cslot = pj % 3
                    for c, th in [divmod(u, 2)]:
                        if upfront:
                            qb0, qb1 = (0, 1) if u % 2 == 0 else (2, 3)
                        else:
                            qb0, qb1 = 6, 7
                        if True:
                            waits = [("d_wq", dwq_at[pj]), ("d_xT", dxT)]
                            if upfront:
                                waits += [("c_si", C("c_si")), ("c_hm", C("c_hm")),
                                          ("c_tp", C("c_tp")), ("c_te", C("c_te")),
                                          ("c_y", C("c_y")), ("c_rb", C("c_rb")),
                                          ("c_x", C("c_x")),
                                          ("c_qke", qk_state["upf"].get(u % 2, 0))]
                            else:
                                waits += [("c_qke", C("c_qke")), ("c_ve", C("c_ve")),
                                          ("c_y2", y2_final)]
                            def mk_qk(E, wslot=wslot, c=c, th=th, qb0=qb0, qb1=qb1):
                                inst = None
                                for dc in range(8):
                                    st_, sp = dc == 0, dc == 7
                                    lhsT = qkw(wslot)[:, 256 * dc + 128 * c:256 * dc + 128 * (c + 1)]
                                    E.matmul(ps[qb0][:], lhsT,
                                             xTb(dc)[:, 1024 * th:1024 * th + 512],
                                             start=st_, stop=sp)
                                    inst = E.matmul(ps[qb1][:], lhsT,
                                                    xTb(dc)[:, 1024 * th + 512:1024 * (th + 1)],
                                                    start=st_, stop=sp)
                                return inst
                            on("tensor", mk_qk, waits=waits, incs=[("c_qk", 1)])
                            ew = [("c_qk", C("c_qk")), ("d_cst", 16), ("c_dn", dn_final_seq)]
                            if pj >= 3:
                                ew.append(("c_s", sc_pair[pj - 3]))
                            def mk_qke(E, pj=pj, cslot=cslot, c=c, th=th, qb0=qb0, qb1=qb1):
                                dst = qkTs(2 * cslot + c)
                                bcol = pj + 8 * c
                                E.activation(dst[:, 1024 * th:1024 * th + 512], ps[qb0][:],
                                             AF.Identity, bias=bqkT[:, bcol:bcol + 1], scale=1.0)
                                return E.activation(dst[:, 1024 * th + 512:1024 * (th + 1)], ps[qb1][:],
                                                    AF.Identity, bias=bqkT[:, bcol:bcol + 1], scale=1.0)
                            on("scalar", mk_qke, waits=ew, incs=[("c_qke", 1)])
                            if upfront:
                                qk_state["upf"][u % 2] = C("c_qke")
                    if u < 3:
                        qk_state["u"] = u + 1
                        return
                    qke_at[pj] = C("c_qke")
                    nxt = pj + 2
                    if nxt < NPAIR and s > 0:
                        for c in range(2):
                            on("sync", lambda E, nxt=nxt, wslot=wslot, c=c:
                                E.dma_start(out=qkw_dst(wslot, c), in_=qkw_src_b(nxt, c)),
                                waits=[("c_qk", C("c_qk")), rp_wait["wqkv"]],
                                incs=[("d_wq", 16)])
                        dwq_at[nxt] = C("d_wq")
                    qk_state["pj"] = pj + 1
                    qk_state["u"] = 0

                def emit_qk_pair(upfront=False):
                    for _ in range(4):
                        emit_qk_unit(upfront)

                emit_qk_pair(upfront=True)
                emit_qk_pair(upfront=True)

                v_state = {"ti": 0}
                ve_at = {}

                def emit_v(upto):
                    while v_state["ti"] < upto:
                        ti = v_state["ti"]
                        waits = [("d_xT", dxT), ("d_vw", dvw), ("c_qke", C("c_qke"))]
                        if ti > 0:
                            waits.append(("c_ve", ve_at[ti - 1]))
                        else:
                            waits.append(("c_qke", qke_at[1]))
                        def mk_v(E, ti=ti):
                            inst = None
                            for dc in range(8):
                                st_, sp = dc == 0, dc == 7
                                lhsT = xTb(dc)[:, 128 * ti:128 * (ti + 1)]
                                E.matmul(ps[6][:], lhsT, Vw(dc)[:, 0:512], start=st_, stop=sp)
                                inst = E.matmul(ps[7][:], lhsT, Vw(dc)[:, 512:1024], start=st_, stop=sp)
                            return inst
                        on("tensor", mk_v, waits=waits, incs=[("c_v", 1)])
                        def mk_ve(E, ti=ti):
                            E.scalar_tensor_tensor(V2[:, 1024 * ti:1024 * ti + 512], ps[6][:],
                                                   1.0, bvb[:, 0:512], ALU.mult, ALU.add)
                            return E.scalar_tensor_tensor(V2[:, 1024 * ti + 512:1024 * (ti + 1)],
                                                          ps[7][:], 1.0, bvb[:, 512:1024],
                                                          ALU.mult, ALU.add)
                        on("vector", mk_ve,
                           waits=[("c_v", C("c_v")), ("d_cst", 80), ("c_dn", dn_final_seq)],
                           incs=[("c_ve", 1)])
                        ve_at[ti] = C("c_ve")
                        v_state["ti"] = ti + 1

                pending_norm = []

                def flush_norm():
                    while pending_norm:
                        pending_norm.pop(0)()

                def norm_count_logical():
                    return C("c_nm") + len(pending_norm)

                for pj in range(NPAIR):
                    cslot = pj % 3
                    qc = 2048 * (2 * cslot)
                    kc = 2048 * (2 * cslot + 1)
                    cs_at = {}
                    cm_at = {}
                    steps = []
                    for qq in range(NQQ):
                        nkj = 4 * qq + 4
                        for kj in range(nkj):
                            steps.append((qq, kj, kj == 0, kj == nkj - 1))
                    ns = len(steps)

                    def scores_step(idx, pj=pj, qc=qc, kc=kc):
                        qq, kj, first, last = steps[idx]
                        gl = gstep + idx
                        par = gl % 2
                        b0, b1 = (0, 1) if par == 0 else (2, 3)
                        diag = kj >= 4 * qq
                        joff = 128 * (kj - 4 * qq) if diag else 0
                        n = QW - joff
                        waits = [("c_qke", qke_at[pj]), ("c_m", 1)]
                        if gl >= 2:
                            waits.append(("c_x", 2 * (gl - 1)))
                        def mk(E, kj=kj, qq=qq, joff=joff, n=n, b0=b0, b1=b1, qc=qc, kc=kc):
                            qcol = qc + QW * qq + joff
                            kcol = kc + 128 * kj
                            E.matmul(ps[b0][:, joff:joff + n],
                                     T4.ap()[0:64, kcol:kcol + 128],
                                     T4.ap()[0:64, qcol:qcol + n], start=True, stop=True)
                            return E.matmul(ps[b1][:, joff:joff + n],
                                            T4.ap()[64:128, kcol:kcol + 128],
                                            T4.ap()[64:128, qcol:qcol + n], start=True, stop=True)
                        on("tensor", mk, waits=waits, incs=[("c_s", 1)])
                        cs_at[gl] = C("c_s")
                        if diag:
                            def mkm(E, b0=b0, b1=b1, joff=joff):
                                E.tensor_tensor(ps[b0][:, joff:joff + 128],
                                                ps[b0][:, joff:joff + 128], mskt[:], ALU.add)
                                return E.tensor_tensor(ps[b1][:, joff:joff + 128],
                                                       ps[b1][:, joff:joff + 128], mskt[:], ALU.add)
                            on("vector", mkm, waits=[("c_s", cs_at[gl])], incs=[("c_m", 1)])
                        cm_at[gl] = C("c_m")
                        return (par, joff, n, diag)

                    meta = {0: scores_step(0)}
                    for idx in range(ns):
                        qq, kj, first, last = steps[idx]
                        gl = gstep + idx
                        if pj == 0 and kj == 0:
                            emit_v(min(NT, 4 * qq + 4))
                        elif pj == 1 and idx == 0:
                            emit_v(NT)
                        if idx + 1 < ns:
                            meta[idx + 1] = scores_step(idx + 1)
                        par, joff, n, diag = meta[idx]
                        pcol = 1024 * par
                        b0, b1 = (0, 1) if par == 0 else (2, 3)
                        xw2 = [("c_m", cm_at[gl])] if diag else [("c_s", cs_at[gl])]
                        def mex0(E, b0=b0, pcol=pcol, joff=joff, n=n):
                            return E.activation(pT[:, pcol + joff:pcol + joff + n],
                                                ps[b0][:, joff:joff + n],
                                                AF.Exp, bias=negc[:], scale=0.125)
                        def mex1(E, b1=b1, pcol=pcol, joff=joff, n=n):
                            return E.activation(pT[:, pcol + 512 + joff:pcol + 512 + joff + n],
                                                ps[b1][:, joff:joff + n],
                                                AF.Exp, bias=negc[:], scale=0.125)
                        on("scalar", mex0, waits=xw2, incs=[("c_x", 1)])
                        on("scalar", mex1, waits=xw2, incs=[("c_x", 1)])
                        cx_at[gl] = C("c_x")
                        assert cx_at[gl] == 2 * (gl + 1), (gl, cx_at[gl])
                        if first:
                            flush_norm()
                        if pj >= 1 and kj == 1:
                            emit_qk_unit()
                        pvw = [("c_x", cx_at[gl]), ("c_ve", ve_at[kj])]
                        if first:
                            pvw.append(("c_vc", C("c_vc")))
                            pvw.append(("c_y2", y2_final))
                        def mpv(E, kj=kj, pj=pj, pcol=pcol, joff=joff, n=n,
                                first=first, last=last):
                            vb0 = V2[:, 1024 * kj + 128 * pj:1024 * kj + 128 * pj + 64]
                            vb1 = V2[:, 1024 * kj + 128 * pj + 64:1024 * kj + 128 * (pj + 1)]
                            E.matmul(ps[4][0:64, joff:joff + n], vb0,
                                     pT[:, pcol + joff:pcol + joff + n], start=first, stop=last)
                            return E.matmul(ps[4][64:128, joff:joff + n], vb1,
                                            pT[:, pcol + 512 + joff:pcol + 512 + joff + n],
                                            start=first, stop=last, tile_position=(0, 64))
                        on("tensor", mpv, waits=pvw, incs=[])
                        def mpr(E, kj=kj, pcol=pcol, joff=joff, n=n, first=first, last=last):
                            E.matmul(ps[5][0:64, joff:joff + n], ones64[:],
                                     pT[:, pcol + joff:pcol + joff + n], start=first, stop=last)
                            return E.matmul(ps[5][64:128, joff:joff + n], ones64[:],
                                            pT[:, pcol + 512 + joff:pcol + 512 + joff + n],
                                            start=first, stop=last, tile_position=(0, 64))
                        prw = [("c_x", cx_at[gl])]
                        if first:
                            prw.append(("c_g2", C("c_g2")))
                        on("tensor", mpr, waits=prw,
                           incs=[("c_pv", 1)] if last else [])
                        if last:
                            def mcp(E, pj=pj, qq=qq):
                                return E.tensor_copy(
                                    valsT[:, 2048 * pj + QW * qq:2048 * pj + QW * (qq + 1)],
                                    ps[4][:, 0:512])
                            cpw = [("c_pv", C("c_pv"))]
                            if pj == 0 and qq == 0 and s > 0:
                                cpw.append(("c_o", o_final))
                                cpw.append(("c_gu", C("c_gu")))
                                cpw.append(("d_out0", out_cnt[0]))
                                cpw.append(("d_out1", out_cnt[1]))
                            on("vector", mcp, waits=cpw, incs=[("c_vc", 1)])

                            def emit_norm(pj=pj, qq=qq, pv_=C("c_pv"),
                                          nm_=C("c_nm"), vc_=C("c_vc")):
                                on("scalar",
                                   lambda E: E.activation(bobres[:, 0:512],
                                                          ps[5][:, 0:512], AF.Ln),
                                   waits=[("c_pv", pv_), ("c_nm", nm_)],
                                   incs=[("c_g2", 1)])
                                def mexr(E):
                                    E.activation(rrb[:], bobres[:, 0:512],
                                                 AF.Exp, scale=-1.0)
                                    return E.activation(bobres[:, 1016:1024],
                                                        rrb[:, 504:512], AF.Identity)
                                on("scalar", mexr, waits=[("c_g2", C("c_g2"))],
                                   incs=[("c_lg", 1)])
                                def mnm(E, pj=pj, qq=qq):
                                    col = 2048 * pj + QW * qq
                                    return E.tensor_mul(valsT[:, col:col + QW],
                                                        valsT[:, col:col + QW],
                                                        rrb[:])
                                on("vector", mnm,
                                   waits=[("c_lg", C("c_lg")), ("c_vc", vc_)],
                                   incs=[("c_nm", 1)])
                            pending_norm.append(emit_norm)
                    gstep += ns
                    sc_pair[pj] = C("c_s")
                flush_norm()
                nm_final = C("c_nm")
                x_final = C("c_x")
                qk_final = C("c_qk")
                v_final = C("c_v")
                sfin = C("c_s")

                wout_src = wout_b.rearrange("(c p) f -> p c f", p=128)
                on("sync", lambda E: E.dma_start(
                        out=T4.ap()[:, 8192:12288].rearrange("p (c f) -> p c f", c=4),
                        in_=wout_src[:, 0:4, :]),
                   waits=[rp_wait["wout"], ("c_s", sc_pair[5])], incs=[("d_wo", 16)])
                on("sync", lambda E: E.dma_start(
                        out=T4.ap()[:, 0:4096].rearrange("p (c f) -> p c f", c=4),
                        in_=wout_src[:, 4:8, :]),
                   waits=[("c_s", sc_pair[6])], incs=[("d_wo", 16)])
                dwo = C("d_wo")
                o_parity_free = {}
                ln_base = C("c_ln")
                for ti in range(NT):
                    par = ti % 2
                    b0, b1 = (0, 1) if par == 0 else (2, 3)
                    rs = f"d_rs{par}"
                    on("gpsimd", lambda E, s=s, ti=ti, par=par:
                        E.dma_start(out=residt(par), in_=x_d[s, 128 * ti:128 * (ti + 1), :]),
                        waits=[("c_y", rs_cnt[par]), ("c_dn", dn_final)],
                        incs=[(rs, 16)])
                    drs = C(rs)
                    waits = [("c_nm", nm_final), ("d_wo", dwo)]
                    pv_ = o_parity_free.get(par)
                    if pv_ is not None:
                        waits.append(("c_y", pv_))
                    else:
                        waits.append(("c_x", x_final))
                        waits.append(("c_rb", C("c_rb")))
                    def mk_op(E, ti=ti, b0=b0, b1=b1):
                        inst = None
                        for j in range(8):
                            st_, sp = j == 0, j == 7
                            lhsT = valsT[:, 2048 * j + 128 * ti:2048 * j + 128 * (ti + 1)]
                            E.matmul(ps[b0][:], lhsT, woutb(j)[:, 0:512],
                                     start=st_, stop=sp)
                            inst = E.matmul(ps[b1][:], lhsT, woutb(j)[:, 512:1024],
                                            start=st_, stop=sp)
                            if j == 0:
                                E.matmul(ps[b0][:], onesr[:], bob_row[0:1, 0:512],
                                         start=False, stop=False)
                                inst = E.matmul(ps[b1][:], onesr[:],
                                                bob_row[0:1, 512:1024],
                                                start=False, stop=False)
                        return inst
                    on("tensor", mk_op, waits=waits, incs=[("c_o", 1)])
                    def mk_y(E, ti=ti, b0=b0, b1=b1, par=par):
                        xt = x1t(ti)
                        E.tensor_add(xt[:, 0:512], ps[b0][:], residt(par)[:, 0:512])
                        return E.tensor_add(xt[:, 512:1024], ps[b1][:],
                                            residt(par)[:, 512:1024])
                    yw = [("c_o", C("c_o")), (rs, drs), ("d_cst", 80),
                          ("c_qk", qk_final), ("c_v", v_final)]
                    on("vector", mk_y, waits=yw, incs=[("c_y", 1)])
                    rs_cnt[par] = C("c_y")
                    o_parity_free[par] = C("c_y")
                    def mk_bn1(E, ti=ti):
                        xt = x1t(ti)
                        E.activation(statscr[:], xt[:], AF.Identity, accum_out=suma[:, ti:ti + 1])
                        return E.activation(statscr[:], xt[:], AF.Square,
                                            accum_out=sqa[:, ti:ti + 1])
                    on("scalar", mk_bn1, waits=[("c_y", C("c_y"))], incs=[("c_bn", 2)])

                    if ti in (7, 15):
                        hs = 0 if ti == 7 else 8
                        def mk_veps(E, hs=hs, gb_=C("c_g1")):
                            E.tensor_scalar_mul(ln1s[:, 16 + hs:24 + hs],
                                                suma[:, hs:hs + 8], 1.0 / 1024.0).then_inc(sems["c_g1"], 1)
                            E.wait_ge(sems["c_g1"], gb_ + 1)
                            E.tensor_scalar_mul(ln1s[:, 32 + hs:40 + hs],
                                                ln1s[:, 16 + hs:24 + hs], -1.0)
                            E.tensor_mul(ln1s[:, hs:hs + 8], ln1s[:, 16 + hs:24 + hs],
                                         ln1s[:, 16 + hs:24 + hs]).then_inc(sems["c_g1"], 1)
                            E.wait_ge(sems["c_g1"], gb_ + 2)
                            return E.scalar_tensor_tensor(ln1s[:, hs:hs + 8], sqa[:, hs:hs + 8],
                                                          1.0 / 1024.0, ln1s[:, hs:hs + 8],
                                                          ALU.mult, ALU.subtract)
                        on("vector", mk_veps, waits=[("c_bn", C("c_bn"))], incs=[("c_ln", 1)])
                        cnt["c_g1"] += 2
                        on("scalar", lambda E, hs=hs:
                            E.activation(ln1s[:, 16 + hs:24 + hs], ln1s[:, hs:hs + 8], AF.Sqrt,
                                         bias=epsb[:], scale=1.0),
                           waits=[("c_ln", C("c_ln"))], incs=[("c_ln", 1)])
                        on("vector", lambda E, hs=hs:
                            E.reciprocal(ln1s[:, hs:hs + 8], ln1s[:, 16 + hs:24 + hs]),
                           waits=[("c_ln", C("c_ln"))], incs=[("c_ln", 1)])

                    if ti >= 8 and ti - 8 < 4:
                        t0 = ti - 8
                        p0 = t0 % 2
                        def mk_norm1e(E, t0=t0, p0=p0):
                            xt = x1t(t0)
                            E.tensor_scalar(x1b(p0), xt[:], ln1s[:, 32 + t0:33 + t0],
                                            ln1s[:, t0:t0 + 1], ALU.add, ALU.mult)
                            return E.tensor_scalar(xt[:], xt[:], ln1s[:, 32 + t0:33 + t0],
                                                   ln1s[:, t0:t0 + 1], ALU.add, ALU.mult)
                        nw = [("c_ln", ln_base + 3)]
                        if t0 >= 2:
                            nw.append(("c_tp", C("c_tp")))
                        on("vector", mk_norm1e, waits=nw, incs=[("c_xb", 1)])
                        tb = 4 + p0
                        tw = [("c_xb", C("c_xb")), ("c_nm", nm_final),
                              ("c_vc", C("c_vc")), ("c_g2", C("c_g2"))]
                        if t0 >= 2:
                            tw.append(("c_te", C("c_te")))
                        def mk_tp(E, p0=p0, tb=tb):
                            inst = None
                            for dc in range(8):
                                inst = E.transpose(psb[tb][:, 128 * dc:128 * (dc + 1)],
                                                   x1b(p0)[:, 128 * dc:128 * (dc + 1)], idb[:])
                            return inst
                        on("tensor", mk_tp, waits=tw, incs=[("c_tp", 1)])
                        ew = [("c_tp", C("c_tp")), ("c_pv", C("c_pv")),
                              ("c_gu", C("c_gu"))]
                        def mk_te(E, t0=t0, tb=tb):
                            src = psb[tb].rearrange("p (dc n) -> p dc n", dc=8)
                            dst = x1T.rearrange("p (dc n) -> p dc n", dc=8)[:, :, 128 * t0:128 * (t0 + 1)]
                            return E.activation(dst, src, AF.Identity)
                        on("scalar", mk_te, waits=ew, incs=[("c_te", 1)])
                y_final = C("c_y")
                o_final = C("c_o")
                ln1_done = C("c_ln")
                tp_q0 = C("c_tp")

                for hc in range(24):
                    on("gpsimd", lambda E, hc=hc:
                        E.dma_start(out=wdTb(hc), in_=wd_b[128 * hc:128 * (hc + 1), :]),
                        waits=[rp_wait["wd"], ("c_o", o_final), ("c_y", y_final),
                               ("c_s", sfin)],
                        incs=[("d_wd", 16)])
                dwd = C("d_wd")

                si_parity = {}
                hm_parity = {}
                te_parity = {}
                y2_parity = {}
                y2_at.clear()
                def emit_quarter_transposes(tq):
                    if True:
                        for tr in range(4):
                            ti = 4 * tq + tr
                            par = tr % 2
                            def mk_norm1(E, ti=ti, par=par):
                                xt = x1t(ti)
                                E.tensor_scalar(x1b(par), xt[:], ln1s[:, 32 + ti:33 + ti],
                                                ln1s[:, ti:ti + 1], ALU.add, ALU.mult)
                                return E.tensor_scalar(xt[:], xt[:], ln1s[:, 32 + ti:33 + ti],
                                                       ln1s[:, ti:ti + 1], ALU.add, ALU.mult)
                            nw = [("c_ln", ln1_done), ("c_tp", C("c_tp"))]
                            on("vector", mk_norm1, waits=nw, incs=[("c_xb", 1)])
                            tb = par
                            tw = [("c_xb", C("c_xb")), ("c_si", C("c_si")),
                                  ("c_hm", C("c_hm"))]
                            pv_ = te_parity.get(tb)
                            if pv_ is not None:
                                tw.append(("c_te", pv_))
                            else:
                                tw.append(("c_y", y_final))
                            def mk_tp(E, par=par, tb=tb):
                                inst = None
                                for dc in range(8):
                                    inst = E.transpose(psb[tb][:, 128 * dc:128 * (dc + 1)],
                                                       x1b(par)[:, 128 * dc:128 * (dc + 1)], idb[:])
                                return inst
                            on("tensor", mk_tp, waits=tw, incs=[("c_tp", 1)])
                            ew = [("c_tp", C("c_tp")), ("c_gu", C("c_gu"))]
                            def mk_te(E, tr=tr, tb=tb):
                                src = psb[tb].rearrange("p (dc n) -> p dc n", dc=8)
                                dst = x1T.rearrange("p (dc n) -> p dc n", dc=8)[:, :, 128 * tr:128 * (tr + 1)]
                                return E.activation(dst, src, AF.Identity)
                            on("scalar", mk_te, waits=ew, incs=[("c_te", 1)])
                            te_parity[tb] = C("c_te")

                for tq in range(NQQ):
                    te_quarter = C("c_te")

                    for hc in range(24):
                        par = hc % 2
                        gb, ub = 0 + par, 2 + par
                        gs_sem, us_sem = f"d_g{par}", f"d_u{par}"
                        on("sync", lambda E, hc=hc, par=par:
                            E.dma_start(out=wgub(0, par),
                                        in_=wg_b.rearrange("(dc p) f -> p dc f", p=128)[:, :, 128 * hc:128 * (hc + 1)]),
                            waits=[("c_gu", g_cnt[par]), ("c_o", o_final),
                                   rp_wait["wg"]] if par else
                                  [("c_gu", g_cnt[par]), ("c_x", x_final),
                                   ("c_pv", C("c_pv")), rp_wait["wg"]],
                            incs=[(gs_sem, 16)])
                        on("sync", lambda E, hc=hc, par=par:
                            E.dma_start(out=wgub(1, par),
                                        in_=wu_b.rearrange("(dc p) f -> p dc f", p=128)[:, :, 128 * hc:128 * (hc + 1)]),
                            waits=[("c_gu", u_cnt[par]), ("c_o", o_final),
                                   rp_wait["wu"]] if par else
                                  [("c_gu", u_cnt[par]), ("c_x", x_final),
                                   ("c_pv", C("c_pv")), rp_wait["wu"]],
                            incs=[(us_sem, 16)])
                        dg, du = C(gs_sem), C(us_sem)
                        muw = [(gs_sem, dg), (us_sem, du), ("c_te", te_quarter)]
                        pv_ = si_parity.get(par)
                        pu_ = hm_parity.get(par)
                        if pv_ is not None:
                            muw.append(("c_si", pv_))
                        if pu_ is not None:
                            muw.append(("c_hm", pu_))
                        if pv_ is None:
                            muw += [("c_tp", C("c_tp")), ("c_y", y_final)]
                        def mk_gu(E, gb=gb, ub=ub, par=par):
                            inst = None
                            for dc in range(8):
                                st_, sp = dc == 0, dc == 7
                                E.matmul(ps[gb][:], wgub(0, par)[:, 128 * dc:128 * (dc + 1)],
                                         x1T[:, 512 * dc:512 * (dc + 1)], start=st_, stop=sp)
                                inst = E.matmul(ps[ub][:], wgub(1, par)[:, 128 * dc:128 * (dc + 1)],
                                                x1T[:, 512 * dc:512 * (dc + 1)], start=st_, stop=sp)
                            return inst
                        on("tensor", mk_gu, waits=muw, incs=[("c_gu", 1)])
                        g_cnt[par] = C("c_gu")
                        u_cnt[par] = C("c_gu")
                        siw = [("c_gu", C("c_gu"))]
                        if hm_parity.get(par) is not None:
                            siw.append(("c_hm", hm_parity[par]))
                        on("scalar", lambda E, gb=gb, par=par:
                            E.activation(gsb[:, 512 * par:512 * (par + 1)], ps[gb][:],
                                         AF.Silu, bias=0.0, scale=DEQ),
                            waits=siw, incs=[("c_si", 1)])
                        si_parity[par] = C("c_si")
                        hw_ = [("c_si", C("c_si")), ("c_dn", C("c_dn"))]
                        if tq == 0 and hc < 2:
                            hw_.append(("c_ve", ve_at[NT - 1]))
                            hw_.append(("c_pv", C("c_pv")))
                        on("vector", lambda E, ub=ub, par=par, hc=hc:
                            E.scalar_tensor_tensor(hT(hc), ps[ub][:], DEQ,
                                                   gsb[:, 512 * par:512 * (par + 1)],
                                                   ALU.mult, ALU.mult),
                            waits=hw_, incs=[("c_hm", 1)])
                        hm_parity[par] = C("c_hm")
                    hm_quarter = C("c_hm")

                    for tr in range(4):
                        ti = 4 * tq + tr
                        par = tr % 2
                        b0, b1 = (4, 5) if par == 0 else (6, 7)
                        dnw = [("c_hm", hm_quarter), ("d_wd", dwd)]
                        pv_ = y2_parity.get(par)
                        if pv_ is not None:
                            dnw.append(("c_y2", pv_))
                        elif par == 0:
                            dnw += [("c_te", te_quarter), ("c_tp", tp_q0),
                                    ("c_vc", C("c_vc")), ("c_g2", C("c_g2")),
                                    ("c_nm", nm_final)]
                        else:
                            dnw += [("c_qke", C("c_qke")), ("c_ve", ve_at[NT - 1]),
                                    ("c_qk", qk_final), ("c_v", v_final)]
                        def mk_dn(E, tr=tr, b0=b0, b1=b1):
                            inst = None
                            for hc in range(24):
                                st_, sp = hc == 0, hc == 23
                                lhsT = hT(hc)[:, 128 * tr:128 * (tr + 1)]
                                E.matmul(ps[b0][:], lhsT, wdTb(hc)[:, 0:512], start=st_, stop=sp)
                                inst = E.matmul(ps[b1][:], lhsT, wdTb(hc)[:, 512:1024],
                                                start=st_, stop=sp)
                            return inst
                        on("tensor", mk_dn, waits=dnw, incs=[("c_dn", 1)])
                        oc = f"d_out{tr % 2}"
                        y2w = [("c_dn", C("c_dn")), (oc, out_cnt[tr % 2]),
                               ("c_o", o_final)]
                        def mk_y2(E, ti=ti, tr=tr, b0=b0, b1=b1):
                            yq = y2q(tr)
                            xt = x1t(ti)
                            E.scalar_tensor_tensor(yq[:, 0:512], ps[b0][:], DEQ, xt[:, 0:512],
                                                   ALU.mult, ALU.add)
                            return E.scalar_tensor_tensor(yq[:, 512:1024], ps[b1][:], DEQ,
                                                          xt[:, 512:1024], ALU.mult, ALU.add)
                        on("vector", mk_y2, waits=y2w, incs=[("c_y2", 1)])
                        y2_parity[par] = C("c_y2")
                        y2_at[ti] = C("c_y2")
                        def mk_bn2(E, tr=tr):
                            yq = y2q(tr)
                            E.activation(statscr[:], yq[:], AF.Identity,
                                         accum_out=suma2[:, tr:tr + 1])
                            return E.activation(statscr[:], yq[:], AF.Square,
                                                accum_out=sqa2[:, tr:tr + 1])
                        on("scalar", mk_bn2, waits=[("c_y2", C("c_y2"))], incs=[("c_bn2", 2)])

                    if tq + 1 < NQQ:
                        emit_quarter_transposes(tq + 1)

                    def mk_veps2(E, gb_=C("c_g1")):
                        E.tensor_scalar_mul(ln2s[:, 4:8], suma2[:], 1.0 / 1024.0).then_inc(sems["c_g1"], 1)
                        E.wait_ge(sems["c_g1"], gb_ + 1)
                        E.tensor_scalar_mul(ln2s[:, 8:12], ln2s[:, 4:8], -1.0)
                        E.tensor_mul(ln2s[:, 0:4], ln2s[:, 4:8], ln2s[:, 4:8]).then_inc(sems["c_g1"], 1)
                        E.wait_ge(sems["c_g1"], gb_ + 2)
                        return E.scalar_tensor_tensor(ln2s[:, 0:4], sqa2[:], 1.0 / 1024.0,
                                                      ln2s[:, 0:4], ALU.mult, ALU.subtract)
                    on("vector", mk_veps2, waits=[("c_bn2", C("c_bn2"))], incs=[("c_ln2", 1)])
                    cnt["c_g1"] += 2
                    on("scalar", lambda E: E.activation(ln2s[:, 4:8], ln2s[:, 0:4], AF.Sqrt,
                                                        bias=epsb[:], scale=1.0),
                       waits=[("c_ln2", C("c_ln2"))], incs=[("c_ln2", 1)])
                    on("vector", lambda E: E.reciprocal(ln2s[:, 0:4], ln2s[:, 4:8]),
                       waits=[("c_ln2", C("c_ln2"))], incs=[("c_ln2", 1)])
                    for tr in range(4):
                        ti = 4 * tq + tr
                        on("vector", lambda E, tr=tr:
                            E.tensor_scalar(y2q(tr)[:], y2q(tr)[:], ln2s[:, 8 + tr:9 + tr],
                                            ln2s[:, tr:tr + 1], ALU.add, ALU.mult),
                            waits=[("c_ln2", C("c_ln2"))], incs=[("c_n2", 1)])
                        oc = f"d_out{tr % 2}"
                        on("scalar", lambda E, s=s, ti=ti, tr=tr:
                            E.dma_start(out=out_d[s, 128 * ti:128 * (ti + 1), :], in_=y2q(tr)),
                            waits=[("c_n2", C("c_n2"))], incs=[(oc, 16)])
                        out_cnt[tr % 2] = C(oc)
                dn_final = C("c_dn")
                y2_final = C("c_y2")

            on("scalar", lambda E: E.nop(),
               waits=[("d_out0", out_cnt[0]), ("d_out1", out_cnt[1])])

        walk("audit")
        _check_schedule(audit_streams)

        block = e(nc.Block())

        @block.sync
        def _(E):
            walk("sync")

        @block.gpsimd
        def _(E):
            walk("gpsimd")

        @block.tensor
        def _(E):
            walk("tensor")

        @block.vector
        def _(E):
            walk("vector")

        @block.scalar
        def _(E):
            walk("scalar")

    return nc


def _check_schedule(streams):
    if not streams:
        return
    pos = {k: 0 for k in streams}
    cnt = {}
    progress = True
    while progress:
        progress = False
        for eng, ops in streams.items():
            while pos[eng] < len(ops):
                waits, incs = ops[pos[eng]]
                if all(cnt.get(s_, 0) >= v for s_, v in waits):
                    for s_, n in incs:
                        cnt[s_] = cnt.get(s_, 0) + n
                    pos[eng] += 1
                    progress = True
                else:
                    break
    stuck = {k: (pos[k], len(v)) for k, v in streams.items() if pos[k] < len(v)}
    if stuck:
        detail = {}
        for eng in stuck:
            waits, _ = streams[eng][pos[eng]]
            detail[eng] = [(s_, v, cnt.get(s_, 0)) for s_, v in waits
                           if cnt.get(s_, 0) < v]
        raise RuntimeError(f"schedule deadlock: {stuck}; blocking: {detail}")


_PROG_CACHE = {}


def _get_program():
    if "nc" not in _PROG_CACHE:
        _PROG_CACHE["nc"] = build_program()
    return _PROG_CACHE["nc"]


def kernel(L0_attn_x, L0_attn_W_qkv, L0_attn_b_qkv, L0_attn_W_out, L0_attn_b_out,
           L0_ffn_w_gate, L0_ffn_w_up, L0_ffn_w_down):
    x = np.asarray(L0_attn_x, dtype=np.float32)
    wqkv = np.ascontiguousarray(np.asarray(L0_attn_W_qkv, dtype=np.float32))
    bqkv = np.asarray(L0_attn_b_qkv, dtype=np.float32)
    wout = np.ascontiguousarray(np.asarray(L0_attn_W_out, dtype=np.float32))
    bout = np.asarray(L0_attn_b_out, dtype=np.float32)
    wg = np.asarray(L0_ffn_w_gate, dtype=np.int32)
    wu = np.asarray(L0_ffn_w_up, dtype=np.int32)
    wd = np.asarray(L0_ffn_w_down, dtype=np.int32)

    wgT = np.ascontiguousarray(wg.T)
    wuT = np.ascontiguousarray(wu.T)
    wdT = np.ascontiguousarray(wd.T)
    bqkT = np.ascontiguousarray(bqkv[0:2048].reshape(16, 128).T)
    bvb = np.ascontiguousarray(np.broadcast_to(bqkv[2048:3072], (128, 1024)))
    bob = np.ascontiguousarray(bout.reshape(1, 1024))
    maskT = np.where(np.arange(128)[:, None] > np.arange(128)[None, :],
                     np.float32(-80000.0), np.float32(0.0)).astype(np.float32)
    ident = np.eye(128, dtype=ml_dtypes.bfloat16)

    nc = _get_program()
    in_maps = []
    for c in range(NCORE):
        xs = np.ascontiguousarray(x[NSEQ * c:NSEQ * (c + 1)])
        xTs = np.ascontiguousarray(xs.transpose(0, 2, 1))
        in_maps.append({
            "x": xs, "xT": xTs, "wqkv": wqkv, "wout": wout,
            "wgT": wgT, "wuT": wuT, "wdT": wdT,
            "bqkT": bqkT, "bvb": bvb, "bob": bob,
            "maskT": maskT, "ident": ident,
        })
    res = run_bass_kernel_spmd(nc, in_maps, core_ids=list(range(NCORE)))
    return np.concatenate([r["out"] for r in res.results], axis=0)


# version 47
# speedup vs baseline: 1.0264x; 1.0264x over previous
import numpy as np
import ml_dtypes
import concourse.bass as bass
import concourse.mybir as mybir
from concourse.bass_utils import run_bass_kernel_spmd
from contextlib import ExitStack

F32 = mybir.dt.float32
BF16 = mybir.dt.bfloat16
I32 = mybir.dt.int32
AF = mybir.ActivationFunctionType
ALU = mybir.AluOpType

NCORE = 8
NSEQ = 4
S = 2048
D = 1024
H = 16
FQ = 3072
FF = 3072
NPAIR = 8
QW = 512
NQQ = 4
NT = 16
EPS = 1e-5
DEQ = 1.0 / 127.0


def build_program():
    nc = bass.Bass("TRN2", target_bir_lowering=False)

    x_d = nc.dram_tensor("x", [NSEQ, S, D], F32, kind="ExternalInput").ap()
    xT_d = nc.dram_tensor("xT", [NSEQ, D, S], F32, kind="ExternalInput").ap()
    wqkv_d = nc.dram_tensor("wqkv", [D, FQ], F32, kind="ExternalInput").ap()
    wout_d = nc.dram_tensor("wout", [D, D], F32, kind="ExternalInput").ap()
    wgT_d = nc.dram_tensor("wgT", [D, FF], I32, kind="ExternalInput").ap()
    wuT_d = nc.dram_tensor("wuT", [D, FF], I32, kind="ExternalInput").ap()
    wdT_d = nc.dram_tensor("wdT", [FF, D], I32, kind="ExternalInput").ap()
    bqkT_d = nc.dram_tensor("bqkT", [128, 16], F32, kind="ExternalInput").ap()
    bvb_d = nc.dram_tensor("bvb", [128, D], F32, kind="ExternalInput").ap()
    bob_d = nc.dram_tensor("bob", [128, D], F32, kind="ExternalInput").ap()
    maskT_d = nc.dram_tensor("maskT", [128, 128], F32, kind="ExternalInput").ap()
    ident_d = nc.dram_tensor("ident", [128, 128], BF16, kind="ExternalInput").ap()
    out_d = nc.dram_tensor("out", [NSEQ, S, D], F32, kind="ExternalOutput").ap()

    wqkv_b = nc.dram_tensor("wqkv_b", [D, FQ], BF16, kind="Internal").ap()
    wout_b = nc.dram_tensor("wout_b", [D, D], BF16, kind="Internal").ap()
    wg_b = nc.dram_tensor("wg_b", [D, FF], BF16, kind="Internal").ap()
    wu_b = nc.dram_tensor("wu_b", [D, FF], BF16, kind="Internal").ap()
    wd_b = nc.dram_tensor("wd_b", [FF, D], BF16, kind="Internal").ap()

    with ExitStack() as es:
        e = es.enter_context

        T1 = e(nc.sbuf_tensor("T1", [128, 32768], BF16))
        T2 = e(nc.sbuf_tensor("T2", [128, 16384], BF16))
        T3 = e(nc.sbuf_tensor("T3", [128, 16384], BF16))
        T4 = e(nc.sbuf_tensor("T4", [128, 24576], BF16))
        pT = e(nc.sbuf_tensor("pT", [128, 2048], BF16))
        gsb = e(nc.sbuf_tensor("gsb", [128, 1024], BF16))
        x1bb = e(nc.sbuf_tensor("x1bb", [128, 2048], BF16))

        bqkT = e(nc.sbuf_tensor("bqkT_s", [128, 16], F32))
        bvb = e(nc.sbuf_tensor("bvb_s", [128, D], F32))
        bob = e(nc.sbuf_tensor("bob_s", [128, D], F32))
        mskt = e(nc.sbuf_tensor("mskt", [128, 128], F32))
        idb = e(nc.sbuf_tensor("idb", [128, 128], BF16))
        negc = e(nc.sbuf_tensor("negc", [128, 1], F32))
        ones64 = e(nc.sbuf_tensor("ones64", [128, 64], BF16))
        ones128b = e(nc.sbuf_tensor("ones128b", [128, 1], BF16))
        rrb = e(nc.sbuf_tensor("rrb", [128, 512], F32))
        suma = e(nc.sbuf_tensor("suma", [128, 16], F32))
        sqa = e(nc.sbuf_tensor("sqa", [128, 16], F32))
        suma2 = e(nc.sbuf_tensor("suma2", [128, 4], F32))
        sqa2 = e(nc.sbuf_tensor("sqa2", [128, 4], F32))
        epsb = e(nc.sbuf_tensor("epsb", [128, 1], F32))
        bobres = e(nc.sbuf_tensor("bobres", [128, 1024], F32))
        ln1s = e(nc.sbuf_tensor("ln1s", [128, 48], F32))
        ln2s = e(nc.sbuf_tensor("ln2s", [128, 12], F32))


        T1f = T1.ap().bitcast(F32)
        T3f = T3.ap().bitcast(F32)
        T4f = T4.ap().bitcast(F32)

        def xTb(dc):
            return T1.ap()[:, 2048 * dc:2048 * (dc + 1)]

        def qkw(slot):
            return T1.ap()[:, 16384 + 2048 * slot:16384 + 2048 * (slot + 1)]

        def Vw(dc):
            return T1.ap()[:, 20480 + 1024 * dc:20480 + 1024 * (dc + 1)]

        def x1t(ti):
            return T1f[:, 1024 * ti:1024 * (ti + 1)]

        V2 = T2.ap()[:, 0:16384]

        def hT(hc):
            return T2.ap()[:, 512 * hc:512 * (hc + 1)]

        x1T = T2.ap()[:, 12288:16384]

        valsT = T3.ap()[:, 0:16384]

        def wgub(mat, par):
            if par == 0:
                return pT.ap()[:, 1024 * mat:1024 * (mat + 1)]
            off = 2048 * mat + 1024
            return T3.ap()[:, off:off + 1024]

        def y2q(tr):
            return T3f[:, 3072 + 1024 * tr:3072 + 1024 * (tr + 1)]

        def qkTs(chunk):
            return T4.ap()[:, 2048 * chunk:2048 * (chunk + 1)]

        def woutb(j):
            base = 8192 + 1024 * j if j < 4 else 1024 * (j - 4)
            return T4.ap()[:, base:base + 1024]

        def residt(slot):
            return T4f[:, 6144 + 1024 * slot:6144 + 1024 * (slot + 1)]

        def wdTb(hc):
            return T4.ap()[:, 1024 * hc:1024 * (hc + 1)]

        def x1b(par):
            return x1bb.ap()[:, 1024 * par:1024 * (par + 1)]

        ps = [e(nc.psum_tensor(f"ps{i}", [128, 512], F32)) for i in range(8)]
        psb = [p.ap().bitcast(BF16) for p in ps]

        sem_names = [
            "d_xT", "d_wq", "d_vw", "d_rs0", "d_rs1", "d_wo", "d_wd",
            "d_g0", "d_g1", "d_u0", "d_u1", "d_out0", "d_out1",
            "d_ri0", "d_ri1", "d_ro0", "d_ro1", "d_cst",
            "c_qk", "c_qke", "c_v", "c_ve", "c_s", "c_m", "c_x",
            "c_pv", "c_pq", "c_rr", "c_rb", "c_nm", "c_o", "c_y", "c_bn",
            "c_ln", "c_xb", "c_tp", "c_te", "c_gu", "c_si", "c_hm",
            "c_dn", "c_y2", "c_bn2", "c_ln2", "c_n2", "c_g1", "c_g2",
            "c_lg", "c_vc",
        ]
        sems = {n: e(nc.semaphore(n)) for n in sem_names}

        audit_streams = {}

        def walk(which):
            cnt = {n: 0 for n in sem_names}
            waited = {n: -1 for n in sem_names}

            def on(engine, fn, waits=(), incs=()):
                wl = [(s_, v) for (s_, v) in waits if v > 0]
                if which == "audit":
                    audit_streams.setdefault(engine, []).append((list(wl), list(incs)))
                inst = None
                if which == engine:
                    E = getattr(nc, engine)
                    for s_, v in wl:
                        if waited[s_] < v:
                            E.wait_ge(sems[s_], v)
                            waited[s_] = v
                    inst = fn(E)
                for s_, n in incs:
                    cnt[s_] += n
                    if inst is not None:
                        inst.then_inc(sems[s_], n)
                return inst

            def C(s_):
                return cnt[s_]

            on("sync", lambda E: E.dma_start(out=bqkT[:], in_=bqkT_d), incs=[("d_cst", 16)])
            on("sync", lambda E: E.dma_start(out=bvb[:], in_=bvb_d), incs=[("d_cst", 16)])
            on("sync", lambda E: E.dma_start(out=bob[:], in_=bob_d), incs=[("d_cst", 16)])
            on("sync", lambda E: E.dma_start(out=mskt[:], in_=maskT_d), incs=[("d_cst", 16)])
            on("sync", lambda E: E.dma_start(out=idb[:], in_=ident_d), incs=[("d_cst", 16)])

            def mk_init(E):
                E.memset(negc[:], -4.0)
                E.memset(epsb[:], EPS)
                E.memset(ones128b[:], 1.0)
                return E.memset(ones64[:], 1.0)
            on("vector", mk_init, incs=[("c_m", 1)])


            def qkw_src_f32(pj, c):
                return wqkv_d.rearrange("(dc p) f -> p dc f", p=128)[
                    :, :, 1024 * c + 128 * pj:1024 * c + 128 * (pj + 1)]

            def qkw_src_b(pj, c):
                return wqkv_b.rearrange("(dc p) f -> p dc f", p=128)[
                    :, :, 1024 * c + 128 * pj:1024 * c + 128 * (pj + 1)]

            def qkw_dst(slot, c):
                return qkw(slot).rearrange("p (dc c f) -> p dc c f", dc=8, c=2)[:, :, c, :]

            for c in range(2):
                on("gpsimd", lambda E, c=c:
                    E.dma_start(out=qkw_dst(0, c), in_=qkw_src_f32(0, c)),
                    incs=[("d_wq", 16)])
            for dc in range(8):
                on("gpsimd", lambda E, dc=dc:
                    E.dma_start(out=xTb(dc), in_=xT_d[0, 128 * dc:128 * (dc + 1), :]),
                    incs=[("d_xT", 16)])
            for c in range(2):
                on("gpsimd", lambda E, c=c:
                    E.dma_start(out=qkw_dst(1, c), in_=qkw_src_f32(1, c)),
                    incs=[("d_wq", 16)])
            on("gpsimd", lambda E:
                E.dma_start(out=T1.ap()[:, 20480:28672].rearrange("p (dc f) -> p dc f", dc=8),
                            in_=wqkv_d.rearrange("(dc p) f -> p dc f", p=128)[:, :, 2048:3072]),
                incs=[("d_vw", 16)])
            s0_dwq_at = {0: 32, 1: 64}
            def qk_done_cnt(p):
                return 4 * (p + 1) if p <= 1 else 8 + 8 * (p - 1)
            for nxt in range(2, NPAIR):
                for c in range(2):
                    on("gpsimd", lambda E, nxt=nxt, c=c:
                        E.dma_start(out=qkw_dst(nxt % 2, c), in_=qkw_src_f32(nxt, c)),
                        waits=[("c_qk", qk_done_cnt(nxt - 2))], incs=[("d_wq", 16)])
                s0_dwq_at[nxt] = C("d_wq")

            rp_wait = {}
            rp_list = [("wout", wout_d, wout_b), ("wg", wgT_d, wg_b),
                       ("wu", wuT_d, wu_b), ("wd", wdT_d, wd_b),
                       ("wqkv", wqkv_d, wqkv_b)]
            for i, (key, src, dst) in enumerate(rp_list):
                on("gpsimd", lambda E, src=src, dst=dst:
                    E.dma_start(out=dst, in_=src), incs=[("d_ro0", 16)])
                rp_wait[key] = ("d_ro0", C("d_ro0"))

            dn_final = 0
            y2_final = 0
            y_final = 0
            o_final = 0
            x_final = 0
            nm_final = 0
            out_cnt = [0, 0]
            rs_cnt = [0, 0]
            g_cnt = [0, 0]
            u_cnt = [0, 0]
            gstep = 0
            cx_at = {}
            y2_at = {}

            last_vc = {4: 0, 6: 0}
            for s in range(NSEQ):
                if s > 0:
                    for dc in range(8):
                        on("gpsimd", lambda E, s=s, dc=dc:
                            E.dma_start(out=xTb(dc), in_=xT_d[s, 128 * dc:128 * (dc + 1), :]),
                            waits=[("c_y2", y2_at.get(7, y2_final))], incs=[("d_xT", 16)])
                    on("sync", lambda E:
                        E.dma_start(out=T1.ap()[:, 20480:28672].rearrange("p (dc f) -> p dc f", dc=8),
                                    in_=wqkv_b.rearrange("(dc p) f -> p dc f", p=128)[:, :, 2048:3072]),
                        waits=[rp_wait["wqkv"], ("c_y2", y2_at.get(13, y2_final))], incs=[("d_vw", 16)])
                    for pj in range(2):
                        for c in range(2):
                            on("sync", lambda E, pj=pj, c=c:
                                E.dma_start(out=qkw_dst(pj, c), in_=qkw_src_b(pj, c)),
                                waits=[rp_wait["wqkv"], ("c_y2", y2_at.get(9, y2_final))],
                                incs=[("d_wq", 16)])
                dxT = C("d_xT")
                dvw = C("d_vw")
                if s == 0:
                    dwq_at = dict(s0_dwq_at)
                else:
                    dwq_at = {0: C("d_wq") - 32, 1: C("d_wq")}

                qke_at = {}
                sc_pair = {}
                qk_state = {"pj": 0, "u": 0, "half": 0, "upf": {}}
                dn_final_seq = dn_final

                def emit_qk_unit(upfront=False):
                    pj = qk_state["pj"]
                    if pj >= NPAIR:
                        return
                    u = qk_state["u"]
                    wslot = pj % 2
                    cslot = pj % 3
                    for c, th in [divmod(u, 2)]:
                        if upfront:
                            qb0, qb1 = (0, 1) if u % 2 == 0 else (2, 3)
                        else:
                            qb0, qb1 = 6, 7
                        if True:
                            waits = [("d_wq", dwq_at[pj]), ("d_xT", dxT)]
                            if upfront:
                                waits += [("c_si", C("c_si")), ("c_hm", C("c_hm")),
                                          ("c_tp", C("c_tp")), ("c_te", C("c_te")),
                                          ("c_y", C("c_y")), ("c_rb", C("c_rb")),
                                          ("c_x", C("c_x")),
                                          ("c_qke", qk_state["upf"].get(u % 2, 0))]
                            else:
                                waits += [("c_qke", C("c_qke")), ("c_ve", C("c_ve")),
                                          ("c_y2", y2_final)]
                            def mk_qk(E, wslot=wslot, c=c, th=th, qb0=qb0, qb1=qb1):
                                inst = None
                                for dc in range(8):
                                    st_, sp = dc == 0, dc == 7
                                    lhsT = qkw(wslot)[:, 256 * dc + 128 * c:256 * dc + 128 * (c + 1)]
                                    E.matmul(ps[qb0][:], lhsT,
                                             xTb(dc)[:, 1024 * th:1024 * th + 512],
                                             start=st_, stop=sp)
                                    inst = E.matmul(ps[qb1][:], lhsT,
                                                    xTb(dc)[:, 1024 * th + 512:1024 * (th + 1)],
                                                    start=st_, stop=sp)
                                return inst
                            on("tensor", mk_qk, waits=waits, incs=[("c_qk", 1)])
                            ew = [("c_qk", C("c_qk")), ("d_cst", 16), ("c_dn", dn_final_seq)]
                            if pj >= 3:
                                ew.append(("c_s", sc_pair[pj - 3]))
                            def mk_qke(E, pj=pj, cslot=cslot, c=c, th=th, qb0=qb0, qb1=qb1):
                                dst = qkTs(2 * cslot + c)
                                bcol = pj + 8 * c
                                E.activation(dst[:, 1024 * th:1024 * th + 512], ps[qb0][:],
                                             AF.Identity, bias=bqkT[:, bcol:bcol + 1], scale=1.0)
                                return E.activation(dst[:, 1024 * th + 512:1024 * (th + 1)], ps[qb1][:],
                                                    AF.Identity, bias=bqkT[:, bcol:bcol + 1], scale=1.0)
                            on("scalar", mk_qke, waits=ew, incs=[("c_qke", 1)])
                            if upfront:
                                qk_state["upf"][u % 2] = C("c_qke")
                    if u < 3:
                        qk_state["u"] = u + 1
                        return
                    qke_at[pj] = C("c_qke")
                    nxt = pj + 2
                    if nxt < NPAIR and s > 0:
                        for c in range(2):
                            on("sync", lambda E, nxt=nxt, wslot=wslot, c=c:
                                E.dma_start(out=qkw_dst(wslot, c), in_=qkw_src_b(nxt, c)),
                                waits=[("c_qk", C("c_qk")), rp_wait["wqkv"]],
                                incs=[("d_wq", 16)])
                        dwq_at[nxt] = C("d_wq")
                    qk_state["pj"] = pj + 1
                    qk_state["u"] = 0

                def emit_qk_half():
                    pj = qk_state["pj"]
                    if pj >= NPAIR:
                        return
                    u = qk_state["u"]
                    half = qk_state["half"]
                    wslot = pj % 2
                    cslot = pj % 3
                    c, th = divmod(u, 2)
                    waits = [("d_wq", dwq_at[pj]), ("d_xT", dxT),
                             ("c_qke", C("c_qke")), ("c_ve", C("c_ve")),
                             ("c_y2", y2_final)]
                    def mk_qkh(E, wslot=wslot, c=c, th=th, half=half):
                        inst = None
                        for dc in range(8):
                            lhsT = qkw(wslot)[:, 256 * dc + 128 * c:256 * dc + 128 * (c + 1)]
                            inst = E.matmul(
                                ps[7][:], lhsT,
                                xTb(dc)[:, 1024 * th + 512 * half:1024 * th + 512 * half + 512],
                                start=dc == 0, stop=dc == 7)
                        return inst
                    on("tensor", mk_qkh, waits=waits, incs=[("c_qk", 1)])
                    ew = [("c_qk", C("c_qk")), ("d_cst", 16), ("c_dn", dn_final_seq)]
                    if pj >= 3:
                        ew.append(("c_s", sc_pair[pj - 3]))
                    def mk_qkeh(E, pj=pj, cslot=cslot, c=c, th=th, half=half):
                        dst = qkTs(2 * cslot + c)
                        bcol = pj + 8 * c
                        return E.activation(
                            dst[:, 1024 * th + 512 * half:1024 * th + 512 * half + 512],
                            ps[7][:], AF.Identity, bias=bqkT[:, bcol:bcol + 1],
                            scale=1.0)
                    on("scalar", mk_qkeh, waits=ew, incs=[("c_qke", 1)])
                    if half == 0:
                        qk_state["half"] = 1
                        return
                    qk_state["half"] = 0
                    if u < 3:
                        qk_state["u"] = u + 1
                        return
                    qke_at[pj] = C("c_qke")
                    nxt = pj + 2
                    if nxt < NPAIR and s > 0:
                        for c2 in range(2):
                            on("sync", lambda E, nxt=nxt, wslot=wslot, c2=c2:
                                E.dma_start(out=qkw_dst(wslot, c2), in_=qkw_src_b(nxt, c2)),
                                waits=[("c_qk", C("c_qk")), rp_wait["wqkv"]],
                                incs=[("d_wq", 16)])
                        dwq_at[nxt] = C("d_wq")
                    qk_state["pj"] = pj + 1
                    qk_state["u"] = 0

                def emit_qk_pair(upfront=False):
                    for _ in range(4):
                        emit_qk_unit(upfront)

                emit_qk_pair(upfront=True)
                emit_qk_pair(upfront=True)

                v_state = {"ti": 0}
                ve_at = {}

                def emit_v(upto):
                    while v_state["ti"] < upto:
                        ti = v_state["ti"]
                        waits = [("d_xT", dxT), ("d_vw", dvw), ("c_qke", C("c_qke"))]
                        if ti > 0:
                            waits.append(("c_ve", ve_at[ti - 1]))
                        else:
                            waits.append(("c_qke", qke_at[1]))
                        def mk_v(E, ti=ti):
                            inst = None
                            for dc in range(8):
                                st_, sp = dc == 0, dc == 7
                                lhsT = xTb(dc)[:, 128 * ti:128 * (ti + 1)]
                                E.matmul(ps[6][:], lhsT, Vw(dc)[:, 0:512], start=st_, stop=sp)
                                inst = E.matmul(ps[7][:], lhsT, Vw(dc)[:, 512:1024], start=st_, stop=sp)
                            return inst
                        on("tensor", mk_v, waits=waits, incs=[("c_v", 1)])
                        def mk_ve(E, ti=ti):
                            E.scalar_tensor_tensor(V2[:, 1024 * ti:1024 * ti + 512], ps[6][:],
                                                   1.0, bvb[:, 0:512], ALU.mult, ALU.add)
                            return E.scalar_tensor_tensor(V2[:, 1024 * ti + 512:1024 * (ti + 1)],
                                                          ps[7][:], 1.0, bvb[:, 512:1024],
                                                          ALU.mult, ALU.add)
                        on("vector", mk_ve,
                           waits=[("c_v", C("c_v")), ("d_cst", 80), ("c_dn", dn_final_seq)],
                           incs=[("c_ve", 1)])
                        ve_at[ti] = C("c_ve")
                        v_state["ti"] = ti + 1

                pending_norm = []
                def vbank(pj_, qq_):
                    if pj_ == 0:
                        return 4
                    return 4 if (4 * pj_ + qq_) % 2 == 0 else 6

                def flush_norm():
                    while pending_norm:
                        pending_norm.pop(0)()

                def norm_count_logical():
                    return C("c_nm") + len(pending_norm)

                for pj in range(NPAIR):
                    cslot = pj % 3
                    qc = 2048 * (2 * cslot)
                    kc = 2048 * (2 * cslot + 1)
                    cs_at = {}
                    cm_at = {}
                    steps = []
                    for qq in range(NQQ):
                        nkj = 4 * qq + 4
                        for kj in range(nkj):
                            steps.append((qq, kj, kj == 0, kj == nkj - 1))
                    ns = len(steps)

                    def scores_step(idx, pj=pj, qc=qc, kc=kc):
                        qq, kj, first, last = steps[idx]
                        gl = gstep + idx
                        par = gl % 2
                        b0, b1 = (0, 1) if par == 0 else (2, 3)
                        diag = kj >= 4 * qq
                        joff = 128 * (kj - 4 * qq) if diag else 0
                        n = QW - joff
                        waits = [("c_qke", qke_at[pj]), ("c_m", 1)]
                        if gl >= 2:
                            waits.append(("c_x", 2 * (gl - 1)))
                        def mk(E, kj=kj, qq=qq, joff=joff, n=n, b0=b0, b1=b1, qc=qc, kc=kc):
                            qcol = qc + QW * qq + joff
                            kcol = kc + 128 * kj
                            E.matmul(ps[b0][:, joff:joff + n],
                                     T4.ap()[0:64, kcol:kcol + 128],
                                     T4.ap()[0:64, qcol:qcol + n], start=True, stop=True)
                            return E.matmul(ps[b1][:, joff:joff + n],
                                            T4.ap()[64:128, kcol:kcol + 128],
                                            T4.ap()[64:128, qcol:qcol + n], start=True, stop=True)
                        on("tensor", mk, waits=waits, incs=[("c_s", 1)])
                        cs_at[gl] = C("c_s")
                        if diag:
                            def mkm(E, b0=b0, b1=b1, joff=joff):
                                E.tensor_tensor(ps[b0][:, joff:joff + 128],
                                                ps[b0][:, joff:joff + 128], mskt[:], ALU.add)
                                return E.tensor_tensor(ps[b1][:, joff:joff + 128],
                                                       ps[b1][:, joff:joff + 128], mskt[:], ALU.add)
                            on("vector", mkm, waits=[("c_s", cs_at[gl])], incs=[("c_m", 1)])
                        cm_at[gl] = C("c_m")
                        return (par, joff, n, diag)

                    meta = {0: scores_step(0)}
                    for idx in range(ns):
                        qq, kj, first, last = steps[idx]
                        gl = gstep + idx
                        if pj == 0 and kj == 0:
                            emit_v(min(NT, 4 * qq + 4))
                        elif pj == 1 and idx == 0:
                            emit_v(NT)
                        if idx + 1 < ns:
                            meta[idx + 1] = scores_step(idx + 1)
                        par, joff, n, diag = meta[idx]
                        pcol = 1024 * par
                        b0, b1 = (0, 1) if par == 0 else (2, 3)
                        xw2 = [("c_m", cm_at[gl])] if diag else [("c_s", cs_at[gl])]
                        def mex0(E, b0=b0, pcol=pcol, joff=joff, n=n):
                            return E.activation(pT[:, pcol + joff:pcol + joff + n],
                                                ps[b0][:, joff:joff + n],
                                                AF.Exp, bias=negc[:], scale=0.125)
                        def mex1(E, b1=b1, pcol=pcol, joff=joff, n=n):
                            return E.activation(pT[:, pcol + 512 + joff:pcol + 512 + joff + n],
                                                ps[b1][:, joff:joff + n],
                                                AF.Exp, bias=negc[:], scale=0.125)
                        on("scalar", mex0, waits=xw2, incs=[("c_x", 1)])
                        on("scalar", mex1, waits=xw2, incs=[("c_x", 1)])
                        cx_at[gl] = C("c_x")
                        assert cx_at[gl] == 2 * (gl + 1), (gl, cx_at[gl])
                        if first:
                            flush_norm()
                        if pj >= 1 and kj in (1, 2):
                            emit_qk_half()
                        vb = vbank(pj, qq)
                        pvw = [("c_x", cx_at[gl]), ("c_ve", ve_at[kj])]
                        if first:
                            pvw.append(("c_vc", last_vc[vb]))
                            pvw.append(("c_y2", y2_final))
                            if pj == 1 and vb == 6:
                                pvw.append(("c_ve", ve_at[NT - 1]))
                        def mpv(E, kj=kj, pj=pj, pcol=pcol, joff=joff, n=n,
                                first=first, last=last, vb=vb):
                            vb0 = V2[:, 1024 * kj + 128 * pj:1024 * kj + 128 * pj + 64]
                            vb1 = V2[:, 1024 * kj + 128 * pj + 64:1024 * kj + 128 * (pj + 1)]
                            E.matmul(ps[vb][0:64, joff:joff + n], vb0,
                                     pT[:, pcol + joff:pcol + joff + n], start=first, stop=last)
                            return E.matmul(ps[vb][64:128, joff:joff + n], vb1,
                                            pT[:, pcol + 512 + joff:pcol + 512 + joff + n],
                                            start=first, stop=last, tile_position=(0, 64))
                        on("tensor", mpv, waits=pvw, incs=[])
                        def mpr(E, kj=kj, pcol=pcol, joff=joff, n=n, first=first, last=last):
                            E.matmul(ps[5][0:64, joff:joff + n], ones64[:],
                                     pT[:, pcol + joff:pcol + joff + n], start=first, stop=last)
                            return E.matmul(ps[5][64:128, joff:joff + n], ones64[:],
                                            pT[:, pcol + 512 + joff:pcol + 512 + joff + n],
                                            start=first, stop=last, tile_position=(0, 64))
                        prw = [("c_x", cx_at[gl])]
                        if first:
                            prw.append(("c_g2", C("c_g2")))
                        on("tensor", mpr, waits=prw,
                           incs=[("c_pv", 1)] if last else [])
                        if last:
                            def mcp(E, pj=pj, qq=qq, vb=vb):
                                return E.tensor_copy(
                                    valsT[:, 2048 * pj + QW * qq:2048 * pj + QW * (qq + 1)],
                                    ps[vb][:, 0:512])
                            cpw = [("c_pv", C("c_pv"))]
                            if pj == 0 and qq == 0 and s > 0:
                                cpw.append(("c_o", o_final))
                                cpw.append(("c_gu", C("c_gu")))
                                cpw.append(("d_out0", out_cnt[0]))
                                cpw.append(("d_out1", out_cnt[1]))
                            on("vector", mcp, waits=cpw, incs=[("c_vc", 1)])
                            last_vc[vb] = C("c_vc")

                            def emit_norm(pj=pj, qq=qq, pv_=C("c_pv"),
                                          nm_=C("c_nm"), vc_=C("c_vc")):
                                on("scalar",
                                   lambda E: E.activation(bobres[:, 0:512],
                                                          ps[5][:, 0:512], AF.Ln),
                                   waits=[("c_pv", pv_), ("c_nm", nm_)],
                                   incs=[("c_g2", 1)])
                                def mexr(E):
                                    E.activation(rrb[:], bobres[:, 0:512],
                                                 AF.Exp, scale=-1.0)
                                    return E.activation(bobres[:, 1016:1024],
                                                        rrb[:, 504:512], AF.Identity)
                                on("scalar", mexr, waits=[("c_g2", C("c_g2"))],
                                   incs=[("c_lg", 1)])
                                def mnm(E, pj=pj, qq=qq):
                                    col = 2048 * pj + QW * qq
                                    return E.tensor_mul(valsT[:, col:col + QW],
                                                        valsT[:, col:col + QW],
                                                        rrb[:])
                                on("vector", mnm,
                                   waits=[("c_lg", C("c_lg")), ("c_vc", vc_)],
                                   incs=[("c_nm", 1)])
                            pending_norm.append(emit_norm)
                    gstep += ns
                    sc_pair[pj] = C("c_s")
                flush_norm()
                nm_final = C("c_nm")
                x_final = C("c_x")
                qk_final = C("c_qk")
                v_final = C("c_v")
                sfin = C("c_s")

                wout_src = wout_b.rearrange("(c p) f -> p c f", p=128)
                on("sync", lambda E: E.dma_start(
                        out=T4.ap()[:, 8192:12288].rearrange("p (c f) -> p c f", c=4),
                        in_=wout_src[:, 0:4, :]),
                   waits=[rp_wait["wout"], ("c_s", sc_pair[5])], incs=[("d_wo", 16)])
                on("sync", lambda E: E.dma_start(
                        out=T4.ap()[:, 0:4096].rearrange("p (c f) -> p c f", c=4),
                        in_=wout_src[:, 4:8, :]),
                   waits=[("c_s", sc_pair[6])], incs=[("d_wo", 16)])
                dwo = C("d_wo")
                o_parity_free = {}
                ln_base = C("c_ln")
                for ti in range(NT):
                    par = ti % 2
                    b0, b1 = (0, 1) if par == 0 else (2, 3)
                    rs = f"d_rs{par}"
                    on("gpsimd", lambda E, s=s, ti=ti, par=par:
                        E.dma_start(out=residt(par), in_=x_d[s, 128 * ti:128 * (ti + 1), :]),
                        waits=[("c_y", rs_cnt[par]), ("c_dn", dn_final)],
                        incs=[(rs, 16)])
                    drs = C(rs)
                    waits = [("c_nm", nm_final), ("d_wo", dwo)]
                    pv_ = o_parity_free.get(par)
                    if pv_ is not None:
                        waits.append(("c_y", pv_))
                    else:
                        waits.append(("c_x", x_final))
                        waits.append(("c_rb", C("c_rb")))
                    def mk_op(E, ti=ti, b0=b0, b1=b1):
                        inst = None
                        for j in range(8):
                            st_, sp = j == 0, j == 7
                            lhsT = valsT[:, 2048 * j + 128 * ti:2048 * j + 128 * (ti + 1)]
                            E.matmul(ps[b0][:], lhsT, woutb(j)[:, 0:512],
                                     start=st_, stop=sp)
                            inst = E.matmul(ps[b1][:], lhsT, woutb(j)[:, 512:1024],
                                            start=st_, stop=sp)
                        return inst
                    on("tensor", mk_op, waits=waits, incs=[("c_o", 1)])
                    def mk_y(E, ti=ti, b0=b0, b1=b1, par=par):
                        xt = x1t(ti)
                        E.tensor_add(bobres[:], bob[:], residt(par)[:])
                        E.tensor_add(xt[:, 0:512], ps[b0][:], bobres[:, 0:512])
                        return E.tensor_add(xt[:, 512:1024], ps[b1][:], bobres[:, 512:1024])
                    yw = [("c_o", C("c_o")), (rs, drs), ("d_cst", 80),
                          ("c_qk", qk_final), ("c_v", v_final)]
                    on("vector", mk_y, waits=yw, incs=[("c_y", 1)])
                    rs_cnt[par] = C("c_y")
                    o_parity_free[par] = C("c_y")
                    def mk_bn1(E, ti=ti):
                        xt = x1t(ti)
                        E.activation(gsb[:], xt[:], AF.Identity, accum_out=suma[:, ti:ti + 1])
                        return E.activation(gsb[:], xt[:], AF.Square,
                                            accum_out=sqa[:, ti:ti + 1])
                    on("scalar", mk_bn1, waits=[("c_y", C("c_y"))], incs=[("c_bn", 2)])

                    if ti in (7, 15):
                        hs = 0 if ti == 7 else 8
                        def mk_veps(E, hs=hs, gb_=C("c_g1")):
                            E.tensor_scalar_mul(ln1s[:, 16 + hs:24 + hs],
                                                suma[:, hs:hs + 8], 1.0 / 1024.0).then_inc(sems["c_g1"], 1)
                            E.wait_ge(sems["c_g1"], gb_ + 1)
                            E.tensor_scalar_mul(ln1s[:, 32 + hs:40 + hs],
                                                ln1s[:, 16 + hs:24 + hs], -1.0)
                            E.tensor_mul(ln1s[:, hs:hs + 8], ln1s[:, 16 + hs:24 + hs],
                                         ln1s[:, 16 + hs:24 + hs]).then_inc(sems["c_g1"], 1)
                            E.wait_ge(sems["c_g1"], gb_ + 2)
                            return E.scalar_tensor_tensor(ln1s[:, hs:hs + 8], sqa[:, hs:hs + 8],
                                                          1.0 / 1024.0, ln1s[:, hs:hs + 8],
                                                          ALU.mult, ALU.subtract)
                        on("vector", mk_veps, waits=[("c_bn", C("c_bn"))], incs=[("c_ln", 1)])
                        cnt["c_g1"] += 2
                        on("scalar", lambda E, hs=hs:
                            E.activation(ln1s[:, 16 + hs:24 + hs], ln1s[:, hs:hs + 8], AF.Sqrt,
                                         bias=epsb[:], scale=1.0),
                           waits=[("c_ln", C("c_ln"))], incs=[("c_ln", 1)])
                        on("vector", lambda E, hs=hs:
                            E.reciprocal(ln1s[:, hs:hs + 8], ln1s[:, 16 + hs:24 + hs]),
                           waits=[("c_ln", C("c_ln"))], incs=[("c_ln", 1)])

                    if ti >= 8 and ti - 8 < 4:
                        t0 = ti - 8
                        p0 = t0 % 2
                        def mk_norm1e(E, t0=t0, p0=p0):
                            xt = x1t(t0)
                            E.tensor_scalar(x1b(p0), xt[:], ln1s[:, 32 + t0:33 + t0],
                                            ln1s[:, t0:t0 + 1], ALU.add, ALU.mult)
                            return E.tensor_scalar(xt[:], xt[:], ln1s[:, 32 + t0:33 + t0],
                                                   ln1s[:, t0:t0 + 1], ALU.add, ALU.mult)
                        nw = [("c_ln", ln_base + 3)]
                        if t0 >= 2:
                            nw.append(("c_tp", C("c_tp")))
                        on("vector", mk_norm1e, waits=nw, incs=[("c_xb", 1)])
                        tb = 4 + p0
                        tw = [("c_xb", C("c_xb")), ("c_nm", nm_final),
                              ("c_vc", C("c_vc")), ("c_g2", C("c_g2"))]
                        if t0 >= 2:
                            tw.append(("c_te", C("c_te")))
                        def mk_tp(E, p0=p0, tb=tb):
                            inst = None
                            for dc in range(8):
                                inst = E.transpose(psb[tb][:, 128 * dc:128 * (dc + 1)],
                                                   x1b(p0)[:, 128 * dc:128 * (dc + 1)], idb[:])
                            return inst
                        on("tensor", mk_tp, waits=tw, incs=[("c_tp", 1)])
                        ew = [("c_tp", C("c_tp")), ("c_pv", C("c_pv")),
                              ("c_gu", C("c_gu"))]
                        def mk_te(E, t0=t0, tb=tb):
                            src = psb[tb].rearrange("p (dc n) -> p dc n", dc=8)
                            dst = x1T.rearrange("p (dc n) -> p dc n", dc=8)[:, :, 128 * t0:128 * (t0 + 1)]
                            return E.activation(dst, src, AF.Identity)
                        on("scalar", mk_te, waits=ew, incs=[("c_te", 1)])
                y_final = C("c_y")
                o_final = C("c_o")
                ln1_done = C("c_ln")
                tp_q0 = C("c_tp")

                for hc in range(24):
                    on("gpsimd", lambda E, hc=hc:
                        E.dma_start(out=wdTb(hc), in_=wd_b[128 * hc:128 * (hc + 1), :]),
                        waits=[rp_wait["wd"], ("c_o", o_final), ("c_y", y_final),
                               ("c_s", sfin)],
                        incs=[("d_wd", 16)])
                dwd = C("d_wd")

                si_parity = {}
                hm_parity = {}
                te_parity = {}
                y2_parity = {}
                y2_at.clear()
                def emit_quarter_transposes(tq):
                    if True:
                        for tr in range(4):
                            ti = 4 * tq + tr
                            par = tr % 2
                            def mk_norm1(E, ti=ti, par=par):
                                xt = x1t(ti)
                                E.tensor_scalar(x1b(par), xt[:], ln1s[:, 32 + ti:33 + ti],
                                                ln1s[:, ti:ti + 1], ALU.add, ALU.mult)
                                return E.tensor_scalar(xt[:], xt[:], ln1s[:, 32 + ti:33 + ti],
                                                       ln1s[:, ti:ti + 1], ALU.add, ALU.mult)
                            nw = [("c_ln", ln1_done), ("c_tp", C("c_tp"))]
                            on("vector", mk_norm1, waits=nw, incs=[("c_xb", 1)])
                            tb = par
                            tw = [("c_xb", C("c_xb")), ("c_si", C("c_si")),
                                  ("c_hm", C("c_hm"))]
                            pv_ = te_parity.get(tb)
                            if pv_ is not None:
                                tw.append(("c_te", pv_))
                            else:
                                tw.append(("c_y", y_final))
                            def mk_tp(E, par=par, tb=tb):
                                inst = None
                                for dc in range(8):
                                    inst = E.transpose(psb[tb][:, 128 * dc:128 * (dc + 1)],
                                                       x1b(par)[:, 128 * dc:128 * (dc + 1)], idb[:])
                                return inst
                            on("tensor", mk_tp, waits=tw, incs=[("c_tp", 1)])
                            ew = [("c_tp", C("c_tp")), ("c_gu", C("c_gu"))]
                            def mk_te(E, tr=tr, tb=tb):
                                src = psb[tb].rearrange("p (dc n) -> p dc n", dc=8)
                                dst = x1T.rearrange("p (dc n) -> p dc n", dc=8)[:, :, 128 * tr:128 * (tr + 1)]
                                return E.activation(dst, src, AF.Identity)
                            on("scalar", mk_te, waits=ew, incs=[("c_te", 1)])
                            te_parity[tb] = C("c_te")

                for tq in range(NQQ):
                    te_quarter = C("c_te")

                    for hc in range(24):
                        par = hc % 2
                        gb, ub = 0 + par, 2 + par
                        gs_sem, us_sem = f"d_g{par}", f"d_u{par}"
                        on("sync", lambda E, hc=hc, par=par:
                            E.dma_start(out=wgub(0, par),
                                        in_=wg_b.rearrange("(dc p) f -> p dc f", p=128)[:, :, 128 * hc:128 * (hc + 1)]),
                            waits=[("c_gu", g_cnt[par]), ("c_o", o_final),
                                   rp_wait["wg"]] if par else
                                  [("c_gu", g_cnt[par]), ("c_x", x_final),
                                   ("c_pv", C("c_pv")), rp_wait["wg"]],
                            incs=[(gs_sem, 16)])
                        on("sync", lambda E, hc=hc, par=par:
                            E.dma_start(out=wgub(1, par),
                                        in_=wu_b.rearrange("(dc p) f -> p dc f", p=128)[:, :, 128 * hc:128 * (hc + 1)]),
                            waits=[("c_gu", u_cnt[par]), ("c_o", o_final),
                                   rp_wait["wu"]] if par else
                                  [("c_gu", u_cnt[par]), ("c_x", x_final),
                                   ("c_pv", C("c_pv")), rp_wait["wu"]],
                            incs=[(us_sem, 16)])
                        dg, du = C(gs_sem), C(us_sem)
                        muw = [(gs_sem, dg), (us_sem, du), ("c_te", te_quarter)]
                        pv_ = si_parity.get(par)
                        pu_ = hm_parity.get(par)
                        if pv_ is not None:
                            muw.append(("c_si", pv_))
                        if pu_ is not None:
                            muw.append(("c_hm", pu_))
                        if pv_ is None:
                            muw += [("c_tp", C("c_tp")), ("c_y", y_final)]
                        def mk_gu(E, gb=gb, ub=ub, par=par):
                            inst = None
                            for dc in range(8):
                                st_, sp = dc == 0, dc == 7
                                E.matmul(ps[gb][:], wgub(0, par)[:, 128 * dc:128 * (dc + 1)],
                                         x1T[:, 512 * dc:512 * (dc + 1)], start=st_, stop=sp)
                                inst = E.matmul(ps[ub][:], wgub(1, par)[:, 128 * dc:128 * (dc + 1)],
                                                x1T[:, 512 * dc:512 * (dc + 1)], start=st_, stop=sp)
                            return inst
                        on("tensor", mk_gu, waits=muw, incs=[("c_gu", 1)])
                        g_cnt[par] = C("c_gu")
                        u_cnt[par] = C("c_gu")
                        siw = [("c_gu", C("c_gu"))]
                        if hm_parity.get(par) is not None:
                            siw.append(("c_hm", hm_parity[par]))
                        on("scalar", lambda E, gb=gb, par=par:
                            E.activation(gsb[:, 512 * par:512 * (par + 1)], ps[gb][:],
                                         AF.Silu, bias=0.0, scale=DEQ),
                            waits=siw, incs=[("c_si", 1)])
                        si_parity[par] = C("c_si")
                        hw_ = [("c_si", C("c_si")), ("c_dn", C("c_dn"))]
                        if tq == 0 and hc < 2:
                            hw_.append(("c_ve", ve_at[NT - 1]))
                            hw_.append(("c_pv", C("c_pv")))
                        on("vector", lambda E, ub=ub, par=par, hc=hc:
                            E.scalar_tensor_tensor(hT(hc), ps[ub][:], DEQ,
                                                   gsb[:, 512 * par:512 * (par + 1)],
                                                   ALU.mult, ALU.mult),
                            waits=hw_, incs=[("c_hm", 1)])
                        hm_parity[par] = C("c_hm")
                    hm_quarter = C("c_hm")

                    for tr in range(4):
                        ti = 4 * tq + tr
                        par = tr % 2
                        b0, b1 = (4, 5) if par == 0 else (6, 7)
                        dnw = [("c_hm", hm_quarter), ("d_wd", dwd)]
                        pv_ = y2_parity.get(par)
                        if pv_ is not None:
                            dnw.append(("c_y2", pv_))
                        elif par == 0:
                            dnw += [("c_te", te_quarter), ("c_tp", tp_q0),
                                    ("c_vc", C("c_vc")), ("c_g2", C("c_g2")),
                                    ("c_nm", nm_final)]
                        else:
                            dnw += [("c_qke", C("c_qke")), ("c_ve", ve_at[NT - 1]),
                                    ("c_qk", qk_final), ("c_v", v_final),
                                    ("c_vc", C("c_vc"))]
                        def mk_dn(E, tr=tr, b0=b0, b1=b1):
                            inst = None
                            for hc in range(24):
                                st_, sp = hc == 0, hc == 23
                                lhsT = hT(hc)[:, 128 * tr:128 * (tr + 1)]
                                E.matmul(ps[b0][:], lhsT, wdTb(hc)[:, 0:512], start=st_, stop=sp)
                                inst = E.matmul(ps[b1][:], lhsT, wdTb(hc)[:, 512:1024],
                                                start=st_, stop=sp)
                            return inst
                        on("tensor", mk_dn, waits=dnw, incs=[("c_dn", 1)])
                        oc = f"d_out{tr % 2}"
                        y2w = [("c_dn", C("c_dn")), (oc, out_cnt[tr % 2]),
                               ("c_o", o_final)]
                        def mk_y2(E, ti=ti, tr=tr, b0=b0, b1=b1):
                            yq = y2q(tr)
                            xt = x1t(ti)
                            E.scalar_tensor_tensor(yq[:, 0:512], ps[b0][:], DEQ, xt[:, 0:512],
                                                   ALU.mult, ALU.add)
                            return E.scalar_tensor_tensor(yq[:, 512:1024], ps[b1][:], DEQ,
                                                          xt[:, 512:1024], ALU.mult, ALU.add)
                        on("vector", mk_y2, waits=y2w, incs=[("c_y2", 1)])
                        y2_parity[par] = C("c_y2")
                        y2_at[ti] = C("c_y2")
                        def mk_bn2(E, tr=tr):
                            yq = y2q(tr)
                            E.activation(bobres[:], yq[:], AF.Identity,
                                         accum_out=suma2[:, tr:tr + 1])
                            return E.activation(bobres[:], yq[:], AF.Square,
                                                accum_out=sqa2[:, tr:tr + 1])
                        on("scalar", mk_bn2, waits=[("c_y2", C("c_y2"))], incs=[("c_bn2", 2)])

                    if tq + 1 < NQQ:
                        emit_quarter_transposes(tq + 1)

                    def mk_veps2(E, gb_=C("c_g1")):
                        E.tensor_scalar_mul(ln2s[:, 4:8], suma2[:], 1.0 / 1024.0).then_inc(sems["c_g1"], 1)
                        E.wait_ge(sems["c_g1"], gb_ + 1)
                        E.tensor_scalar_mul(ln2s[:, 8:12], ln2s[:, 4:8], -1.0)
                        E.tensor_mul(ln2s[:, 0:4], ln2s[:, 4:8], ln2s[:, 4:8]).then_inc(sems["c_g1"], 1)
                        E.wait_ge(sems["c_g1"], gb_ + 2)
                        return E.scalar_tensor_tensor(ln2s[:, 0:4], sqa2[:], 1.0 / 1024.0,
                                                      ln2s[:, 0:4], ALU.mult, ALU.subtract)
                    on("vector", mk_veps2, waits=[("c_bn2", C("c_bn2"))], incs=[("c_ln2", 1)])
                    cnt["c_g1"] += 2
                    on("scalar", lambda E: E.activation(ln2s[:, 4:8], ln2s[:, 0:4], AF.Sqrt,
                                                        bias=epsb[:], scale=1.0),
                       waits=[("c_ln2", C("c_ln2"))], incs=[("c_ln2", 1)])
                    on("vector", lambda E: E.reciprocal(ln2s[:, 0:4], ln2s[:, 4:8]),
                       waits=[("c_ln2", C("c_ln2"))], incs=[("c_ln2", 1)])
                    for tr in range(4):
                        ti = 4 * tq + tr
                        on("vector", lambda E, tr=tr:
                            E.tensor_scalar(y2q(tr)[:], y2q(tr)[:], ln2s[:, 8 + tr:9 + tr],
                                            ln2s[:, tr:tr + 1], ALU.add, ALU.mult),
                            waits=[("c_ln2", C("c_ln2"))], incs=[("c_n2", 1)])
                        oc = f"d_out{tr % 2}"
                        on("scalar", lambda E, s=s, ti=ti, tr=tr:
                            E.dma_start(out=out_d[s, 128 * ti:128 * (ti + 1), :], in_=y2q(tr)),
                            waits=[("c_n2", C("c_n2"))], incs=[(oc, 16)])
                        out_cnt[tr % 2] = C(oc)
                dn_final = C("c_dn")
                y2_final = C("c_y2")

            on("scalar", lambda E: E.nop(),
               waits=[("d_out0", out_cnt[0]), ("d_out1", out_cnt[1])])

        walk("audit")
        _check_schedule(audit_streams)

        block = e(nc.Block())

        @block.sync
        def _(E):
            walk("sync")

        @block.gpsimd
        def _(E):
            walk("gpsimd")

        @block.tensor
        def _(E):
            walk("tensor")

        @block.vector
        def _(E):
            walk("vector")

        @block.scalar
        def _(E):
            walk("scalar")

    return nc


def _check_schedule(streams):
    if not streams:
        return
    pos = {k: 0 for k in streams}
    cnt = {}
    progress = True
    while progress:
        progress = False
        for eng, ops in streams.items():
            while pos[eng] < len(ops):
                waits, incs = ops[pos[eng]]
                if all(cnt.get(s_, 0) >= v for s_, v in waits):
                    for s_, n in incs:
                        cnt[s_] = cnt.get(s_, 0) + n
                    pos[eng] += 1
                    progress = True
                else:
                    break
    stuck = {k: (pos[k], len(v)) for k, v in streams.items() if pos[k] < len(v)}
    if stuck:
        detail = {}
        for eng in stuck:
            waits, _ = streams[eng][pos[eng]]
            detail[eng] = [(s_, v, cnt.get(s_, 0)) for s_, v in waits
                           if cnt.get(s_, 0) < v]
        raise RuntimeError(f"schedule deadlock: {stuck}; blocking: {detail}")


_PROG_CACHE = {}


def _get_program():
    if "nc" not in _PROG_CACHE:
        _PROG_CACHE["nc"] = build_program()
    return _PROG_CACHE["nc"]


def kernel(L0_attn_x, L0_attn_W_qkv, L0_attn_b_qkv, L0_attn_W_out, L0_attn_b_out,
           L0_ffn_w_gate, L0_ffn_w_up, L0_ffn_w_down):
    x = np.asarray(L0_attn_x, dtype=np.float32)
    wqkv = np.ascontiguousarray(np.asarray(L0_attn_W_qkv, dtype=np.float32))
    bqkv = np.asarray(L0_attn_b_qkv, dtype=np.float32)
    wout = np.ascontiguousarray(np.asarray(L0_attn_W_out, dtype=np.float32))
    bout = np.asarray(L0_attn_b_out, dtype=np.float32)
    wg = np.asarray(L0_ffn_w_gate, dtype=np.int32)
    wu = np.asarray(L0_ffn_w_up, dtype=np.int32)
    wd = np.asarray(L0_ffn_w_down, dtype=np.int32)

    wgT = np.ascontiguousarray(wg.T)
    wuT = np.ascontiguousarray(wu.T)
    wdT = np.ascontiguousarray(wd.T)
    bqkT = np.ascontiguousarray(bqkv[0:2048].reshape(16, 128).T)
    bvb = np.ascontiguousarray(np.broadcast_to(bqkv[2048:3072], (128, 1024)))
    bob = np.ascontiguousarray(np.broadcast_to(bout, (128, 1024)))
    maskT = np.where(np.arange(128)[:, None] > np.arange(128)[None, :],
                     np.float32(-80000.0), np.float32(0.0)).astype(np.float32)
    ident = np.eye(128, dtype=ml_dtypes.bfloat16)

    nc = _get_program()
    in_maps = []
    for c in range(NCORE):
        xs = np.ascontiguousarray(x[NSEQ * c:NSEQ * (c + 1)])
        xTs = np.ascontiguousarray(xs.transpose(0, 2, 1))
        in_maps.append({
            "x": xs, "xT": xTs, "wqkv": wqkv, "wout": wout,
            "wgT": wgT, "wuT": wuT, "wdT": wdT,
            "bqkT": bqkT, "bvb": bvb, "bob": bob,
            "maskT": maskT, "ident": ident,
        })
    res = run_bass_kernel_spmd(nc, in_maps, core_ids=list(range(NCORE)))
    return np.concatenate([r["out"] for r in res.results], axis=0)


# version 48
# speedup vs baseline: 1.0583x; 1.0311x over previous
import numpy as np
import ml_dtypes
import concourse.bass as bass
import concourse.mybir as mybir
from concourse.bass_utils import run_bass_kernel_spmd
from contextlib import ExitStack

F32 = mybir.dt.float32
BF16 = mybir.dt.bfloat16
I32 = mybir.dt.int32
AF = mybir.ActivationFunctionType
ALU = mybir.AluOpType

NCORE = 8
NSEQ = 4
S = 2048
D = 1024
H = 16
FQ = 3072
FF = 3072
NPAIR = 8
QW = 512
NQQ = 4
NT = 16
EPS = 1e-5
DEQ = 1.0 / 127.0


def build_program():
    nc = bass.Bass("TRN2", target_bir_lowering=False)

    x_d = nc.dram_tensor("x", [NSEQ, S, D], F32, kind="ExternalInput").ap()
    xT_d = nc.dram_tensor("xT", [NSEQ, D, S], F32, kind="ExternalInput").ap()
    wqkv_d = nc.dram_tensor("wqkv", [D, FQ], F32, kind="ExternalInput").ap()
    wout_d = nc.dram_tensor("wout", [D, D], F32, kind="ExternalInput").ap()
    wgT_d = nc.dram_tensor("wgT", [D, FF], I32, kind="ExternalInput").ap()
    wuT_d = nc.dram_tensor("wuT", [D, FF], I32, kind="ExternalInput").ap()
    wdT_d = nc.dram_tensor("wdT", [FF, D], I32, kind="ExternalInput").ap()
    bqkT_d = nc.dram_tensor("bqkT", [128, 16], F32, kind="ExternalInput").ap()
    bvb_d = nc.dram_tensor("bvb", [128, D], F32, kind="ExternalInput").ap()
    bob_d = nc.dram_tensor("bob", [128, D], F32, kind="ExternalInput").ap()
    maskT_d = nc.dram_tensor("maskT", [128, 128], F32, kind="ExternalInput").ap()
    ident_d = nc.dram_tensor("ident", [128, 128], BF16, kind="ExternalInput").ap()
    out_d = nc.dram_tensor("out", [NSEQ, S, D], F32, kind="ExternalOutput").ap()

    wqkv_b = nc.dram_tensor("wqkv_b", [D, FQ], BF16, kind="Internal").ap()
    wout_b = nc.dram_tensor("wout_b", [D, D], BF16, kind="Internal").ap()
    wg_b = nc.dram_tensor("wg_b", [D, FF], BF16, kind="Internal").ap()
    wu_b = nc.dram_tensor("wu_b", [D, FF], BF16, kind="Internal").ap()
    wd_b = nc.dram_tensor("wd_b", [FF, D], BF16, kind="Internal").ap()

    with ExitStack() as es:
        e = es.enter_context

        T1 = e(nc.sbuf_tensor("T1", [128, 32768], BF16))
        T2 = e(nc.sbuf_tensor("T2", [128, 16384], BF16))
        T3 = e(nc.sbuf_tensor("T3", [128, 16384], BF16))
        T4 = e(nc.sbuf_tensor("T4", [128, 24576], BF16))
        pT = e(nc.sbuf_tensor("pT", [128, 2048], BF16))
        gsb = e(nc.sbuf_tensor("gsb", [128, 1024], BF16))
        x1bb = e(nc.sbuf_tensor("x1bb", [128, 2048], BF16))

        bqkT = e(nc.sbuf_tensor("bqkT_s", [128, 16], F32))
        bvb = e(nc.sbuf_tensor("bvb_s", [128, D], F32))
        bob = e(nc.sbuf_tensor("bob_s", [128, D], F32))
        mskt = e(nc.sbuf_tensor("mskt", [128, 128], F32))
        idb = e(nc.sbuf_tensor("idb", [128, 128], BF16))
        negc = e(nc.sbuf_tensor("negc", [128, 1], F32))
        ones64 = e(nc.sbuf_tensor("ones64", [128, 64], BF16))
        ones128b = e(nc.sbuf_tensor("ones128b", [128, 1], BF16))
        rrb = e(nc.sbuf_tensor("rrb", [128, 512], F32))
        suma = e(nc.sbuf_tensor("suma", [128, 16], F32))
        sqa = e(nc.sbuf_tensor("sqa", [128, 16], F32))
        suma2 = e(nc.sbuf_tensor("suma2", [128, 4], F32))
        sqa2 = e(nc.sbuf_tensor("sqa2", [128, 4], F32))
        epsb = e(nc.sbuf_tensor("epsb", [128, 1], F32))
        bobres = e(nc.sbuf_tensor("bobres", [128, 1024], F32))
        ln1s = e(nc.sbuf_tensor("ln1s", [128, 48], F32))
        ln2s = e(nc.sbuf_tensor("ln2s", [128, 12], F32))


        T1f = T1.ap().bitcast(F32)
        T3f = T3.ap().bitcast(F32)
        T4f = T4.ap().bitcast(F32)

        def xTb(dc):
            return T1.ap()[:, 2048 * dc:2048 * (dc + 1)]

        def qkw(slot):
            return T1.ap()[:, 16384 + 2048 * slot:16384 + 2048 * (slot + 1)]

        def Vw(dc):
            return T1.ap()[:, 20480 + 1024 * dc:20480 + 1024 * (dc + 1)]

        def x1t(ti):
            return T1f[:, 1024 * ti:1024 * (ti + 1)]

        V2 = T2.ap()[:, 0:16384]

        def hT(hc):
            return T2.ap()[:, 512 * hc:512 * (hc + 1)]

        x1T = T2.ap()[:, 12288:16384]

        valsT = T3.ap()[:, 0:16384]

        def wgub(mat, par):
            if par == 0:
                return pT.ap()[:, 1024 * mat:1024 * (mat + 1)]
            off = 2048 * mat + 1024
            return T3.ap()[:, off:off + 1024]

        def y2q(tr):
            return T3f[:, 3072 + 1024 * tr:3072 + 1024 * (tr + 1)]

        def qkTs(chunk):
            return T4.ap()[:, 2048 * chunk:2048 * (chunk + 1)]

        def woutb(j):
            base = 8192 + 1024 * j if j < 4 else 1024 * (j - 4)
            return T4.ap()[:, base:base + 1024]

        def residt(slot):
            return T4f[:, 6144 + 1024 * slot:6144 + 1024 * (slot + 1)]

        def wdTb(hc):
            return T4.ap()[:, 1024 * hc:1024 * (hc + 1)]

        def x1b(par):
            return x1bb.ap()[:, 1024 * par:1024 * (par + 1)]

        ps = [e(nc.psum_tensor(f"ps{i}", [128, 512], F32)) for i in range(8)]
        psb = [p.ap().bitcast(BF16) for p in ps]

        sem_names = [
            "d_xT", "d_wq", "d_vw", "d_rs0", "d_rs1", "d_wo", "d_wd",
            "d_g0", "d_g1", "d_u0", "d_u1", "d_out0", "d_out1",
            "d_ri0", "d_ri1", "d_ro0", "d_ro1", "d_cst",
            "c_qk", "c_qke", "c_v", "c_ve", "c_s", "c_m", "c_x",
            "c_pv", "c_pq", "c_rr", "c_rb", "c_nm", "c_o", "c_y", "c_bn",
            "c_ln", "c_xb", "c_tp", "c_te", "c_gu", "c_si", "c_hm",
            "c_dn", "c_y2", "c_bn2", "c_ln2", "c_n2", "c_g1", "c_g2",
            "c_lg", "c_vc",
        ]
        sems = {n: e(nc.semaphore(n)) for n in sem_names}

        audit_streams = {}

        def walk(which):
            cnt = {n: 0 for n in sem_names}
            waited = {n: -1 for n in sem_names}

            def on(engine, fn, waits=(), incs=()):
                wl = [(s_, v) for (s_, v) in waits if v > 0]
                if which == "audit":
                    audit_streams.setdefault(engine, []).append((list(wl), list(incs)))
                inst = None
                if which == engine:
                    E = getattr(nc, engine)
                    for s_, v in wl:
                        if waited[s_] < v:
                            E.wait_ge(sems[s_], v)
                            waited[s_] = v
                    inst = fn(E)
                for s_, n in incs:
                    cnt[s_] += n
                    if inst is not None:
                        inst.then_inc(sems[s_], n)
                return inst

            def C(s_):
                return cnt[s_]

            on("sync", lambda E: E.dma_start(out=bqkT[:], in_=bqkT_d), incs=[("d_cst", 16)])
            on("sync", lambda E: E.dma_start(out=bvb[:], in_=bvb_d), incs=[("d_cst", 16)])
            on("sync", lambda E: E.dma_start(out=bob[:], in_=bob_d), incs=[("d_cst", 16)])
            on("sync", lambda E: E.dma_start(out=mskt[:], in_=maskT_d), incs=[("d_cst", 16)])
            on("sync", lambda E: E.dma_start(out=idb[:], in_=ident_d), incs=[("d_cst", 16)])

            def mk_init(E):
                E.memset(negc[:], -4.0)
                E.memset(epsb[:], EPS)
                E.memset(ones128b[:], 1.0)
                return E.memset(ones64[:], 1.0)
            on("vector", mk_init, incs=[("c_m", 1)])


            def qkw_src_f32(pj, c):
                return wqkv_d.rearrange("(dc p) f -> p dc f", p=128)[
                    :, :, 1024 * c + 128 * pj:1024 * c + 128 * (pj + 1)]

            def qkw_src_b(pj, c):
                return wqkv_b.rearrange("(dc p) f -> p dc f", p=128)[
                    :, :, 1024 * c + 128 * pj:1024 * c + 128 * (pj + 1)]

            def qkw_dst(slot, c):
                return qkw(slot).rearrange("p (dc c f) -> p dc c f", dc=8, c=2)[:, :, c, :]

            for c in range(2):
                on("gpsimd", lambda E, c=c:
                    E.dma_start(out=qkw_dst(0, c), in_=qkw_src_f32(0, c)),
                    incs=[("d_wq", 16)])
            for dc in range(8):
                on("gpsimd", lambda E, dc=dc:
                    E.dma_start(out=xTb(dc), in_=xT_d[0, 128 * dc:128 * (dc + 1), :]),
                    incs=[("d_xT", 16)])
            for c in range(2):
                on("gpsimd", lambda E, c=c:
                    E.dma_start(out=qkw_dst(1, c), in_=qkw_src_f32(1, c)),
                    incs=[("d_wq", 16)])
            on("gpsimd", lambda E:
                E.dma_start(out=T1.ap()[:, 20480:28672].rearrange("p (dc f) -> p dc f", dc=8),
                            in_=wqkv_d.rearrange("(dc p) f -> p dc f", p=128)[:, :, 2048:3072]),
                incs=[("d_vw", 16)])
            s0_dwq_at = {0: 32, 1: 64}
            def qk_done_cnt(p):
                return 4 * (p + 1) if p <= 1 else 8 + 8 * (p - 1)
            for nxt in range(2, NPAIR):
                for c in range(2):
                    on("gpsimd", lambda E, nxt=nxt, c=c:
                        E.dma_start(out=qkw_dst(nxt % 2, c), in_=qkw_src_f32(nxt, c)),
                        waits=[("c_qk", qk_done_cnt(nxt - 2))], incs=[("d_wq", 16)])
                s0_dwq_at[nxt] = C("d_wq")

            rp_wait = {}
            rp_list = [("wout", wout_d, wout_b), ("wg", wgT_d, wg_b),
                       ("wu", wuT_d, wu_b), ("wd", wdT_d, wd_b),
                       ("wqkv", wqkv_d, wqkv_b)]
            for i, (key, src, dst) in enumerate(rp_list):
                on("gpsimd", lambda E, src=src, dst=dst:
                    E.dma_start(out=dst, in_=src), incs=[("d_ro0", 16)])
                rp_wait[key] = ("d_ro0", C("d_ro0"))

            dn_final = 0
            y2_final = 0
            y_final = 0
            o_final = 0
            x_final = 0
            nm_final = 0
            out_cnt = [0, 0]
            rs_cnt = [0, 0]
            g_cnt = [0, 0]
            u_cnt = [0, 0]
            gstep = 0
            cx_at = {}
            y2_at = {}

            last_vc = {4: 0, 6: 0}
            for s in range(NSEQ):
                if s > 0:
                    for dc in range(8):
                        on("gpsimd", lambda E, s=s, dc=dc:
                            E.dma_start(out=xTb(dc), in_=xT_d[s, 128 * dc:128 * (dc + 1), :]),
                            waits=[("c_y2", y2_at.get(7, y2_final))], incs=[("d_xT", 16)])
                    on("sync", lambda E:
                        E.dma_start(out=T1.ap()[:, 20480:28672].rearrange("p (dc f) -> p dc f", dc=8),
                                    in_=wqkv_b.rearrange("(dc p) f -> p dc f", p=128)[:, :, 2048:3072]),
                        waits=[rp_wait["wqkv"], ("c_y2", y2_at.get(13, y2_final))], incs=[("d_vw", 16)])
                    for pj in range(2):
                        for c in range(2):
                            on("sync", lambda E, pj=pj, c=c:
                                E.dma_start(out=qkw_dst(pj, c), in_=qkw_src_b(pj, c)),
                                waits=[rp_wait["wqkv"], ("c_y2", y2_at.get(9, y2_final))],
                                incs=[("d_wq", 16)])
                dxT = C("d_xT")
                dvw = C("d_vw")
                if s == 0:
                    dwq_at = dict(s0_dwq_at)
                else:
                    dwq_at = {0: C("d_wq") - 32, 1: C("d_wq")}

                qke_at = {}
                sc_pair = {}
                qk_state = {"pj": 0, "u": 0, "half": 0, "upf": {}}
                dn_final_seq = dn_final

                def emit_qk_unit(upfront=False):
                    pj = qk_state["pj"]
                    if pj >= NPAIR:
                        return
                    u = qk_state["u"]
                    wslot = pj % 2
                    cslot = pj % 3
                    for c, th in [divmod(u, 2)]:
                        if upfront:
                            qb0, qb1 = (0, 1) if u % 2 == 0 else (2, 3)
                        else:
                            qb0, qb1 = 6, 7
                        if True:
                            waits = [("d_wq", dwq_at[pj]), ("d_xT", dxT)]
                            if upfront:
                                waits += [("c_si", C("c_si")), ("c_hm", C("c_hm")),
                                          ("c_tp", C("c_tp")), ("c_te", C("c_te")),
                                          ("c_y", C("c_y")), ("c_rb", C("c_rb")),
                                          ("c_x", C("c_x")),
                                          ("c_qke", qk_state["upf"].get(u % 2, 0))]
                            else:
                                waits += [("c_qke", C("c_qke")), ("c_ve", C("c_ve")),
                                          ("c_y2", y2_final)]
                            def mk_qk(E, wslot=wslot, c=c, th=th, qb0=qb0, qb1=qb1):
                                inst = None
                                for dc in range(8):
                                    st_, sp = dc == 0, dc == 7
                                    lhsT = qkw(wslot)[:, 256 * dc + 128 * c:256 * dc + 128 * (c + 1)]
                                    E.matmul(ps[qb0][:], lhsT,
                                             xTb(dc)[:, 1024 * th:1024 * th + 512],
                                             start=st_, stop=sp)
                                    inst = E.matmul(ps[qb1][:], lhsT,
                                                    xTb(dc)[:, 1024 * th + 512:1024 * (th + 1)],
                                                    start=st_, stop=sp)
                                return inst
                            on("tensor", mk_qk, waits=waits, incs=[("c_qk", 1)])
                            ew = [("c_qk", C("c_qk")), ("d_cst", 16), ("c_dn", dn_final_seq)]
                            if pj >= 3:
                                ew.append(("c_s", sc_pair[pj - 3]))
                            def mk_qke(E, pj=pj, cslot=cslot, c=c, th=th, qb0=qb0, qb1=qb1):
                                dst = qkTs(2 * cslot + c)
                                bcol = pj + 8 * c
                                E.activation(dst[:, 1024 * th:1024 * th + 512], ps[qb0][:],
                                             AF.Identity, bias=bqkT[:, bcol:bcol + 1], scale=1.0)
                                return E.activation(dst[:, 1024 * th + 512:1024 * (th + 1)], ps[qb1][:],
                                                    AF.Identity, bias=bqkT[:, bcol:bcol + 1], scale=1.0)
                            on("scalar", mk_qke, waits=ew, incs=[("c_qke", 1)])
                            if upfront:
                                qk_state["upf"][u % 2] = C("c_qke")
                    if u < 3:
                        qk_state["u"] = u + 1
                        return
                    qke_at[pj] = C("c_qke")
                    nxt = pj + 2
                    if nxt < NPAIR and s > 0:
                        for c in range(2):
                            on("sync", lambda E, nxt=nxt, wslot=wslot, c=c:
                                E.dma_start(out=qkw_dst(wslot, c), in_=qkw_src_b(nxt, c)),
                                waits=[("c_qk", C("c_qk")), rp_wait["wqkv"]],
                                incs=[("d_wq", 16)])
                        dwq_at[nxt] = C("d_wq")
                    qk_state["pj"] = pj + 1
                    qk_state["u"] = 0

                def emit_qk_half():
                    pj = qk_state["pj"]
                    if pj >= NPAIR:
                        return
                    u = qk_state["u"]
                    half = qk_state["half"]
                    wslot = pj % 2
                    cslot = pj % 3
                    c, th = divmod(u, 2)
                    waits = [("d_wq", dwq_at[pj]), ("d_xT", dxT),
                             ("c_qke", C("c_qke")), ("c_ve", C("c_ve")),
                             ("c_y2", y2_final)]
                    def mk_qkh(E, wslot=wslot, c=c, th=th, half=half):
                        inst = None
                        for dc in range(8):
                            lhsT = qkw(wslot)[:, 256 * dc + 128 * c:256 * dc + 128 * (c + 1)]
                            inst = E.matmul(
                                ps[7][:], lhsT,
                                xTb(dc)[:, 1024 * th + 512 * half:1024 * th + 512 * half + 512],
                                start=dc == 0, stop=dc == 7)
                        return inst
                    on("tensor", mk_qkh, waits=waits, incs=[("c_qk", 1)])
                    ew = [("c_qk", C("c_qk")), ("d_cst", 16), ("c_dn", dn_final_seq)]
                    if pj >= 3:
                        ew.append(("c_s", sc_pair[pj - 3]))
                    def mk_qkeh(E, pj=pj, cslot=cslot, c=c, th=th, half=half):
                        dst = qkTs(2 * cslot + c)
                        bcol = pj + 8 * c
                        return E.activation(
                            dst[:, 1024 * th + 512 * half:1024 * th + 512 * half + 512],
                            ps[7][:], AF.Identity, bias=bqkT[:, bcol:bcol + 1],
                            scale=1.0)
                    on("scalar", mk_qkeh, waits=ew, incs=[("c_qke", 1)])
                    if half == 0:
                        qk_state["half"] = 1
                        return
                    qk_state["half"] = 0
                    if u < 3:
                        qk_state["u"] = u + 1
                        return
                    qke_at[pj] = C("c_qke")
                    nxt = pj + 2
                    if nxt < NPAIR and s > 0:
                        for c2 in range(2):
                            on("sync", lambda E, nxt=nxt, wslot=wslot, c2=c2:
                                E.dma_start(out=qkw_dst(wslot, c2), in_=qkw_src_b(nxt, c2)),
                                waits=[("c_qk", C("c_qk")), rp_wait["wqkv"]],
                                incs=[("d_wq", 16)])
                        dwq_at[nxt] = C("d_wq")
                    qk_state["pj"] = pj + 1
                    qk_state["u"] = 0

                def emit_qk_pair(upfront=False):
                    for _ in range(4):
                        emit_qk_unit(upfront)

                emit_qk_pair(upfront=True)
                emit_qk_pair(upfront=True)

                v_state = {"ti": 0}
                ve_at = {}

                def emit_v(upto):
                    while v_state["ti"] < upto:
                        ti = v_state["ti"]
                        waits = [("d_xT", dxT), ("d_vw", dvw), ("c_qke", C("c_qke"))]
                        if ti > 0:
                            waits.append(("c_ve", ve_at[ti - 1]))
                        else:
                            waits.append(("c_qke", qke_at[1]))
                        def mk_v(E, ti=ti):
                            inst = None
                            for dc in range(8):
                                st_, sp = dc == 0, dc == 7
                                lhsT = xTb(dc)[:, 128 * ti:128 * (ti + 1)]
                                E.matmul(ps[6][:], lhsT, Vw(dc)[:, 0:512], start=st_, stop=sp)
                                inst = E.matmul(ps[7][:], lhsT, Vw(dc)[:, 512:1024], start=st_, stop=sp)
                            return inst
                        on("tensor", mk_v, waits=waits, incs=[("c_v", 1)])
                        def mk_ve(E, ti=ti):
                            E.scalar_tensor_tensor(V2[:, 1024 * ti:1024 * ti + 512], ps[6][:],
                                                   1.0, bvb[:, 0:512], ALU.mult, ALU.add)
                            return E.scalar_tensor_tensor(V2[:, 1024 * ti + 512:1024 * (ti + 1)],
                                                          ps[7][:], 1.0, bvb[:, 512:1024],
                                                          ALU.mult, ALU.add)
                        on("vector", mk_ve,
                           waits=[("c_v", C("c_v")), ("d_cst", 80), ("c_dn", dn_final_seq)],
                           incs=[("c_ve", 1)])
                        ve_at[ti] = C("c_ve")
                        v_state["ti"] = ti + 1

                pending_norm = []
                def vbank(pj_, qq_):
                    if pj_ == 0:
                        return 4
                    return 4 if (4 * pj_ + qq_) % 2 == 0 else 6

                def flush_norm():
                    while pending_norm:
                        pending_norm.pop(0)()

                def norm_count_logical():
                    return C("c_nm") + len(pending_norm)

                for pj in range(NPAIR):
                    cslot = pj % 3
                    qc = 2048 * (2 * cslot)
                    kc = 2048 * (2 * cslot + 1)
                    cs_at = {}
                    cm_at = {}
                    steps = []
                    for qq in range(NQQ):
                        nkj = 4 * qq + 4
                        for kj in range(nkj):
                            steps.append((qq, kj, kj == 0, kj == nkj - 1))
                    ns = len(steps)

                    def scores_step(idx, pj=pj, qc=qc, kc=kc):
                        qq, kj, first, last = steps[idx]
                        gl = gstep + idx
                        par = gl % 2
                        b0, b1 = (0, 1) if par == 0 else (2, 3)
                        diag = kj >= 4 * qq
                        joff = 128 * (kj - 4 * qq) if diag else 0
                        n = QW - joff
                        waits = [("c_qke", qke_at[pj]), ("c_m", 1)]
                        if gl >= 2:
                            waits.append(("c_x", 2 * (gl - 1)))
                        def mk(E, kj=kj, qq=qq, joff=joff, n=n, b0=b0, b1=b1, qc=qc, kc=kc):
                            qcol = qc + QW * qq + joff
                            kcol = kc + 128 * kj
                            E.matmul(ps[b0][:, joff:joff + n],
                                     T4.ap()[0:64, kcol:kcol + 128],
                                     T4.ap()[0:64, qcol:qcol + n], start=True, stop=True)
                            return E.matmul(ps[b1][:, joff:joff + n],
                                            T4.ap()[64:128, kcol:kcol + 128],
                                            T4.ap()[64:128, qcol:qcol + n], start=True, stop=True)
                        on("tensor", mk, waits=waits, incs=[("c_s", 1)])
                        cs_at[gl] = C("c_s")
                        if diag:
                            def mkm(E, b0=b0, b1=b1, joff=joff):
                                E.tensor_tensor(ps[b0][:, joff:joff + 128],
                                                ps[b0][:, joff:joff + 128], mskt[:], ALU.add)
                                return E.tensor_tensor(ps[b1][:, joff:joff + 128],
                                                       ps[b1][:, joff:joff + 128], mskt[:], ALU.add)
                            on("vector", mkm, waits=[("c_s", cs_at[gl])], incs=[("c_m", 1)])
                        cm_at[gl] = C("c_m")
                        return (par, joff, n, diag)

                    meta = {0: scores_step(0)}
                    for idx in range(ns):
                        qq, kj, first, last = steps[idx]
                        gl = gstep + idx
                        if pj == 0 and kj == 0:
                            emit_v(min(NT, 4 * qq + 4))
                        elif pj == 1 and idx == 0:
                            emit_v(NT)
                        if idx + 1 < ns:
                            meta[idx + 1] = scores_step(idx + 1)
                        par, joff, n, diag = meta[idx]
                        pcol = 1024 * par
                        b0, b1 = (0, 1) if par == 0 else (2, 3)
                        xw2 = [("c_m", cm_at[gl])] if diag else [("c_s", cs_at[gl])]
                        def mex0(E, b0=b0, pcol=pcol, joff=joff, n=n):
                            return E.activation(pT[:, pcol + joff:pcol + joff + n],
                                                ps[b0][:, joff:joff + n],
                                                AF.Exp, bias=negc[:], scale=0.125)
                        def mex1(E, b1=b1, pcol=pcol, joff=joff, n=n):
                            return E.activation(pT[:, pcol + 512 + joff:pcol + 512 + joff + n],
                                                ps[b1][:, joff:joff + n],
                                                AF.Exp, bias=negc[:], scale=0.125)
                        on("scalar", mex0, waits=xw2, incs=[("c_x", 1)])
                        on("scalar", mex1, waits=xw2, incs=[("c_x", 1)])
                        cx_at[gl] = C("c_x")
                        assert cx_at[gl] == 2 * (gl + 1), (gl, cx_at[gl])
                        if first:
                            flush_norm()
                        if pj >= 1 and kj in (1, 2):
                            emit_qk_half()
                        vb = vbank(pj, qq)
                        pvw = [("c_x", cx_at[gl]), ("c_ve", ve_at[kj])]
                        if first:
                            pvw.append(("c_vc", last_vc[vb]))
                            pvw.append(("c_y2", y2_final))
                            if pj == 1 and vb == 6:
                                pvw.append(("c_ve", ve_at[NT - 1]))
                        pvw0 = [(sn_, v_ if sn_ != "c_x" else v_ - 1)
                                for sn_, v_ in pvw]
                        def mpv0(E, kj=kj, pj=pj, pcol=pcol, joff=joff, n=n,
                                 first=first, last=last, vb=vb):
                            vb0 = V2[:, 1024 * kj + 128 * pj:1024 * kj + 128 * pj + 64]
                            return E.matmul(ps[vb][0:64, joff:joff + n], vb0,
                                            pT[:, pcol + joff:pcol + joff + n],
                                            start=first, stop=last)
                        on("tensor", mpv0, waits=pvw0, incs=[])
                        def mpr0(E, pcol=pcol, joff=joff, n=n, first=first, last=last):
                            return E.matmul(ps[5][0:64, joff:joff + n], ones64[:],
                                            pT[:, pcol + joff:pcol + joff + n],
                                            start=first, stop=last)
                        prw0 = [("c_x", cx_at[gl] - 1)]
                        if first:
                            prw0.append(("c_g2", C("c_g2")))
                        on("tensor", mpr0, waits=prw0, incs=[])
                        def mpv1(E, kj=kj, pj=pj, pcol=pcol, joff=joff, n=n,
                                 first=first, last=last, vb=vb):
                            vb1 = V2[:, 1024 * kj + 128 * pj + 64:1024 * kj + 128 * (pj + 1)]
                            return E.matmul(ps[vb][64:128, joff:joff + n], vb1,
                                            pT[:, pcol + 512 + joff:pcol + 512 + joff + n],
                                            start=first, stop=last, tile_position=(0, 64))
                        on("tensor", mpv1, waits=[("c_x", cx_at[gl])], incs=[])
                        def mpr1(E, pcol=pcol, joff=joff, n=n, first=first, last=last):
                            return E.matmul(ps[5][64:128, joff:joff + n], ones64[:],
                                            pT[:, pcol + 512 + joff:pcol + 512 + joff + n],
                                            start=first, stop=last, tile_position=(0, 64))
                        on("tensor", mpr1, waits=[("c_x", cx_at[gl])],
                           incs=[("c_pv", 1)] if last else [])
                        if last:
                            def mcp(E, pj=pj, qq=qq, vb=vb):
                                return E.tensor_copy(
                                    valsT[:, 2048 * pj + QW * qq:2048 * pj + QW * (qq + 1)],
                                    ps[vb][:, 0:512])
                            cpw = [("c_pv", C("c_pv"))]
                            if pj == 0 and qq == 0 and s > 0:
                                cpw.append(("c_o", o_final))
                                cpw.append(("c_gu", C("c_gu")))
                                cpw.append(("d_out0", out_cnt[0]))
                                cpw.append(("d_out1", out_cnt[1]))
                            on("vector", mcp, waits=cpw, incs=[("c_vc", 1)])
                            last_vc[vb] = C("c_vc")

                            def emit_norm(pj=pj, qq=qq, pv_=C("c_pv"),
                                          nm_=C("c_nm"), vc_=C("c_vc")):
                                on("scalar",
                                   lambda E: E.activation(bobres[:, 0:512],
                                                          ps[5][:, 0:512], AF.Ln),
                                   waits=[("c_pv", pv_), ("c_nm", nm_)],
                                   incs=[("c_g2", 1)])
                                def mexr(E):
                                    E.activation(rrb[:], bobres[:, 0:512],
                                                 AF.Exp, scale=-1.0)
                                    return E.activation(bobres[:, 1016:1024],
                                                        rrb[:, 504:512], AF.Identity)
                                on("scalar", mexr, waits=[("c_g2", C("c_g2"))],
                                   incs=[("c_lg", 1)])
                                def mnm(E, pj=pj, qq=qq):
                                    col = 2048 * pj + QW * qq
                                    return E.tensor_mul(valsT[:, col:col + QW],
                                                        valsT[:, col:col + QW],
                                                        rrb[:])
                                on("vector", mnm,
                                   waits=[("c_lg", C("c_lg")), ("c_vc", vc_)],
                                   incs=[("c_nm", 1)])
                            pending_norm.append(emit_norm)
                    gstep += ns
                    sc_pair[pj] = C("c_s")
                flush_norm()
                nm_final = C("c_nm")
                x_final = C("c_x")
                qk_final = C("c_qk")
                v_final = C("c_v")
                sfin = C("c_s")

                wout_src = wout_b.rearrange("(c p) f -> p c f", p=128)
                on("sync", lambda E: E.dma_start(
                        out=T4.ap()[:, 8192:12288].rearrange("p (c f) -> p c f", c=4),
                        in_=wout_src[:, 0:4, :]),
                   waits=[rp_wait["wout"], ("c_s", sc_pair[5])], incs=[("d_wo", 16)])
                on("sync", lambda E: E.dma_start(
                        out=T4.ap()[:, 0:4096].rearrange("p (c f) -> p c f", c=4),
                        in_=wout_src[:, 4:8, :]),
                   waits=[("c_s", sc_pair[6])], incs=[("d_wo", 16)])
                dwo = C("d_wo")
                o_parity_free = {}
                ln_base = C("c_ln")
                for ti in range(NT):
                    par = ti % 2
                    b0, b1 = (0, 1) if par == 0 else (2, 3)
                    rs = f"d_rs{par}"
                    on("gpsimd", lambda E, s=s, ti=ti, par=par:
                        E.dma_start(out=residt(par), in_=x_d[s, 128 * ti:128 * (ti + 1), :]),
                        waits=[("c_y", rs_cnt[par]), ("c_dn", dn_final)],
                        incs=[(rs, 16)])
                    drs = C(rs)
                    waits = [("c_nm", nm_final), ("d_wo", dwo)]
                    pv_ = o_parity_free.get(par)
                    if pv_ is not None:
                        waits.append(("c_y", pv_))
                    else:
                        waits.append(("c_x", x_final))
                        waits.append(("c_rb", C("c_rb")))
                    def mk_op(E, ti=ti, b0=b0, b1=b1):
                        inst = None
                        for j in range(8):
                            st_, sp = j == 0, j == 7
                            lhsT = valsT[:, 2048 * j + 128 * ti:2048 * j + 128 * (ti + 1)]
                            E.matmul(ps[b0][:], lhsT, woutb(j)[:, 0:512],
                                     start=st_, stop=sp)
                            inst = E.matmul(ps[b1][:], lhsT, woutb(j)[:, 512:1024],
                                            start=st_, stop=sp)
                        return inst
                    on("tensor", mk_op, waits=waits, incs=[("c_o", 1)])
                    def mk_y(E, ti=ti, b0=b0, b1=b1, par=par):
                        xt = x1t(ti)
                        E.tensor_add(bobres[:], bob[:], residt(par)[:])
                        E.tensor_add(xt[:, 0:512], ps[b0][:], bobres[:, 0:512])
                        return E.tensor_add(xt[:, 512:1024], ps[b1][:], bobres[:, 512:1024])
                    yw = [("c_o", C("c_o")), (rs, drs), ("d_cst", 80),
                          ("c_qk", qk_final), ("c_v", v_final)]
                    on("vector", mk_y, waits=yw, incs=[("c_y", 1)])
                    rs_cnt[par] = C("c_y")
                    o_parity_free[par] = C("c_y")
                    def mk_bn1(E, ti=ti):
                        xt = x1t(ti)
                        E.activation(gsb[:], xt[:], AF.Identity, accum_out=suma[:, ti:ti + 1])
                        return E.activation(gsb[:], xt[:], AF.Square,
                                            accum_out=sqa[:, ti:ti + 1])
                    on("scalar", mk_bn1, waits=[("c_y", C("c_y"))], incs=[("c_bn", 2)])

                    if ti in (7, 15):
                        hs = 0 if ti == 7 else 8
                        def mk_veps(E, hs=hs, gb_=C("c_g1")):
                            E.tensor_scalar_mul(ln1s[:, 16 + hs:24 + hs],
                                                suma[:, hs:hs + 8], 1.0 / 1024.0).then_inc(sems["c_g1"], 1)
                            E.wait_ge(sems["c_g1"], gb_ + 1)
                            E.tensor_scalar_mul(ln1s[:, 32 + hs:40 + hs],
                                                ln1s[:, 16 + hs:24 + hs], -1.0)
                            E.tensor_mul(ln1s[:, hs:hs + 8], ln1s[:, 16 + hs:24 + hs],
                                         ln1s[:, 16 + hs:24 + hs]).then_inc(sems["c_g1"], 1)
                            E.wait_ge(sems["c_g1"], gb_ + 2)
                            return E.scalar_tensor_tensor(ln1s[:, hs:hs + 8], sqa[:, hs:hs + 8],
                                                          1.0 / 1024.0, ln1s[:, hs:hs + 8],
                                                          ALU.mult, ALU.subtract)
                        on("vector", mk_veps, waits=[("c_bn", C("c_bn"))], incs=[("c_ln", 1)])
                        cnt["c_g1"] += 2
                        on("scalar", lambda E, hs=hs:
                            E.activation(ln1s[:, 16 + hs:24 + hs], ln1s[:, hs:hs + 8], AF.Sqrt,
                                         bias=epsb[:], scale=1.0),
                           waits=[("c_ln", C("c_ln"))], incs=[("c_ln", 1)])
                        on("vector", lambda E, hs=hs:
                            E.reciprocal(ln1s[:, hs:hs + 8], ln1s[:, 16 + hs:24 + hs]),
                           waits=[("c_ln", C("c_ln"))], incs=[("c_ln", 1)])

                    if ti >= 8 and ti - 8 < 4:
                        t0 = ti - 8
                        p0 = t0 % 2
                        def mk_norm1e(E, t0=t0, p0=p0):
                            xt = x1t(t0)
                            E.tensor_scalar(x1b(p0), xt[:], ln1s[:, 32 + t0:33 + t0],
                                            ln1s[:, t0:t0 + 1], ALU.add, ALU.mult)
                            return E.tensor_scalar(xt[:], xt[:], ln1s[:, 32 + t0:33 + t0],
                                                   ln1s[:, t0:t0 + 1], ALU.add, ALU.mult)
                        nw = [("c_ln", ln_base + 3)]
                        if t0 >= 2:
                            nw.append(("c_tp", C("c_tp")))
                        on("vector", mk_norm1e, waits=nw, incs=[("c_xb", 1)])
                        tb = 4 + p0
                        tw = [("c_xb", C("c_xb")), ("c_nm", nm_final),
                              ("c_vc", C("c_vc")), ("c_g2", C("c_g2"))]
                        if t0 >= 2:
                            tw.append(("c_te", C("c_te")))
                        def mk_tp(E, p0=p0, tb=tb):
                            inst = None
                            for dc in range(8):
                                inst = E.transpose(psb[tb][:, 128 * dc:128 * (dc + 1)],
                                                   x1b(p0)[:, 128 * dc:128 * (dc + 1)], idb[:])
                            return inst
                        on("tensor", mk_tp, waits=tw, incs=[("c_tp", 1)])
                        ew = [("c_tp", C("c_tp")), ("c_pv", C("c_pv")),
                              ("c_gu", C("c_gu"))]
                        def mk_te(E, t0=t0, tb=tb):
                            src = psb[tb].rearrange("p (dc n) -> p dc n", dc=8)
                            dst = x1T.rearrange("p (dc n) -> p dc n", dc=8)[:, :, 128 * t0:128 * (t0 + 1)]
                            return E.activation(dst, src, AF.Identity)
                        on("scalar", mk_te, waits=ew, incs=[("c_te", 1)])
                y_final = C("c_y")
                o_final = C("c_o")
                ln1_done = C("c_ln")
                tp_q0 = C("c_tp")

                for hc in range(24):
                    on("gpsimd", lambda E, hc=hc:
                        E.dma_start(out=wdTb(hc), in_=wd_b[128 * hc:128 * (hc + 1), :]),
                        waits=[rp_wait["wd"], ("c_o", o_final), ("c_y", y_final),
                               ("c_s", sfin)],
                        incs=[("d_wd", 16)])
                dwd = C("d_wd")

                si_parity = {}
                hm_parity = {}
                te_parity = {}
                y2_parity = {}
                y2_at.clear()
                def emit_quarter_transposes(tq):
                    if True:
                        for tr in range(4):
                            ti = 4 * tq + tr
                            par = tr % 2
                            def mk_norm1(E, ti=ti, par=par):
                                xt = x1t(ti)
                                E.tensor_scalar(x1b(par), xt[:], ln1s[:, 32 + ti:33 + ti],
                                                ln1s[:, ti:ti + 1], ALU.add, ALU.mult)
                                return E.tensor_scalar(xt[:], xt[:], ln1s[:, 32 + ti:33 + ti],
                                                       ln1s[:, ti:ti + 1], ALU.add, ALU.mult)
                            nw = [("c_ln", ln1_done), ("c_tp", C("c_tp"))]
                            on("vector", mk_norm1, waits=nw, incs=[("c_xb", 1)])
                            tb = par
                            tw = [("c_xb", C("c_xb")), ("c_si", C("c_si")),
                                  ("c_hm", C("c_hm"))]
                            pv_ = te_parity.get(tb)
                            if pv_ is not None:
                                tw.append(("c_te", pv_))
                            else:
                                tw.append(("c_y", y_final))
                            def mk_tp(E, par=par, tb=tb):
                                inst = None
                                for dc in range(8):
                                    inst = E.transpose(psb[tb][:, 128 * dc:128 * (dc + 1)],
                                                       x1b(par)[:, 128 * dc:128 * (dc + 1)], idb[:])
                                return inst
                            on("tensor", mk_tp, waits=tw, incs=[("c_tp", 1)])
                            ew = [("c_tp", C("c_tp")), ("c_gu", C("c_gu"))]
                            def mk_te(E, tr=tr, tb=tb):
                                src = psb[tb].rearrange("p (dc n) -> p dc n", dc=8)
                                dst = x1T.rearrange("p (dc n) -> p dc n", dc=8)[:, :, 128 * tr:128 * (tr + 1)]
                                return E.activation(dst, src, AF.Identity)
                            on("scalar", mk_te, waits=ew, incs=[("c_te", 1)])
                            te_parity[tb] = C("c_te")

                for tq in range(NQQ):
                    te_quarter = C("c_te")

                    for hc in range(24):
                        par = hc % 2
                        gb, ub = 0 + par, 2 + par
                        gs_sem, us_sem = f"d_g{par}", f"d_u{par}"
                        on("sync", lambda E, hc=hc, par=par:
                            E.dma_start(out=wgub(0, par),
                                        in_=wg_b.rearrange("(dc p) f -> p dc f", p=128)[:, :, 128 * hc:128 * (hc + 1)]),
                            waits=[("c_gu", g_cnt[par]), ("c_o", o_final),
                                   rp_wait["wg"]] if par else
                                  [("c_gu", g_cnt[par]), ("c_x", x_final),
                                   ("c_pv", C("c_pv")), rp_wait["wg"]],
                            incs=[(gs_sem, 16)])
                        on("sync", lambda E, hc=hc, par=par:
                            E.dma_start(out=wgub(1, par),
                                        in_=wu_b.rearrange("(dc p) f -> p dc f", p=128)[:, :, 128 * hc:128 * (hc + 1)]),
                            waits=[("c_gu", u_cnt[par]), ("c_o", o_final),
                                   rp_wait["wu"]] if par else
                                  [("c_gu", u_cnt[par]), ("c_x", x_final),
                                   ("c_pv", C("c_pv")), rp_wait["wu"]],
                            incs=[(us_sem, 16)])
                        dg, du = C(gs_sem), C(us_sem)
                        muw = [(gs_sem, dg), (us_sem, du), ("c_te", te_quarter)]
                        pv_ = si_parity.get(par)
                        pu_ = hm_parity.get(par)
                        if pv_ is not None:
                            muw.append(("c_si", pv_))
                        if pu_ is not None:
                            muw.append(("c_hm", pu_))
                        if pv_ is None:
                            muw += [("c_tp", C("c_tp")), ("c_y", y_final)]
                        def mk_gu(E, gb=gb, ub=ub, par=par):
                            inst = None
                            for dc in range(8):
                                st_, sp = dc == 0, dc == 7
                                E.matmul(ps[gb][:], wgub(0, par)[:, 128 * dc:128 * (dc + 1)],
                                         x1T[:, 512 * dc:512 * (dc + 1)], start=st_, stop=sp)
                                inst = E.matmul(ps[ub][:], wgub(1, par)[:, 128 * dc:128 * (dc + 1)],
                                                x1T[:, 512 * dc:512 * (dc + 1)], start=st_, stop=sp)
                            return inst
                        on("tensor", mk_gu, waits=muw, incs=[("c_gu", 1)])
                        g_cnt[par] = C("c_gu")
                        u_cnt[par] = C("c_gu")
                        siw = [("c_gu", C("c_gu"))]
                        if hm_parity.get(par) is not None:
                            siw.append(("c_hm", hm_parity[par]))
                        on("scalar", lambda E, gb=gb, par=par:
                            E.activation(gsb[:, 512 * par:512 * (par + 1)], ps[gb][:],
                                         AF.Silu, bias=0.0, scale=DEQ),
                            waits=siw, incs=[("c_si", 1)])
                        si_parity[par] = C("c_si")
                        hw_ = [("c_si", C("c_si")), ("c_dn", C("c_dn"))]
                        if tq == 0 and hc < 2:
                            hw_.append(("c_ve", ve_at[NT - 1]))
                            hw_.append(("c_pv", C("c_pv")))
                        on("vector", lambda E, ub=ub, par=par, hc=hc:
                            E.scalar_tensor_tensor(hT(hc), ps[ub][:], DEQ,
                                                   gsb[:, 512 * par:512 * (par + 1)],
                                                   ALU.mult, ALU.mult),
                            waits=hw_, incs=[("c_hm", 1)])
                        hm_parity[par] = C("c_hm")
                    hm_quarter = C("c_hm")

                    for tr in range(4):
                        ti = 4 * tq + tr
                        par = tr % 2
                        b0, b1 = (4, 5) if par == 0 else (6, 7)
                        dnw = [("c_hm", hm_quarter), ("d_wd", dwd)]
                        pv_ = y2_parity.get(par)
                        if pv_ is not None:
                            dnw.append(("c_y2", pv_))
                        elif par == 0:
                            dnw += [("c_te", te_quarter), ("c_tp", tp_q0),
                                    ("c_vc", C("c_vc")), ("c_g2", C("c_g2")),
                                    ("c_nm", nm_final)]
                        else:
                            dnw += [("c_qke", C("c_qke")), ("c_ve", ve_at[NT - 1]),
                                    ("c_qk", qk_final), ("c_v", v_final),
                                    ("c_vc", C("c_vc"))]
                        def mk_dn(E, tr=tr, b0=b0, b1=b1):
                            inst = None
                            for hc in range(24):
                                st_, sp = hc == 0, hc == 23
                                lhsT = hT(hc)[:, 128 * tr:128 * (tr + 1)]
                                E.matmul(ps[b0][:], lhsT, wdTb(hc)[:, 0:512], start=st_, stop=sp)
                                inst = E.matmul(ps[b1][:], lhsT, wdTb(hc)[:, 512:1024],
                                                start=st_, stop=sp)
                            return inst
                        on("tensor", mk_dn, waits=dnw, incs=[("c_dn", 1)])
                        oc = f"d_out{tr % 2}"
                        y2w = [("c_dn", C("c_dn")), (oc, out_cnt[tr % 2]),
                               ("c_o", o_final)]
                        def mk_y2(E, ti=ti, tr=tr, b0=b0, b1=b1):
                            yq = y2q(tr)
                            xt = x1t(ti)
                            E.scalar_tensor_tensor(yq[:, 0:512], ps[b0][:], DEQ, xt[:, 0:512],
                                                   ALU.mult, ALU.add)
                            return E.scalar_tensor_tensor(yq[:, 512:1024], ps[b1][:], DEQ,
                                                          xt[:, 512:1024], ALU.mult, ALU.add)
                        on("vector", mk_y2, waits=y2w, incs=[("c_y2", 1)])
                        y2_parity[par] = C("c_y2")
                        y2_at[ti] = C("c_y2")
                        def mk_bn2(E, tr=tr):
                            yq = y2q(tr)
                            E.activation(bobres[:], yq[:], AF.Identity,
                                         accum_out=suma2[:, tr:tr + 1])
                            return E.activation(bobres[:], yq[:], AF.Square,
                                                accum_out=sqa2[:, tr:tr + 1])
                        on("scalar", mk_bn2, waits=[("c_y2", C("c_y2"))], incs=[("c_bn2", 2)])

                    if tq + 1 < NQQ:
                        emit_quarter_transposes(tq + 1)

                    def mk_veps2(E, gb_=C("c_g1")):
                        E.tensor_scalar_mul(ln2s[:, 4:8], suma2[:], 1.0 / 1024.0).then_inc(sems["c_g1"], 1)
                        E.wait_ge(sems["c_g1"], gb_ + 1)
                        E.tensor_scalar_mul(ln2s[:, 8:12], ln2s[:, 4:8], -1.0)
                        E.tensor_mul(ln2s[:, 0:4], ln2s[:, 4:8], ln2s[:, 4:8]).then_inc(sems["c_g1"], 1)
                        E.wait_ge(sems["c_g1"], gb_ + 2)
                        return E.scalar_tensor_tensor(ln2s[:, 0:4], sqa2[:], 1.0 / 1024.0,
                                                      ln2s[:, 0:4], ALU.mult, ALU.subtract)
                    on("vector", mk_veps2, waits=[("c_bn2", C("c_bn2"))], incs=[("c_ln2", 1)])
                    cnt["c_g1"] += 2
                    on("scalar", lambda E: E.activation(ln2s[:, 4:8], ln2s[:, 0:4], AF.Sqrt,
                                                        bias=epsb[:], scale=1.0),
                       waits=[("c_ln2", C("c_ln2"))], incs=[("c_ln2", 1)])
                    on("vector", lambda E: E.reciprocal(ln2s[:, 0:4], ln2s[:, 4:8]),
                       waits=[("c_ln2", C("c_ln2"))], incs=[("c_ln2", 1)])
                    for tr in range(4):
                        ti = 4 * tq + tr
                        on("vector", lambda E, tr=tr:
                            E.tensor_scalar(y2q(tr)[:], y2q(tr)[:], ln2s[:, 8 + tr:9 + tr],
                                            ln2s[:, tr:tr + 1], ALU.add, ALU.mult),
                            waits=[("c_ln2", C("c_ln2"))], incs=[("c_n2", 1)])
                        oc = f"d_out{tr % 2}"
                        on("scalar", lambda E, s=s, ti=ti, tr=tr:
                            E.dma_start(out=out_d[s, 128 * ti:128 * (ti + 1), :], in_=y2q(tr)),
                            waits=[("c_n2", C("c_n2"))], incs=[(oc, 16)])
                        out_cnt[tr % 2] = C(oc)
                dn_final = C("c_dn")
                y2_final = C("c_y2")

            on("scalar", lambda E: E.nop(),
               waits=[("d_out0", out_cnt[0]), ("d_out1", out_cnt[1])])

        walk("audit")
        _check_schedule(audit_streams)

        block = e(nc.Block())

        @block.sync
        def _(E):
            walk("sync")

        @block.gpsimd
        def _(E):
            walk("gpsimd")

        @block.tensor
        def _(E):
            walk("tensor")

        @block.vector
        def _(E):
            walk("vector")

        @block.scalar
        def _(E):
            walk("scalar")

    return nc


def _check_schedule(streams):
    if not streams:
        return
    pos = {k: 0 for k in streams}
    cnt = {}
    progress = True
    while progress:
        progress = False
        for eng, ops in streams.items():
            while pos[eng] < len(ops):
                waits, incs = ops[pos[eng]]
                if all(cnt.get(s_, 0) >= v for s_, v in waits):
                    for s_, n in incs:
                        cnt[s_] = cnt.get(s_, 0) + n
                    pos[eng] += 1
                    progress = True
                else:
                    break
    stuck = {k: (pos[k], len(v)) for k, v in streams.items() if pos[k] < len(v)}
    if stuck:
        detail = {}
        for eng in stuck:
            waits, _ = streams[eng][pos[eng]]
            detail[eng] = [(s_, v, cnt.get(s_, 0)) for s_, v in waits
                           if cnt.get(s_, 0) < v]
        raise RuntimeError(f"schedule deadlock: {stuck}; blocking: {detail}")


_PROG_CACHE = {}


def _get_program():
    if "nc" not in _PROG_CACHE:
        _PROG_CACHE["nc"] = build_program()
    return _PROG_CACHE["nc"]


def kernel(L0_attn_x, L0_attn_W_qkv, L0_attn_b_qkv, L0_attn_W_out, L0_attn_b_out,
           L0_ffn_w_gate, L0_ffn_w_up, L0_ffn_w_down):
    x = np.asarray(L0_attn_x, dtype=np.float32)
    wqkv = np.ascontiguousarray(np.asarray(L0_attn_W_qkv, dtype=np.float32))
    bqkv = np.asarray(L0_attn_b_qkv, dtype=np.float32)
    wout = np.ascontiguousarray(np.asarray(L0_attn_W_out, dtype=np.float32))
    bout = np.asarray(L0_attn_b_out, dtype=np.float32)
    wg = np.asarray(L0_ffn_w_gate, dtype=np.int32)
    wu = np.asarray(L0_ffn_w_up, dtype=np.int32)
    wd = np.asarray(L0_ffn_w_down, dtype=np.int32)

    wgT = np.ascontiguousarray(wg.T)
    wuT = np.ascontiguousarray(wu.T)
    wdT = np.ascontiguousarray(wd.T)
    bqkT = np.ascontiguousarray(bqkv[0:2048].reshape(16, 128).T)
    bvb = np.ascontiguousarray(np.broadcast_to(bqkv[2048:3072], (128, 1024)))
    bob = np.ascontiguousarray(np.broadcast_to(bout, (128, 1024)))
    maskT = np.where(np.arange(128)[:, None] > np.arange(128)[None, :],
                     np.float32(-80000.0), np.float32(0.0)).astype(np.float32)
    ident = np.eye(128, dtype=ml_dtypes.bfloat16)

    nc = _get_program()
    in_maps = []
    for c in range(NCORE):
        xs = np.ascontiguousarray(x[NSEQ * c:NSEQ * (c + 1)])
        xTs = np.ascontiguousarray(xs.transpose(0, 2, 1))
        in_maps.append({
            "x": xs, "xT": xTs, "wqkv": wqkv, "wout": wout,
            "wgT": wgT, "wuT": wuT, "wdT": wdT,
            "bqkT": bqkT, "bvb": bvb, "bob": bob,
            "maskT": maskT, "ident": ident,
        })
    res = run_bass_kernel_spmd(nc, in_maps, core_ids=list(range(NCORE)))
    return np.concatenate([r["out"] for r in res.results], axis=0)


# version 49
# speedup vs baseline: 1.0645x; 1.0058x over previous
import numpy as np
import ml_dtypes
import concourse.bass as bass
import concourse.mybir as mybir
from concourse.bass_utils import run_bass_kernel_spmd
from contextlib import ExitStack

F32 = mybir.dt.float32
BF16 = mybir.dt.bfloat16
I32 = mybir.dt.int32
AF = mybir.ActivationFunctionType
ALU = mybir.AluOpType

NCORE = 8
NSEQ = 4
S = 2048
D = 1024
H = 16
FQ = 3072
FF = 3072
NPAIR = 8
QW = 512
NQQ = 4
NT = 16
EPS = 1e-5
DEQ = 1.0 / 127.0


def build_program():
    nc = bass.Bass("TRN2", target_bir_lowering=False)

    x_d = nc.dram_tensor("x", [NSEQ, S, D], F32, kind="ExternalInput").ap()
    xT_d = nc.dram_tensor("xT", [NSEQ, D, S], F32, kind="ExternalInput").ap()
    wqkv_d = nc.dram_tensor("wqkv", [D, FQ], F32, kind="ExternalInput").ap()
    wout_d = nc.dram_tensor("wout", [D, D], F32, kind="ExternalInput").ap()
    wgT_d = nc.dram_tensor("wgT", [D, FF], I32, kind="ExternalInput").ap()
    wuT_d = nc.dram_tensor("wuT", [D, FF], I32, kind="ExternalInput").ap()
    wdT_d = nc.dram_tensor("wdT", [FF, D], I32, kind="ExternalInput").ap()
    bqkT_d = nc.dram_tensor("bqkT", [128, 16], F32, kind="ExternalInput").ap()
    bvb_d = nc.dram_tensor("bvb", [128, D], F32, kind="ExternalInput").ap()
    bob_d = nc.dram_tensor("bob", [128, D], F32, kind="ExternalInput").ap()
    maskT_d = nc.dram_tensor("maskT", [128, 128], F32, kind="ExternalInput").ap()
    ident_d = nc.dram_tensor("ident", [128, 128], BF16, kind="ExternalInput").ap()
    out_d = nc.dram_tensor("out", [NSEQ, S, D], F32, kind="ExternalOutput").ap()

    wqkv_b = nc.dram_tensor("wqkv_b", [D, FQ], BF16, kind="Internal").ap()
    wout_b = nc.dram_tensor("wout_b", [D, D], BF16, kind="Internal").ap()
    wg_b = nc.dram_tensor("wg_b", [D, FF], BF16, kind="Internal").ap()
    wu_b = nc.dram_tensor("wu_b", [D, FF], BF16, kind="Internal").ap()
    wd_b = nc.dram_tensor("wd_b", [FF, D], BF16, kind="Internal").ap()

    with ExitStack() as es:
        e = es.enter_context

        T1 = e(nc.sbuf_tensor("T1", [128, 32768], BF16))
        T2 = e(nc.sbuf_tensor("T2", [128, 16384], BF16))
        T3 = e(nc.sbuf_tensor("T3", [128, 16384], BF16))
        T4 = e(nc.sbuf_tensor("T4", [128, 24576], BF16))
        pT = e(nc.sbuf_tensor("pT", [128, 2048], BF16))
        gsb = e(nc.sbuf_tensor("gsb", [128, 1024], BF16))
        x1bb = e(nc.sbuf_tensor("x1bb", [128, 2048], BF16))

        bqkT = e(nc.sbuf_tensor("bqkT_s", [128, 16], F32))
        bvb = e(nc.sbuf_tensor("bvb_s", [128, D], F32))
        bob = e(nc.sbuf_tensor("bob_s", [128, D], F32))
        mskt = e(nc.sbuf_tensor("mskt", [128, 128], F32))
        idb = e(nc.sbuf_tensor("idb", [128, 128], BF16))
        negc = e(nc.sbuf_tensor("negc", [128, 1], F32))
        ones64 = e(nc.sbuf_tensor("ones64", [128, 64], BF16))
        ones128b = e(nc.sbuf_tensor("ones128b", [128, 1], BF16))
        rrb = e(nc.sbuf_tensor("rrb", [128, 512], F32))
        suma = e(nc.sbuf_tensor("suma", [128, 16], F32))
        sqa = e(nc.sbuf_tensor("sqa", [128, 16], F32))
        suma2 = e(nc.sbuf_tensor("suma2", [128, 4], F32))
        sqa2 = e(nc.sbuf_tensor("sqa2", [128, 4], F32))
        epsb = e(nc.sbuf_tensor("epsb", [128, 1], F32))
        bobres = e(nc.sbuf_tensor("bobres", [128, 1024], F32))
        ln1s = e(nc.sbuf_tensor("ln1s", [128, 48], F32))
        ln2s = e(nc.sbuf_tensor("ln2s", [128, 12], F32))


        T1f = T1.ap().bitcast(F32)
        T3f = T3.ap().bitcast(F32)
        T4f = T4.ap().bitcast(F32)

        def xTb(dc):
            return T1.ap()[:, 2048 * dc:2048 * (dc + 1)]

        def qkw(slot):
            return T1.ap()[:, 16384 + 2048 * slot:16384 + 2048 * (slot + 1)]

        def Vw(dc):
            return T1.ap()[:, 20480 + 1024 * dc:20480 + 1024 * (dc + 1)]

        def x1t(ti):
            return T1f[:, 1024 * ti:1024 * (ti + 1)]

        V2 = T2.ap()[:, 0:16384]

        def hT(hc):
            return T2.ap()[:, 512 * hc:512 * (hc + 1)]

        x1T = T2.ap()[:, 12288:16384]

        valsT = T3.ap()[:, 0:16384]

        def wgub(mat, par):
            if par == 0:
                return pT.ap()[:, 1024 * mat:1024 * (mat + 1)]
            off = 2048 * mat + 1024
            return T3.ap()[:, off:off + 1024]

        def y2q(tr):
            return T3f[:, 3072 + 1024 * tr:3072 + 1024 * (tr + 1)]

        def qkTs(chunk):
            return T4.ap()[:, 2048 * chunk:2048 * (chunk + 1)]

        def woutb(j):
            base = 8192 + 1024 * j if j < 4 else 1024 * (j - 4)
            return T4.ap()[:, base:base + 1024]

        def residt(slot):
            return T4f[:, 6144 + 1024 * slot:6144 + 1024 * (slot + 1)]

        def wdTb(hc):
            return T4.ap()[:, 1024 * hc:1024 * (hc + 1)]

        def x1b(par):
            return x1bb.ap()[:, 1024 * par:1024 * (par + 1)]

        ps = [e(nc.psum_tensor(f"ps{i}", [128, 512], F32)) for i in range(8)]
        psb = [p.ap().bitcast(BF16) for p in ps]

        sem_names = [
            "d_xT", "d_wq", "d_vw", "d_rs0", "d_rs1", "d_wo", "d_wd",
            "d_g0", "d_g1", "d_u0", "d_u1", "d_out0", "d_out1",
            "d_ri0", "d_ri1", "d_ro0", "d_ro1", "d_cst",
            "c_qk", "c_qke", "c_v", "c_ve", "c_s", "c_m", "c_x",
            "c_pv", "c_pq", "c_rr", "c_rb", "c_nm", "c_o", "c_y", "c_bn",
            "c_ln", "c_xb", "c_tp", "c_te", "c_gu", "c_si", "c_hm",
            "c_dn", "c_y2", "c_bn2", "c_ln2", "c_n2", "c_g1", "c_g2",
            "c_lg", "c_vc", "c_br",
        ]
        sems = {n: e(nc.semaphore(n)) for n in sem_names}

        audit_streams = {}

        def walk(which):
            cnt = {n: 0 for n in sem_names}
            waited = {n: -1 for n in sem_names}

            def on(engine, fn, waits=(), incs=()):
                wl = [(s_, v) for (s_, v) in waits if v > 0]
                if which == "audit":
                    audit_streams.setdefault(engine, []).append((list(wl), list(incs)))
                inst = None
                if which == engine:
                    E = getattr(nc, engine)
                    for s_, v in wl:
                        if waited[s_] < v:
                            E.wait_ge(sems[s_], v)
                            waited[s_] = v
                    inst = fn(E)
                for s_, n in incs:
                    cnt[s_] += n
                    if inst is not None:
                        inst.then_inc(sems[s_], n)
                return inst

            def C(s_):
                return cnt[s_]

            on("sync", lambda E: E.dma_start(out=bqkT[:], in_=bqkT_d), incs=[("d_cst", 16)])
            on("sync", lambda E: E.dma_start(out=bvb[:], in_=bvb_d), incs=[("d_cst", 16)])
            on("sync", lambda E: E.dma_start(out=bob[:], in_=bob_d), incs=[("d_cst", 16)])
            on("sync", lambda E: E.dma_start(out=mskt[:], in_=maskT_d), incs=[("d_cst", 16)])
            on("sync", lambda E: E.dma_start(out=idb[:], in_=ident_d), incs=[("d_cst", 16)])

            def mk_init(E):
                E.memset(negc[:], -4.0)
                E.memset(epsb[:], EPS)
                E.memset(ones128b[:], 1.0)
                return E.memset(ones64[:], 1.0)
            on("vector", mk_init, incs=[("c_m", 1)])


            def qkw_src_f32(pj, c):
                return wqkv_d.rearrange("(dc p) f -> p dc f", p=128)[
                    :, :, 1024 * c + 128 * pj:1024 * c + 128 * (pj + 1)]

            def qkw_src_b(pj, c):
                return wqkv_b.rearrange("(dc p) f -> p dc f", p=128)[
                    :, :, 1024 * c + 128 * pj:1024 * c + 128 * (pj + 1)]

            def qkw_dst(slot, c):
                return qkw(slot).rearrange("p (dc c f) -> p dc c f", dc=8, c=2)[:, :, c, :]

            for c in range(2):
                on("gpsimd", lambda E, c=c:
                    E.dma_start(out=qkw_dst(0, c), in_=qkw_src_f32(0, c)),
                    incs=[("d_wq", 16)])
            for dc in range(8):
                on("gpsimd", lambda E, dc=dc:
                    E.dma_start(out=xTb(dc), in_=xT_d[0, 128 * dc:128 * (dc + 1), :]),
                    incs=[("d_xT", 16)])
            for c in range(2):
                on("gpsimd", lambda E, c=c:
                    E.dma_start(out=qkw_dst(1, c), in_=qkw_src_f32(1, c)),
                    incs=[("d_wq", 16)])
            on("gpsimd", lambda E:
                E.dma_start(out=T1.ap()[:, 20480:28672].rearrange("p (dc f) -> p dc f", dc=8),
                            in_=wqkv_d.rearrange("(dc p) f -> p dc f", p=128)[:, :, 2048:3072]),
                incs=[("d_vw", 16)])
            s0_dwq_at = {0: 32, 1: 64}
            def qk_done_cnt(p):
                return 4 * (p + 1) if p <= 1 else 8 + 8 * (p - 1)
            for nxt in range(2, NPAIR):
                for c in range(2):
                    on("gpsimd", lambda E, nxt=nxt, c=c:
                        E.dma_start(out=qkw_dst(nxt % 2, c), in_=qkw_src_f32(nxt, c)),
                        waits=[("c_qk", qk_done_cnt(nxt - 2))], incs=[("d_wq", 16)])
                s0_dwq_at[nxt] = C("d_wq")

            rp_wait = {}
            rp_list = [("wout", wout_d, wout_b), ("wg", wgT_d, wg_b),
                       ("wu", wuT_d, wu_b), ("wd", wdT_d, wd_b),
                       ("wqkv", wqkv_d, wqkv_b)]
            for i, (key, src, dst) in enumerate(rp_list):
                on("gpsimd", lambda E, src=src, dst=dst:
                    E.dma_start(out=dst, in_=src), incs=[("d_ro0", 16)])
                rp_wait[key] = ("d_ro0", C("d_ro0"))

            dn_final = 0
            y2_final = 0
            y_final = 0
            o_final = 0
            x_final = 0
            nm_final = 0
            out_cnt = [0, 0]
            rs_cnt = [0, 0]
            g_cnt = [0, 0]
            u_cnt = [0, 0]
            gstep = 0
            cx_at = {}
            y2_at = {}

            last_vc = {4: 0, 6: 0}
            for s in range(NSEQ):
                if s > 0:
                    for dc in range(8):
                        on("gpsimd", lambda E, s=s, dc=dc:
                            E.dma_start(out=xTb(dc), in_=xT_d[s, 128 * dc:128 * (dc + 1), :]),
                            waits=[("c_y2", y2_at.get(7, y2_final))], incs=[("d_xT", 16)])
                    on("sync", lambda E:
                        E.dma_start(out=T1.ap()[:, 20480:28672].rearrange("p (dc f) -> p dc f", dc=8),
                                    in_=wqkv_b.rearrange("(dc p) f -> p dc f", p=128)[:, :, 2048:3072]),
                        waits=[rp_wait["wqkv"], ("c_y2", y2_at.get(13, y2_final))], incs=[("d_vw", 16)])
                    for pj in range(2):
                        for c in range(2):
                            on("sync", lambda E, pj=pj, c=c:
                                E.dma_start(out=qkw_dst(pj, c), in_=qkw_src_b(pj, c)),
                                waits=[rp_wait["wqkv"], ("c_y2", y2_at.get(9, y2_final))],
                                incs=[("d_wq", 16)])
                dxT = C("d_xT")
                dvw = C("d_vw")
                if s == 0:
                    dwq_at = dict(s0_dwq_at)
                else:
                    dwq_at = {0: C("d_wq") - 32, 1: C("d_wq")}

                qke_at = {}
                sc_pair = {}
                qk_state = {"pj": 0, "u": 0, "half": 0, "upf": {}}
                dn_final_seq = dn_final

                def emit_qk_unit(upfront=False):
                    pj = qk_state["pj"]
                    if pj >= NPAIR:
                        return
                    u = qk_state["u"]
                    wslot = pj % 2
                    cslot = pj % 3
                    for c, th in [divmod(u, 2)]:
                        if upfront:
                            qb0, qb1 = (0, 1) if u % 2 == 0 else (2, 3)
                        else:
                            qb0, qb1 = 6, 7
                        if True:
                            waits = [("d_wq", dwq_at[pj]), ("d_xT", dxT)]
                            if upfront:
                                waits += [("c_si", C("c_si")), ("c_hm", C("c_hm")),
                                          ("c_tp", C("c_tp")), ("c_te", C("c_te")),
                                          ("c_y", C("c_y")), ("c_rb", C("c_rb")),
                                          ("c_x", C("c_x")),
                                          ("c_qke", qk_state["upf"].get(u % 2, 0))]
                            else:
                                waits += [("c_qke", C("c_qke")), ("c_ve", C("c_ve")),
                                          ("c_y2", y2_final)]
                            def mk_qk(E, wslot=wslot, c=c, th=th, qb0=qb0, qb1=qb1):
                                inst = None
                                for dc in range(8):
                                    st_, sp = dc == 0, dc == 7
                                    lhsT = qkw(wslot)[:, 256 * dc + 128 * c:256 * dc + 128 * (c + 1)]
                                    E.matmul(ps[qb0][:], lhsT,
                                             xTb(dc)[:, 1024 * th:1024 * th + 512],
                                             start=st_, stop=sp)
                                    inst = E.matmul(ps[qb1][:], lhsT,
                                                    xTb(dc)[:, 1024 * th + 512:1024 * (th + 1)],
                                                    start=st_, stop=sp)
                                return inst
                            on("tensor", mk_qk, waits=waits, incs=[("c_qk", 1)])
                            ew = [("c_qk", C("c_qk")), ("d_cst", 16), ("c_dn", dn_final_seq)]
                            if pj >= 3:
                                ew.append(("c_s", sc_pair[pj - 3]))
                            def mk_qke(E, pj=pj, cslot=cslot, c=c, th=th, qb0=qb0, qb1=qb1):
                                dst = qkTs(2 * cslot + c)
                                bcol = pj + 8 * c
                                E.activation(dst[:, 1024 * th:1024 * th + 512], ps[qb0][:],
                                             AF.Identity, bias=bqkT[:, bcol:bcol + 1], scale=1.0)
                                return E.activation(dst[:, 1024 * th + 512:1024 * (th + 1)], ps[qb1][:],
                                                    AF.Identity, bias=bqkT[:, bcol:bcol + 1], scale=1.0)
                            on("scalar", mk_qke, waits=ew, incs=[("c_qke", 1)])
                            if upfront:
                                qk_state["upf"][u % 2] = C("c_qke")
                    if u < 3:
                        qk_state["u"] = u + 1
                        return
                    qke_at[pj] = C("c_qke")
                    nxt = pj + 2
                    if nxt < NPAIR and s > 0:
                        for c in range(2):
                            on("sync", lambda E, nxt=nxt, wslot=wslot, c=c:
                                E.dma_start(out=qkw_dst(wslot, c), in_=qkw_src_b(nxt, c)),
                                waits=[("c_qk", C("c_qk")), rp_wait["wqkv"]],
                                incs=[("d_wq", 16)])
                        dwq_at[nxt] = C("d_wq")
                    qk_state["pj"] = pj + 1
                    qk_state["u"] = 0

                def emit_qk_half():
                    pj = qk_state["pj"]
                    if pj >= NPAIR:
                        return
                    u = qk_state["u"]
                    half = qk_state["half"]
                    wslot = pj % 2
                    cslot = pj % 3
                    c, th = divmod(u, 2)
                    waits = [("d_wq", dwq_at[pj]), ("d_xT", dxT),
                             ("c_qke", C("c_qke")), ("c_ve", C("c_ve")),
                             ("c_y2", y2_final)]
                    def mk_qkh(E, wslot=wslot, c=c, th=th, half=half):
                        inst = None
                        for dc in range(8):
                            lhsT = qkw(wslot)[:, 256 * dc + 128 * c:256 * dc + 128 * (c + 1)]
                            inst = E.matmul(
                                ps[7][:], lhsT,
                                xTb(dc)[:, 1024 * th + 512 * half:1024 * th + 512 * half + 512],
                                start=dc == 0, stop=dc == 7)
                        return inst
                    on("tensor", mk_qkh, waits=waits, incs=[("c_qk", 1)])
                    ew = [("c_qk", C("c_qk")), ("d_cst", 16), ("c_dn", dn_final_seq)]
                    if pj >= 3:
                        ew.append(("c_s", sc_pair[pj - 3]))
                    def mk_qkeh(E, pj=pj, cslot=cslot, c=c, th=th, half=half):
                        dst = qkTs(2 * cslot + c)
                        bcol = pj + 8 * c
                        return E.activation(
                            dst[:, 1024 * th + 512 * half:1024 * th + 512 * half + 512],
                            ps[7][:], AF.Identity, bias=bqkT[:, bcol:bcol + 1],
                            scale=1.0)
                    on("scalar", mk_qkeh, waits=ew, incs=[("c_qke", 1)])
                    if half == 0:
                        qk_state["half"] = 1
                        return
                    qk_state["half"] = 0
                    if u < 3:
                        qk_state["u"] = u + 1
                        return
                    qke_at[pj] = C("c_qke")
                    nxt = pj + 2
                    if nxt < NPAIR and s > 0:
                        for c2 in range(2):
                            on("sync", lambda E, nxt=nxt, wslot=wslot, c2=c2:
                                E.dma_start(out=qkw_dst(wslot, c2), in_=qkw_src_b(nxt, c2)),
                                waits=[("c_qk", C("c_qk")), rp_wait["wqkv"]],
                                incs=[("d_wq", 16)])
                        dwq_at[nxt] = C("d_wq")
                    qk_state["pj"] = pj + 1
                    qk_state["u"] = 0

                def emit_qk_pair(upfront=False):
                    for _ in range(4):
                        emit_qk_unit(upfront)

                emit_qk_pair(upfront=True)
                emit_qk_pair(upfront=True)

                v_state = {"ti": 0}
                ve_at = {}

                def emit_v(upto):
                    while v_state["ti"] < upto:
                        ti = v_state["ti"]
                        waits = [("d_xT", dxT), ("d_vw", dvw), ("c_qke", C("c_qke"))]
                        if ti > 0:
                            waits.append(("c_ve", ve_at[ti - 1]))
                        else:
                            waits.append(("c_qke", qke_at[1]))
                        def mk_v(E, ti=ti):
                            inst = None
                            for dc in range(8):
                                st_, sp = dc == 0, dc == 7
                                lhsT = xTb(dc)[:, 128 * ti:128 * (ti + 1)]
                                E.matmul(ps[6][:], lhsT, Vw(dc)[:, 0:512], start=st_, stop=sp)
                                inst = E.matmul(ps[7][:], lhsT, Vw(dc)[:, 512:1024], start=st_, stop=sp)
                            return inst
                        on("tensor", mk_v, waits=waits, incs=[("c_v", 1)])
                        def mk_ve(E, ti=ti):
                            E.scalar_tensor_tensor(V2[:, 1024 * ti:1024 * ti + 512], ps[6][:],
                                                   1.0, bvb[:, 0:512], ALU.mult, ALU.add)
                            return E.scalar_tensor_tensor(V2[:, 1024 * ti + 512:1024 * (ti + 1)],
                                                          ps[7][:], 1.0, bvb[:, 512:1024],
                                                          ALU.mult, ALU.add)
                        on("vector", mk_ve,
                           waits=[("c_v", C("c_v")), ("d_cst", 80), ("c_dn", dn_final_seq)],
                           incs=[("c_ve", 1)])
                        ve_at[ti] = C("c_ve")
                        v_state["ti"] = ti + 1

                pending_norm = []
                def vbank(pj_, qq_):
                    if pj_ == 0:
                        return 4
                    return 4 if (4 * pj_ + qq_) % 2 == 0 else 6

                def flush_norm():
                    while pending_norm:
                        pending_norm.pop(0)()

                def norm_count_logical():
                    return C("c_nm") + len(pending_norm)

                for pj in range(NPAIR):
                    cslot = pj % 3
                    qc = 2048 * (2 * cslot)
                    kc = 2048 * (2 * cslot + 1)
                    cs_at = {}
                    cm_at = {}
                    steps = []
                    for qq in range(NQQ):
                        nkj = 4 * qq + 4
                        for kj in range(nkj):
                            steps.append((qq, kj, kj == 0, kj == nkj - 1))
                    ns = len(steps)

                    def scores_step(idx, pj=pj, qc=qc, kc=kc):
                        qq, kj, first, last = steps[idx]
                        gl = gstep + idx
                        par = gl % 2
                        b0, b1 = (0, 1) if par == 0 else (2, 3)
                        diag = kj >= 4 * qq
                        joff = 128 * (kj - 4 * qq) if diag else 0
                        n = QW - joff
                        waits = [("c_qke", qke_at[pj]), ("c_m", 1)]
                        if gl >= 2:
                            waits.append(("c_x", 2 * (gl - 1)))
                        def mk(E, kj=kj, qq=qq, joff=joff, n=n, b0=b0, b1=b1, qc=qc, kc=kc):
                            qcol = qc + QW * qq + joff
                            kcol = kc + 128 * kj
                            E.matmul(ps[b0][:, joff:joff + n],
                                     T4.ap()[0:64, kcol:kcol + 128],
                                     T4.ap()[0:64, qcol:qcol + n], start=True, stop=True)
                            return E.matmul(ps[b1][:, joff:joff + n],
                                            T4.ap()[64:128, kcol:kcol + 128],
                                            T4.ap()[64:128, qcol:qcol + n], start=True, stop=True)
                        on("tensor", mk, waits=waits, incs=[("c_s", 1)])
                        cs_at[gl] = C("c_s")
                        if diag:
                            def mkm(E, b0=b0, b1=b1, joff=joff):
                                E.tensor_tensor(ps[b0][:, joff:joff + 128],
                                                ps[b0][:, joff:joff + 128], mskt[:], ALU.add)
                                return E.tensor_tensor(ps[b1][:, joff:joff + 128],
                                                       ps[b1][:, joff:joff + 128], mskt[:], ALU.add)
                            on("vector", mkm, waits=[("c_s", cs_at[gl])], incs=[("c_m", 1)])
                        cm_at[gl] = C("c_m")
                        return (par, joff, n, diag)

                    meta = {0: scores_step(0)}
                    for idx in range(ns):
                        qq, kj, first, last = steps[idx]
                        gl = gstep + idx
                        if pj == 0 and kj == 0:
                            emit_v(min(NT, 4 * qq + 4))
                        elif pj == 1 and idx == 0:
                            emit_v(NT)
                        if idx + 1 < ns:
                            meta[idx + 1] = scores_step(idx + 1)
                        par, joff, n, diag = meta[idx]
                        pcol = 1024 * par
                        b0, b1 = (0, 1) if par == 0 else (2, 3)
                        xw2 = [("c_m", cm_at[gl])] if diag else [("c_s", cs_at[gl])]
                        def mex0(E, b0=b0, pcol=pcol, joff=joff, n=n):
                            return E.activation(pT[:, pcol + joff:pcol + joff + n],
                                                ps[b0][:, joff:joff + n],
                                                AF.Exp, bias=negc[:], scale=0.125)
                        def mex1(E, b1=b1, pcol=pcol, joff=joff, n=n):
                            return E.activation(pT[:, pcol + 512 + joff:pcol + 512 + joff + n],
                                                ps[b1][:, joff:joff + n],
                                                AF.Exp, bias=negc[:], scale=0.125)
                        on("scalar", mex0, waits=xw2, incs=[("c_x", 1)])
                        on("scalar", mex1, waits=xw2, incs=[("c_x", 1)])
                        cx_at[gl] = C("c_x")
                        assert cx_at[gl] == 2 * (gl + 1), (gl, cx_at[gl])
                        if first:
                            flush_norm()
                        if pj >= 1 and kj in (1, 2):
                            emit_qk_half()
                        vb = vbank(pj, qq)
                        pvw = [("c_x", cx_at[gl]), ("c_ve", ve_at[kj])]
                        if first:
                            pvw.append(("c_vc", last_vc[vb]))
                            pvw.append(("c_y2", y2_final))
                            if pj == 1 and vb == 6:
                                pvw.append(("c_ve", ve_at[NT - 1]))
                        def mpv(E, kj=kj, pj=pj, pcol=pcol, joff=joff, n=n,
                                first=first, last=last, vb=vb):
                            vb0 = V2[:, 1024 * kj + 128 * pj:1024 * kj + 128 * pj + 64]
                            vb1 = V2[:, 1024 * kj + 128 * pj + 64:1024 * kj + 128 * (pj + 1)]
                            E.matmul(ps[vb][0:64, joff:joff + n], vb0,
                                     pT[:, pcol + joff:pcol + joff + n], start=first, stop=last)
                            return E.matmul(ps[vb][64:128, joff:joff + n], vb1,
                                            pT[:, pcol + 512 + joff:pcol + 512 + joff + n],
                                            start=first, stop=last, tile_position=(0, 64))
                        on("tensor", mpv, waits=pvw, incs=[])
                        def mpr(E, kj=kj, pcol=pcol, joff=joff, n=n, first=first, last=last):
                            E.matmul(ps[5][0:64, joff:joff + n], ones64[:],
                                     pT[:, pcol + joff:pcol + joff + n], start=first, stop=last)
                            return E.matmul(ps[5][64:128, joff:joff + n], ones64[:],
                                            pT[:, pcol + 512 + joff:pcol + 512 + joff + n],
                                            start=first, stop=last, tile_position=(0, 64))
                        prw = [("c_x", cx_at[gl])]
                        if first:
                            prw.append(("c_g2", C("c_g2")))
                        on("tensor", mpr, waits=prw,
                           incs=[("c_pv", 1)] if last else [])
                        if last:
                            def mcp(E, pj=pj, qq=qq, vb=vb):
                                return E.tensor_copy(
                                    valsT[:, 2048 * pj + QW * qq:2048 * pj + QW * (qq + 1)],
                                    ps[vb][:, 0:512])
                            cpw = [("c_pv", C("c_pv"))]
                            if pj == 0 and qq == 0 and s > 0:
                                cpw.append(("c_o", o_final))
                                cpw.append(("c_gu", C("c_gu")))
                                cpw.append(("d_out0", out_cnt[0]))
                                cpw.append(("d_out1", out_cnt[1]))
                            on("vector", mcp, waits=cpw, incs=[("c_vc", 1)])
                            last_vc[vb] = C("c_vc")

                            def emit_norm(pj=pj, qq=qq, pv_=C("c_pv"),
                                          nm_=C("c_nm"), vc_=C("c_vc")):
                                on("scalar",
                                   lambda E: E.activation(bobres[:, 0:512],
                                                          ps[5][:, 0:512], AF.Ln),
                                   waits=[("c_pv", pv_), ("c_nm", nm_)],
                                   incs=[("c_g2", 1)])
                                def mexr(E):
                                    E.activation(rrb[:], bobres[:, 0:512],
                                                 AF.Exp, scale=-1.0)
                                    return E.activation(bobres[:, 1016:1024],
                                                        rrb[:, 504:512], AF.Identity)
                                on("scalar", mexr, waits=[("c_g2", C("c_g2"))],
                                   incs=[("c_lg", 1)])
                                def mnm(E, pj=pj, qq=qq):
                                    col = 2048 * pj + QW * qq
                                    return E.tensor_mul(valsT[:, col:col + QW],
                                                        valsT[:, col:col + QW],
                                                        rrb[:])
                                on("vector", mnm,
                                   waits=[("c_lg", C("c_lg")), ("c_vc", vc_)],
                                   incs=[("c_nm", 1)])
                            pending_norm.append(emit_norm)
                    gstep += ns
                    sc_pair[pj] = C("c_s")
                flush_norm()
                nm_final = C("c_nm")
                x_final = C("c_x")
                qk_final = C("c_qk")
                v_final = C("c_v")
                sfin = C("c_s")

                wout_src = wout_b.rearrange("(c p) f -> p c f", p=128)
                on("sync", lambda E: E.dma_start(
                        out=T4.ap()[:, 8192:12288].rearrange("p (c f) -> p c f", c=4),
                        in_=wout_src[:, 0:4, :]),
                   waits=[rp_wait["wout"], ("c_s", sc_pair[5])], incs=[("d_wo", 16)])
                on("sync", lambda E: E.dma_start(
                        out=T4.ap()[:, 0:4096].rearrange("p (c f) -> p c f", c=4),
                        in_=wout_src[:, 4:8, :]),
                   waits=[("c_s", sc_pair[6])], incs=[("d_wo", 16)])
                dwo = C("d_wo")
                o_parity_free = {}
                ln_base = C("c_ln")
                for ti in range(NT):
                    par = ti % 2
                    b0, b1 = (0, 1) if par == 0 else (2, 3)
                    rs = f"d_rs{par}"
                    on("gpsimd", lambda E, s=s, ti=ti, par=par:
                        E.dma_start(out=residt(par), in_=x_d[s, 128 * ti:128 * (ti + 1), :]),
                        waits=[("c_y", rs_cnt[par]), ("c_dn", dn_final)],
                        incs=[(rs, 16)])
                    drs = C(rs)
                    waits = [("c_nm", nm_final), ("d_wo", dwo)]
                    pv_ = o_parity_free.get(par)
                    if pv_ is not None:
                        waits.append(("c_y", pv_))
                    else:
                        waits.append(("c_x", x_final))
                        waits.append(("c_rb", C("c_rb")))
                    def mk_op(E, ti=ti, b0=b0, b1=b1):
                        inst = None
                        for j in range(8):
                            st_, sp = j == 0, j == 7
                            lhsT = valsT[:, 2048 * j + 128 * ti:2048 * j + 128 * (ti + 1)]
                            E.matmul(ps[b0][:], lhsT, woutb(j)[:, 0:512],
                                     start=st_, stop=sp)
                            inst = E.matmul(ps[b1][:], lhsT, woutb(j)[:, 512:1024],
                                            start=st_, stop=sp)
                        return inst
                    on("tensor", mk_op, waits=waits, incs=[("c_o", 1)])
                    brw = [(rs, drs), ("d_cst", 80)]
                    if ti == 0:
                        brw.append(("c_lg", C("c_lg")))
                    on("vector", lambda E, par=par:
                        E.tensor_add(bobres[:], bob[:], residt(par)[:]),
                        waits=brw, incs=[("c_br", 1)])
                    def mk_y(E, ti=ti, b0=b0, b1=b1):
                        xt = x1t(ti)
                        E.tensor_add(xt[:, 0:512], ps[b0][:], bobres[:, 0:512])
                        return E.tensor_add(xt[:, 512:1024], ps[b1][:], bobres[:, 512:1024])
                    yw = [("c_o", C("c_o")), ("c_br", C("c_br")),
                          ("c_qk", qk_final), ("c_v", v_final)]
                    on("vector", mk_y, waits=yw, incs=[("c_y", 1)])
                    rs_cnt[par] = C("c_y")
                    o_parity_free[par] = C("c_y")
                    def mk_bn1(E, ti=ti):
                        xt = x1t(ti)
                        E.activation(gsb[:], xt[:], AF.Identity, accum_out=suma[:, ti:ti + 1])
                        return E.activation(gsb[:], xt[:], AF.Square,
                                            accum_out=sqa[:, ti:ti + 1])
                    on("scalar", mk_bn1, waits=[("c_y", C("c_y"))], incs=[("c_bn", 2)])

                    if ti in (7, 15):
                        hs = 0 if ti == 7 else 8
                        def mk_veps(E, hs=hs, gb_=C("c_g1")):
                            E.tensor_scalar_mul(ln1s[:, 16 + hs:24 + hs],
                                                suma[:, hs:hs + 8], 1.0 / 1024.0).then_inc(sems["c_g1"], 1)
                            E.wait_ge(sems["c_g1"], gb_ + 1)
                            E.tensor_scalar_mul(ln1s[:, 32 + hs:40 + hs],
                                                ln1s[:, 16 + hs:24 + hs], -1.0)
                            E.tensor_mul(ln1s[:, hs:hs + 8], ln1s[:, 16 + hs:24 + hs],
                                         ln1s[:, 16 + hs:24 + hs]).then_inc(sems["c_g1"], 1)
                            E.wait_ge(sems["c_g1"], gb_ + 2)
                            return E.scalar_tensor_tensor(ln1s[:, hs:hs + 8], sqa[:, hs:hs + 8],
                                                          1.0 / 1024.0, ln1s[:, hs:hs + 8],
                                                          ALU.mult, ALU.subtract)
                        on("vector", mk_veps, waits=[("c_bn", C("c_bn"))], incs=[("c_ln", 1)])
                        cnt["c_g1"] += 2
                        on("scalar", lambda E, hs=hs:
                            E.activation(ln1s[:, 16 + hs:24 + hs], ln1s[:, hs:hs + 8], AF.Sqrt,
                                         bias=epsb[:], scale=1.0),
                           waits=[("c_ln", C("c_ln"))], incs=[("c_ln", 1)])
                        on("vector", lambda E, hs=hs:
                            E.reciprocal(ln1s[:, hs:hs + 8], ln1s[:, 16 + hs:24 + hs]),
                           waits=[("c_ln", C("c_ln"))], incs=[("c_ln", 1)])

                    if ti >= 8 and ti - 8 < 4:
                        t0 = ti - 8
                        p0 = t0 % 2
                        def mk_norm1e(E, t0=t0, p0=p0):
                            xt = x1t(t0)
                            E.tensor_scalar(x1b(p0), xt[:], ln1s[:, 32 + t0:33 + t0],
                                            ln1s[:, t0:t0 + 1], ALU.add, ALU.mult)
                            return E.tensor_scalar(xt[:], xt[:], ln1s[:, 32 + t0:33 + t0],
                                                   ln1s[:, t0:t0 + 1], ALU.add, ALU.mult)
                        nw = [("c_ln", ln_base + 3)]
                        if t0 >= 2:
                            nw.append(("c_tp", C("c_tp")))
                        on("vector", mk_norm1e, waits=nw, incs=[("c_xb", 1)])
                        tb = 4 + p0
                        tw = [("c_xb", C("c_xb")), ("c_nm", nm_final),
                              ("c_vc", C("c_vc")), ("c_g2", C("c_g2"))]
                        if t0 >= 2:
                            tw.append(("c_te", C("c_te")))
                        def mk_tp(E, p0=p0, tb=tb):
                            inst = None
                            for dc in range(8):
                                inst = E.transpose(psb[tb][:, 128 * dc:128 * (dc + 1)],
                                                   x1b(p0)[:, 128 * dc:128 * (dc + 1)], idb[:])
                            return inst
                        on("tensor", mk_tp, waits=tw, incs=[("c_tp", 1)])
                        ew = [("c_tp", C("c_tp")), ("c_pv", C("c_pv")),
                              ("c_gu", C("c_gu"))]
                        def mk_te(E, t0=t0, tb=tb):
                            src = psb[tb].rearrange("p (dc n) -> p dc n", dc=8)
                            dst = x1T.rearrange("p (dc n) -> p dc n", dc=8)[:, :, 128 * t0:128 * (t0 + 1)]
                            return E.activation(dst, src, AF.Identity)
                        on("scalar", mk_te, waits=ew, incs=[("c_te", 1)])
                y_final = C("c_y")
                o_final = C("c_o")
                ln1_done = C("c_ln")
                tp_q0 = C("c_tp")

                for hc in range(24):
                    on("gpsimd", lambda E, hc=hc:
                        E.dma_start(out=wdTb(hc), in_=wd_b[128 * hc:128 * (hc + 1), :]),
                        waits=[rp_wait["wd"], ("c_o", o_final), ("c_y", y_final),
                               ("c_s", sfin)],
                        incs=[("d_wd", 16)])
                dwd = C("d_wd")

                si_parity = {}
                hm_parity = {}
                te_parity = {}
                y2_parity = {}
                y2_at.clear()
                def emit_quarter_transposes(tq):
                    if True:
                        for tr in range(4):
                            ti = 4 * tq + tr
                            par = tr % 2
                            def mk_norm1(E, ti=ti, par=par):
                                xt = x1t(ti)
                                E.tensor_scalar(x1b(par), xt[:], ln1s[:, 32 + ti:33 + ti],
                                                ln1s[:, ti:ti + 1], ALU.add, ALU.mult)
                                return E.tensor_scalar(xt[:], xt[:], ln1s[:, 32 + ti:33 + ti],
                                                       ln1s[:, ti:ti + 1], ALU.add, ALU.mult)
                            nw = [("c_ln", ln1_done), ("c_tp", C("c_tp"))]
                            on("vector", mk_norm1, waits=nw, incs=[("c_xb", 1)])
                            tb = par
                            tw = [("c_xb", C("c_xb")), ("c_si", C("c_si")),
                                  ("c_hm", C("c_hm"))]
                            pv_ = te_parity.get(tb)
                            if pv_ is not None:
                                tw.append(("c_te", pv_))
                            else:
                                tw.append(("c_y", y_final))
                            def mk_tp(E, par=par, tb=tb):
                                inst = None
                                for dc in range(8):
                                    inst = E.transpose(psb[tb][:, 128 * dc:128 * (dc + 1)],
                                                       x1b(par)[:, 128 * dc:128 * (dc + 1)], idb[:])
                                return inst
                            on("tensor", mk_tp, waits=tw, incs=[("c_tp", 1)])
                            ew = [("c_tp", C("c_tp")), ("c_gu", C("c_gu"))]
                            def mk_te(E, tr=tr, tb=tb):
                                src = psb[tb].rearrange("p (dc n) -> p dc n", dc=8)
                                dst = x1T.rearrange("p (dc n) -> p dc n", dc=8)[:, :, 128 * tr:128 * (tr + 1)]
                                return E.activation(dst, src, AF.Identity)
                            on("scalar", mk_te, waits=ew, incs=[("c_te", 1)])
                            te_parity[tb] = C("c_te")

                for tq in range(NQQ):
                    te_quarter = C("c_te")

                    for hc in range(24):
                        par = hc % 2
                        gb, ub = 0 + par, 2 + par
                        gs_sem, us_sem = f"d_g{par}", f"d_u{par}"
                        on("sync", lambda E, hc=hc, par=par:
                            E.dma_start(out=wgub(0, par),
                                        in_=wg_b.rearrange("(dc p) f -> p dc f", p=128)[:, :, 128 * hc:128 * (hc + 1)]),
                            waits=[("c_gu", g_cnt[par]), ("c_o", o_final),
                                   rp_wait["wg"]] if par else
                                  [("c_gu", g_cnt[par]), ("c_x", x_final),
                                   ("c_pv", C("c_pv")), rp_wait["wg"]],
                            incs=[(gs_sem, 16)])
                        on("sync", lambda E, hc=hc, par=par:
                            E.dma_start(out=wgub(1, par),
                                        in_=wu_b.rearrange("(dc p) f -> p dc f", p=128)[:, :, 128 * hc:128 * (hc + 1)]),
                            waits=[("c_gu", u_cnt[par]), ("c_o", o_final),
                                   rp_wait["wu"]] if par else
                                  [("c_gu", u_cnt[par]), ("c_x", x_final),
                                   ("c_pv", C("c_pv")), rp_wait["wu"]],
                            incs=[(us_sem, 16)])
                        dg, du = C(gs_sem), C(us_sem)
                        muw = [(gs_sem, dg), (us_sem, du), ("c_te", te_quarter)]
                        pv_ = si_parity.get(par)
                        pu_ = hm_parity.get(par)
                        if pv_ is not None:
                            muw.append(("c_si", pv_))
                        if pu_ is not None:
                            muw.append(("c_hm", pu_))
                        if pv_ is None:
                            muw += [("c_tp", C("c_tp")), ("c_y", y_final)]
                        def mk_gu(E, gb=gb, ub=ub, par=par):
                            inst = None
                            for dc in range(8):
                                st_, sp = dc == 0, dc == 7
                                E.matmul(ps[gb][:], wgub(0, par)[:, 128 * dc:128 * (dc + 1)],
                                         x1T[:, 512 * dc:512 * (dc + 1)], start=st_, stop=sp)
                                inst = E.matmul(ps[ub][:], wgub(1, par)[:, 128 * dc:128 * (dc + 1)],
                                                x1T[:, 512 * dc:512 * (dc + 1)], start=st_, stop=sp)
                            return inst
                        on("tensor", mk_gu, waits=muw, incs=[("c_gu", 1)])
                        g_cnt[par] = C("c_gu")
                        u_cnt[par] = C("c_gu")
                        siw = [("c_gu", C("c_gu"))]
                        if hm_parity.get(par) is not None:
                            siw.append(("c_hm", hm_parity[par]))
                        on("scalar", lambda E, gb=gb, par=par:
                            E.activation(gsb[:, 512 * par:512 * (par + 1)], ps[gb][:],
                                         AF.Silu, bias=0.0, scale=DEQ),
                            waits=siw, incs=[("c_si", 1)])
                        si_parity[par] = C("c_si")
                        hw_ = [("c_si", C("c_si")), ("c_dn", C("c_dn"))]
                        if tq == 0 and hc < 2:
                            hw_.append(("c_ve", ve_at[NT - 1]))
                            hw_.append(("c_pv", C("c_pv")))
                        on("vector", lambda E, ub=ub, par=par, hc=hc:
                            E.scalar_tensor_tensor(hT(hc), ps[ub][:], DEQ,
                                                   gsb[:, 512 * par:512 * (par + 1)],
                                                   ALU.mult, ALU.mult),
                            waits=hw_, incs=[("c_hm", 1)])
                        hm_parity[par] = C("c_hm")
                    hm_quarter = C("c_hm")

                    for tr in range(4):
                        ti = 4 * tq + tr
                        par = tr % 2
                        b0, b1 = (4, 5) if par == 0 else (6, 7)
                        dnw = [("c_hm", hm_quarter), ("d_wd", dwd)]
                        pv_ = y2_parity.get(par)
                        if pv_ is not None:
                            dnw.append(("c_y2", pv_))
                        elif par == 0:
                            dnw += [("c_te", te_quarter), ("c_tp", tp_q0),
                                    ("c_vc", C("c_vc")), ("c_g2", C("c_g2")),
                                    ("c_nm", nm_final)]
                        else:
                            dnw += [("c_qke", C("c_qke")), ("c_ve", ve_at[NT - 1]),
                                    ("c_qk", qk_final), ("c_v", v_final),
                                    ("c_vc", C("c_vc"))]
                        def mk_dn(E, tr=tr, b0=b0, b1=b1):
                            inst = None
                            for hc in range(24):
                                st_, sp = hc == 0, hc == 23
                                lhsT = hT(hc)[:, 128 * tr:128 * (tr + 1)]
                                E.matmul(ps[b0][:], lhsT, wdTb(hc)[:, 0:512], start=st_, stop=sp)
                                inst = E.matmul(ps[b1][:], lhsT, wdTb(hc)[:, 512:1024],
                                                start=st_, stop=sp)
                            return inst
                        on("tensor", mk_dn, waits=dnw, incs=[("c_dn", 1)])
                        oc = f"d_out{tr % 2}"
                        y2w = [("c_dn", C("c_dn")), (oc, out_cnt[tr % 2]),
                               ("c_o", o_final)]
                        def mk_y2(E, ti=ti, tr=tr, b0=b0, b1=b1):
                            yq = y2q(tr)
                            xt = x1t(ti)
                            E.scalar_tensor_tensor(yq[:, 0:512], ps[b0][:], DEQ, xt[:, 0:512],
                                                   ALU.mult, ALU.add)
                            return E.scalar_tensor_tensor(yq[:, 512:1024], ps[b1][:], DEQ,
                                                          xt[:, 512:1024], ALU.mult, ALU.add)
                        on("vector", mk_y2, waits=y2w, incs=[("c_y2", 1)])
                        y2_parity[par] = C("c_y2")
                        y2_at[ti] = C("c_y2")
                        def mk_bn2(E, tr=tr):
                            yq = y2q(tr)
                            E.activation(bobres[:], yq[:], AF.Identity,
                                         accum_out=suma2[:, tr:tr + 1])
                            return E.activation(bobres[:], yq[:], AF.Square,
                                                accum_out=sqa2[:, tr:tr + 1])
                        on("scalar", mk_bn2, waits=[("c_y2", C("c_y2"))], incs=[("c_bn2", 2)])

                    if tq + 1 < NQQ:
                        emit_quarter_transposes(tq + 1)

                    def mk_veps2(E, gb_=C("c_g1")):
                        E.tensor_scalar_mul(ln2s[:, 4:8], suma2[:], 1.0 / 1024.0).then_inc(sems["c_g1"], 1)
                        E.wait_ge(sems["c_g1"], gb_ + 1)
                        E.tensor_scalar_mul(ln2s[:, 8:12], ln2s[:, 4:8], -1.0)
                        E.tensor_mul(ln2s[:, 0:4], ln2s[:, 4:8], ln2s[:, 4:8]).then_inc(sems["c_g1"], 1)
                        E.wait_ge(sems["c_g1"], gb_ + 2)
                        return E.scalar_tensor_tensor(ln2s[:, 0:4], sqa2[:], 1.0 / 1024.0,
                                                      ln2s[:, 0:4], ALU.mult, ALU.subtract)
                    on("vector", mk_veps2, waits=[("c_bn2", C("c_bn2"))], incs=[("c_ln2", 1)])
                    cnt["c_g1"] += 2
                    on("scalar", lambda E: E.activation(ln2s[:, 4:8], ln2s[:, 0:4], AF.Sqrt,
                                                        bias=epsb[:], scale=1.0),
                       waits=[("c_ln2", C("c_ln2"))], incs=[("c_ln2", 1)])
                    on("vector", lambda E: E.reciprocal(ln2s[:, 0:4], ln2s[:, 4:8]),
                       waits=[("c_ln2", C("c_ln2"))], incs=[("c_ln2", 1)])
                    for tr in range(4):
                        ti = 4 * tq + tr
                        on("vector", lambda E, tr=tr:
                            E.tensor_scalar(y2q(tr)[:], y2q(tr)[:], ln2s[:, 8 + tr:9 + tr],
                                            ln2s[:, tr:tr + 1], ALU.add, ALU.mult),
                            waits=[("c_ln2", C("c_ln2"))], incs=[("c_n2", 1)])
                        oc = f"d_out{tr % 2}"
                        on("scalar", lambda E, s=s, ti=ti, tr=tr:
                            E.dma_start(out=out_d[s, 128 * ti:128 * (ti + 1), :], in_=y2q(tr)),
                            waits=[("c_n2", C("c_n2"))], incs=[(oc, 16)])
                        out_cnt[tr % 2] = C(oc)
                dn_final = C("c_dn")
                y2_final = C("c_y2")

            on("scalar", lambda E: E.nop(),
               waits=[("d_out0", out_cnt[0]), ("d_out1", out_cnt[1])])

        walk("audit")
        _check_schedule(audit_streams)

        block = e(nc.Block())

        @block.sync
        def _(E):
            walk("sync")

        @block.gpsimd
        def _(E):
            walk("gpsimd")

        @block.tensor
        def _(E):
            walk("tensor")

        @block.vector
        def _(E):
            walk("vector")

        @block.scalar
        def _(E):
            walk("scalar")

    return nc


def _check_schedule(streams):
    if not streams:
        return
    pos = {k: 0 for k in streams}
    cnt = {}
    progress = True
    while progress:
        progress = False
        for eng, ops in streams.items():
            while pos[eng] < len(ops):
                waits, incs = ops[pos[eng]]
                if all(cnt.get(s_, 0) >= v for s_, v in waits):
                    for s_, n in incs:
                        cnt[s_] = cnt.get(s_, 0) + n
                    pos[eng] += 1
                    progress = True
                else:
                    break
    stuck = {k: (pos[k], len(v)) for k, v in streams.items() if pos[k] < len(v)}
    if stuck:
        detail = {}
        for eng in stuck:
            waits, _ = streams[eng][pos[eng]]
            detail[eng] = [(s_, v, cnt.get(s_, 0)) for s_, v in waits
                           if cnt.get(s_, 0) < v]
        raise RuntimeError(f"schedule deadlock: {stuck}; blocking: {detail}")


_PROG_CACHE = {}


def _get_program():
    if "nc" not in _PROG_CACHE:
        _PROG_CACHE["nc"] = build_program()
    return _PROG_CACHE["nc"]


def kernel(L0_attn_x, L0_attn_W_qkv, L0_attn_b_qkv, L0_attn_W_out, L0_attn_b_out,
           L0_ffn_w_gate, L0_ffn_w_up, L0_ffn_w_down):
    x = np.asarray(L0_attn_x, dtype=np.float32)
    wqkv = np.ascontiguousarray(np.asarray(L0_attn_W_qkv, dtype=np.float32))
    bqkv = np.asarray(L0_attn_b_qkv, dtype=np.float32)
    wout = np.ascontiguousarray(np.asarray(L0_attn_W_out, dtype=np.float32))
    bout = np.asarray(L0_attn_b_out, dtype=np.float32)
    wg = np.asarray(L0_ffn_w_gate, dtype=np.int32)
    wu = np.asarray(L0_ffn_w_up, dtype=np.int32)
    wd = np.asarray(L0_ffn_w_down, dtype=np.int32)

    wgT = np.ascontiguousarray(wg.T)
    wuT = np.ascontiguousarray(wu.T)
    wdT = np.ascontiguousarray(wd.T)
    bqkT = np.ascontiguousarray(bqkv[0:2048].reshape(16, 128).T)
    bvb = np.ascontiguousarray(np.broadcast_to(bqkv[2048:3072], (128, 1024)))
    bob = np.ascontiguousarray(np.broadcast_to(bout, (128, 1024)))
    maskT = np.where(np.arange(128)[:, None] > np.arange(128)[None, :],
                     np.float32(-80000.0), np.float32(0.0)).astype(np.float32)
    ident = np.eye(128, dtype=ml_dtypes.bfloat16)

    nc = _get_program()
    in_maps = []
    for c in range(NCORE):
        xs = np.ascontiguousarray(x[NSEQ * c:NSEQ * (c + 1)])
        xTs = np.ascontiguousarray(xs.transpose(0, 2, 1))
        in_maps.append({
            "x": xs, "xT": xTs, "wqkv": wqkv, "wout": wout,
            "wgT": wgT, "wuT": wuT, "wdT": wdT,
            "bqkT": bqkT, "bvb": bvb, "bob": bob,
            "maskT": maskT, "ident": ident,
        })
    res = run_bass_kernel_spmd(nc, in_maps, core_ids=list(range(NCORE)))
    return np.concatenate([r["out"] for r in res.results], axis=0)
